# version 06
# baseline (speedup 1.0000x reference)
import heapq
import numpy as np
import ml_dtypes
from contextlib import ExitStack

import concourse.bass as bass
import concourse.mybir as mybir
from concourse.bass_utils import run_bass_kernel_spmd

F32 = mybir.dt.float32
BF16 = mybir.dt.bfloat16
AF = mybir.ActivationFunctionType
ALU = mybir.AluOpType
AX = mybir.AxisListType.X

PI = float(np.pi)
MAGIC = 12582912.0
INV2PI = float(1.0 / (2.0 * np.pi))
TWOPI = float(2.0 * np.pi)

D = 1024
FF = 4096
E = 8
T = 4096
TTA = 512
NTT = T // TTA
TTB = 256
NTTB = T // TTB
DCN = 8
SUB = 64
NSUB = TTA // SUB
QT = 1024
NQT = 4
CNT_INV = float(1.0 / (16384.0 * 32.0))


def _act_setids():
    from concourse.hw_specs import get_activation_tables
    tabs = list(get_activation_tables("gen3").values())
    def find(*funcs):
        for i, fs in enumerate(tabs):
            if all(f in fs for f in funcs):
                return i
        raise ValueError(funcs)
    return {
        AF.Sqrt: find(AF.Sqrt, AF.Square, AF.Sign),
        AF.Sigmoid: find(AF.Sigmoid, AF.Arctan, AF.Sign),
        AF.Arctan: find(AF.Sigmoid, AF.Arctan, AF.Sign),
        AF.Ln: find(AF.Ln, AF.Exp),
        AF.Exp: find(AF.Ln, AF.Exp),
        AF.Sin: find(AF.Sin, AF.Silu),
        AF.Silu: find(AF.Sin, AF.Silu),
        AF.Gelu: find(AF.Gelu),
        AF.Gelu_apprx_tanh: find(AF.Gelu_apprx_tanh),
    }


_ACT_WILD = {AF.Square, AF.Sign, AF.Abs, AF.Copy, AF.Identity, AF.Relu,
             AF.Is_finite}


def build_nc(dbg=False):
    nc = bass.Bass("TRN2", target_bir_lowering=False)
    _setid = _act_setids()
    _att = {"cur": None}

    def _tload(a, sid):
        a.add_instruction(mybir.InstLoadActFuncSet(
            name=nc.get_next_instruction_name(),
            act_func_set_id=sid, ins=[], outs=[]))
        _att["cur"] = sid

    def act(a, out, in_, func, **kw):
        if func in _ACT_WILD:
            if _att["cur"] is None:
                _tload(a, _setid[AF.Sqrt])
        elif _setid[func] != _att["cur"]:
            _tload(a, _setid[func])
        return a.activation(out, in_, func, **kw)

    x_d = nc.dram_tensor("x", [T, D], F32, kind="ExternalInput")
    wq_d = nc.dram_tensor("wq", [D, D], F32, kind="ExternalInput")
    wk_d = nc.dram_tensor("wk", [D, D], F32, kind="ExternalInput")
    wv_d = nc.dram_tensor("wv", [D, D], F32, kind="ExternalInput")
    wa_d = nc.dram_tensor("wa", [D, 2 * D], F32, kind="ExternalInput")
    wg_d = nc.dram_tensor("wg", [D, D], F32, kind="ExternalInput")
    wo_d = nc.dram_tensor("wo", [D, D], F32, kind="ExternalInput")
    w1_d = nc.dram_tensor("w1", [D, FF], F32, kind="ExternalInput")
    w2_d = nc.dram_tensor("w2", [FF, D], F32, kind="ExternalInput")
    b1c_d = nc.dram_tensor("b1c", [128, 32], F32, kind="ExternalInput")
    b2b_d = nc.dram_tensor("b2b", [128, D], F32, kind="ExternalInput")
    l1sb_d = nc.dram_tensor("l1sb", [128, D], F32, kind="ExternalInput")
    l1bb_d = nc.dram_tensor("l1bb", [128, D], F32, kind="ExternalInput")
    l2sb_d = nc.dram_tensor("l2sb", [128, D], F32, kind="ExternalInput")
    l2bb_d = nc.dram_tensor("l2bb", [128, D], F32, kind="ExternalInput")
    gnsr_d = nc.dram_tensor("gnsr", [1, D], F32, kind="ExternalInput")
    gnbr_d = nc.dram_tensor("gnbr", [1, D], F32, kind="ExternalInput")
    gwT_d = nc.dram_tensor("gwT", [D, E], F32, kind="ExternalInput")
    guT_d = nc.dram_tensor("guT", [E, D, 2 * FF], F32, kind="ExternalInput")
    dwq_d = nc.dram_tensor("dwq", [E, 32, 128, D], F32, kind="ExternalInput")
    ident_d = nc.dram_tensor("ident", [128, 128], BF16, kind="ExternalInput")
    gmap_d = nc.dram_tensor("gmap", [128, 4], F32, kind="ExternalInput")
    onehot_d = nc.dram_tensor("onehot", [128, 8], F32, kind="ExternalInput")
    bsel_d = nc.dram_tensor("bsel", [128, 2], F32, kind="ExternalInput")

    y_d = nc.dram_tensor("y", [T, D], F32, kind="ExternalOutput")

    stash_d = nc.dram_tensor("stash", [9, DCN, 128, T], F32)
    z_d = nc.dram_tensor("zst", [DCN, 128, T], BF16)
    xnf_d = nc.dram_tensor("xnfd", [DCN, NQT, 128, QT], BF16)
    hact_d = nc.dram_tensor("hact", [2, 4, 8, 8, 128, 128], BF16)
    mfm_d = nc.dram_tensor("mfmd", [NTT, DCN, 128, TTA], BF16)
    gmoe_d = nc.dram_tensor("gmoe", [32, 128, D], BF16)
    wo_out_d = nc.dram_tensor("wo_out", [32, 128, D], F32)
    summ_in_d = nc.dram_tensor("summ_in", [128, 32], F32)
    summ_ag_d = nc.dram_tensor("summ_ag", [1024, 32], F32)
    stats_in_d = nc.dram_tensor("stats_in", [2, 128], F32)
    stats_out_d = nc.dram_tensor("stats_out", [2, 128], F32)
    crow_d = nc.dram_tensor("crow", [2, D], F32)

    if dbg:
        dbg_stash = nc.dram_tensor("dbg_stash", [9, DCN, 128, T], F32,
                                   kind="ExternalOutput")
        dbg_z = nc.dram_tensor("dbg_z", [DCN, 128, T], BF16,
                               kind="ExternalOutput")
        dbg_xnf = nc.dram_tensor("dbg_xnf", [DCN, NQT, 128, QT], BF16,
                                 kind="ExternalOutput")
        dbg_crow = nc.dram_tensor("dbg_crow", [2, D], F32,
                                  kind="ExternalOutput")
        dbg_summ = nc.dram_tensor("dbg_summ", [1024, 32], F32,
                                  kind="ExternalOutput")
        dbg_wo = nc.dram_tensor("dbg_wo", [32, 128, D], F32,
                                kind="ExternalOutput")
        dbg_gm = nc.dram_tensor("dbg_gm", [32, 128, D], BF16,
                                kind="ExternalOutput")
        dbg_mfm = nc.dram_tensor("dbg_mfm", [NTT, DCN, 128, TTA], BF16,
                                 kind="ExternalOutput")
        dbg_hact = nc.dram_tensor("dbg_hact", [2, 4, 8, 8, 128, 128], BF16,
                                  kind="ExternalOutput")
        dbg_xn = nc.dram_tensor("dbg_xn", [NTT, 128, 4 * D], BF16,
                                kind="ExternalOutput")
        dbg_st4 = nc.dram_tensor("dbg_st4", [NTT, 128, 24], F32,
                                 kind="ExternalOutput")
        dbg_comb = nc.dram_tensor("dbg_comb", [128, 256], F32,
                                  kind="ExternalOutput")
        dbg_acc = nc.dram_tensor("dbg_acc", [128, 8 * D], F32,
                                 kind="ExternalOutput")

    es = ExitStack()
    with es:
        def SEM(name):
            return es.enter_context(nc.semaphore(name))
        C = {}
        def cadd(k, n):
            C[k] = C.get(k, 0) + n
            return C[k]
        def cget(k):
            return C.get(k, 0)

        SN = ("sXL sXF sLN sXN sTRd sTR sTRe sPE0 sKVe sMG sSTa "
              "sPQ sQGe sRT sSTb sSTx "
              "sAC sSTsc sCC "
              "sGW sPE1 sACg sACu sGM sHS sDH sDWQ sPE2 sDV2 sSIL sGMS "
              "sWL sWPE sWEV sWST sZL "
              "sB sQL sQS sQO sP0").split()
        S = {n: SEM(n) for n in SN}

        def SB(scope, name, shape, dt):
            return scope.enter_context(nc.sbuf_tensor(name, shape, dt))

        mask64 = SB(es, "mask64", [128, TTA], F32)
        ident = SB(es, "identsb", [128, 128], BF16)
        gmap = SB(es, "gmapsb", [128, 4], F32)
        onehot = SB(es, "onehotsb", [128, 8], F32)
        bsel = SB(es, "bselsb", [128, 2], F32)
        combine = SB(es, "combine", [128, 32 * E], F32)
        summ = SB(es, "summ", [128, 32], F32)
        hg = SB(es, "hg", [128, 16], F32)
        stats_acc = SB(es, "stats_acc", [128, 32], F32)
        c1pp = SB(es, "c1pp", [128, 8], F32)
        c2pp = SB(es, "c2pp", [128, 8], F32)
        epsc = SB(es, "epsc", [128, 4], F32)

        with nc.Block() as blk:
            @blk.gpsimd
            def _(g):
                g.dma_start(ident[:, :], ident_d[:, :]).then_inc(S["sQL"], 16)
                g.dma_start(gmap[:, :], gmap_d[:, :]).then_inc(S["sQL"], 16)
                g.dma_start(onehot[:, :], onehot_d[:, :]).then_inc(S["sQL"], 16)
                g.dma_start(bsel[:, :], bsel_d[:, :]).then_inc(S["sQL"], 16)
                cadd("sQL", 64)
                g.wait_ge(S["sQL"], cget("sQL"))
                g.sem_inc(S["sP0"], 1)

            @blk.vector
            def _(v):
                v.memset(mask64[:, :], 1.0)
                v.memset(mask64.ap()[:, 0:TTA:SUB], 0.0)
                v.memset(summ[:, :], 0.0)
                v.memset(summ[:, 0:8], 1.0)
                v.memset(stats_acc[:, :], 0.0)
                v.memset(epsc[:, 0:1], 1e-5)
                v.memset(epsc[:, 1:2], 1e-6)
                v.memset(epsc[:, 2:3], 1e-30)
                v.sem_inc(S["sP0"], 1)

        with ExitStack() as e0a, ExitStack() as e0ap:
            wk_sb = SB(e0a, "wk_sb", [128, 8 * D], BF16)
            wv_sb = SB(e0a, "wv_sb", [128, 8 * D], BF16)
            wa_sb = SB(e0a, "wa_sb", [128, 8 * 2 * D], BF16)
            x_tm = SB(e0a, "x_tm", [128, 4 * D], F32)
            xn_tmb = [SB(e0a, f"xn_tmb{i}", [128, 4 * D], BF16)
                      for i in range(2)]
            st4 = SB(e0a, "st4", [128, 24], F32)
            lnscr = SB(e0a, "lnscr", [128, D], F32)
            xnf_t = [SB(e0a, f"xnf_t{i}", [128, DCN * TTA], BF16)
                     for i in range(2)]
            rk = SB(e0a, "rk", [128, TTA], F32)
            rkv = [SB(e0a, f"rkv{i}", [128, TTA], F32) for i in range(2)]
            rlg = [SB(e0a, f"rlg{i}", [128, TTA], F32) for i in range(2)]
            rth = [SB(e0a, f"rth{i}", [128, TTA], F32) for i in range(2)]
            mtp = SB(e0a, "mtp", [128, 6 * TTA], F32)
            tr_ps = [e0ap.enter_context(
                nc.psum_tensor(f"tr_ps{i}", [128, TTA], BF16))
                for i in range(2)]
            kvP = [e0ap.enter_context(
                nc.psum_tensor(f"kvP{i}", [128, TTA], F32))
                for i in range(2)]
            arP = [e0ap.enter_context(
                nc.psum_tensor(f"arP{i}", [128, 2 * TTA], F32))
                for i in range(2)]
            m0 = mtp.ap()[:, 0:TTA]
            m1 = mtp.ap()[:, TTA:2 * TTA]
            m2 = mtp.ap()[:, 2 * TTA:3 * TTA]
            m3 = mtp.ap()[:, 3 * TTA:4 * TTA]
            m4 = mtp.ap()[:, 4 * TTA:5 * TTA]
            m5 = mtp.ap()[:, 5 * TTA:6 * TTA]

            with nc.Block() as blk:
                @blk.gpsimd
                def _(g):
                    g.dma_start(wk_sb.ap().rearrange("p (c m) -> p c m", c=8),
                                wk_d.ap().rearrange("(c p) m -> p c m", p=128)
                                ).then_inc(S["sQL"], 16)
                    g.dma_start(wv_sb.ap().rearrange("p (c m) -> p c m", c=8),
                                wv_d.ap().rearrange("(c p) m -> p c m", p=128)
                                ).then_inc(S["sQL"], 16)
                    g.dma_start(wa_sb.ap().rearrange("p (c m) -> p c m", c=8),
                                wa_d.ap().rearrange("(c p) m -> p c m", p=128)
                                ).then_inc(S["sQL"], 16)
                    cadd("sQL", 48)
                    g.wait_ge(S["sQL"], cget("sQL"))

                @blk.sync
                def _(s):
                    for tt in range(NTT):
                        if tt > 0:
                            s.wait_ge(S["sXF"], tt)
                        s.dma_start(
                            x_tm.ap().rearrange("p (s d) -> p s d", s=4),
                            x_d[tt * TTA:(tt + 1) * TTA, :]
                            .rearrange("(s p) d -> p s d", p=128)
                        ).then_inc(S["sXL"], 16)
                        cadd("sXL", 16)
                        if dbg:
                            s.wait_ge(S["sXN"], tt + 1)
                            s.dma_start(dbg_xn[tt], xn_tmb[tt % 2][:, :]
                                        ).then_inc(S["sQS"], 16)
                            s.dma_start(dbg_st4[tt], st4[:, :]
                                        ).then_inc(S["sQS"], 16)
                            cadd("sQS", 32)
                        for dc in range(DCN):
                            i = tt * 8 + dc
                            s.wait_ge(S["sTRe"], i + 1)
                            s.dma_start(
                                xnf_d[dc, tt // 2, :,
                                      (tt % 2) * TTA:(tt % 2 + 1) * TTA],
                                xnf_t[tt % 2][:, dc * TTA:(dc + 1) * TTA]
                            ).then_inc(S["sSTx"], 16)
                            cadd("sSTx", 16)
                            s.wait_ge(S["sKVe"], i + 1)
                            s.dma_start(stash_d[6, dc, :,
                                                tt * TTA:(tt + 1) * TTA],
                                        rkv[i % 2][:, :]).then_inc(S["sSTa"], 16)
                            s.wait_ge(S["sMG"], 4 * i + 4)
                            s.dma_start(stash_d[7, dc, :,
                                                tt * TTA:(tt + 1) * TTA],
                                        rlg[i % 2][:, :]).then_inc(S["sSTa"], 16)
                            s.dma_start(stash_d[8, dc, :,
                                                tt * TTA:(tt + 1) * TTA],
                                        rth[i % 2][:, :]).then_inc(S["sSTa"], 16)
                            cadd("sSTa", 48)

                @blk.vector
                def _(v):
                    for tt in range(NTT):
                        if tt == 0:
                            v.wait_ge(S["sP0"], 1)
                        v.wait_ge(S["sLN"], 4 * tt)
                        v.wait_ge(S["sXL"], 16 * (tt + 1))
                        v.reduce_sum(
                            st4[:, 0:4],
                            x_tm.ap().rearrange("p (s d) -> p s d", s=4),
                            axis=AX).then_inc(S["sLN"], 1)
                        v.wait_ge(S["sLN"], 4 * tt + 2)
                        v.tensor_scalar(st4[:, 4:8], st4[:, 0:4], 1.0 / D,
                                        None, ALU.mult)
                        v.tensor_scalar(st4[:, 12:16], st4[:, 8:12], 1.0 / D,
                                        None, ALU.mult)
                        v.tensor_mul(st4[:, 16:20], st4[:, 4:8], st4[:, 4:8])
                        v.memset(st4[:, 22:23], 0.0)
                        v.tensor_sub(st4[:, 16:20], st4[:, 12:16],
                                     st4[:, 16:20]).then_inc(S["sLN"], 1)
                        v.wait_ge(S["sLN"], 4 * tt + 4)
                        v.reciprocal(st4[:, 12:16], st4[:, 20:24])
                        v.memset(st4[:, 22:23], 0.0)
                        if tt >= 2:
                            v.wait_ge(S["sTRd"], tt - 1)
                        last = None
                        for s4 in range(4):
                            last = v.tensor_scalar(
                                xn_tmb[tt % 2][:, s4 * D:(s4 + 1) * D],
                                x_tm[:, s4 * D:(s4 + 1) * D],
                                st4[:, 4 + s4:5 + s4], st4[:, 12 + s4:13 + s4],
                                ALU.subtract, ALU.mult)
                        last.then_inc(S["sXN"], 1)
                        v.sem_inc(S["sXF"], 1)
                        for dc in range(DCN):
                            i = tt * 8 + dc
                            v.wait_ge(S["sTR"], i + 1)
                            if tt >= 2:
                                v.wait_ge(S["sSTx"],
                                          16 * (8 * (tt - 2) + dc + 1))
                            ev = v.tensor_copy(
                                xnf_t[tt % 2][:, dc * TTA:(dc + 1) * TTA],
                                tr_ps[i % 2][:, :])
                            ev.then_inc(S["sTRe"], 1)
                            if dc == DCN - 1:
                                v.sem_inc(S["sTRd"], 1)
                        for dc in range(DCN):
                            i = tt * 8 + dc
                            v.wait_ge(S["sPE0"], i * 2 + 1)
                            if i >= 2:
                                v.wait_ge(S["sSTa"], 48 * (i - 1))
                            v.tensor_copy(rk[:, :], kvP[i % 2][:, :])
                            v.tensor_mul(rkv[i % 2][:, :], rk[:, :],
                                         kvP[(i + 1) % 2][:, :]
                                         ).then_inc(S["sKVe"], 1)
                            v.wait_ge(S["sMG"], 4 * i + 1)
                            are = arP[i % 2][:, 0:TTA]
                            aim = arP[i % 2][:, TTA:2 * TTA]
                            v.tensor_add(m2, m2, m3)
                            v.tensor_scalar(m0, m0, -1e-37, 1e-37,
                                            ALU.mult, ALU.add)
                            v.tensor_tensor(m4, are, m0, ALU.add)
                            v.reciprocal(m0, m4)
                            v.tensor_tensor(m3, aim, m0, ALU.mult
                                            ).then_inc(S["sMG"], 1)
                            v.wait_ge(S["sMG"], 4 * i + 3)
                            if i >= 2:
                                v.wait_ge(S["sSTa"], 48 * (i - 1))
                            v.tensor_scalar(m5, m5, -PI / 2, PI / 2,
                                            ALU.mult, ALU.add)
                            v.tensor_mul(m5, m5, m1)
                            v.tensor_add(rth[i % 2][:, :], m0, m5
                                         ).then_inc(S["sMG"], 1)

                @blk.scalar
                def _(a):
                    for tt in range(NTT):
                        if tt == 0:
                            a.wait_ge(S["sP0"], 2)
                        a.wait_ge(S["sLN"], 4 * tt + 1)
                        if tt >= 1:
                            a.wait_ge(S["sMG"], 4 * (8 * tt))
                        last = None
                        for s4 in range(4):
                            last = act(a, lnscr[:, :],
                                       x_tm[:, s4 * D:(s4 + 1) * D],
                                       AF.Square,
                                       accum_out=st4[:, 8 + s4:9 + s4])
                        last.then_inc(S["sLN"], 1)
                        a.wait_ge(S["sLN"], 4 * tt + 3)
                        act(a, st4[:, 20:24], st4[:, 16:20], AF.Sqrt,
                            bias=epsc[:, 0:1]).then_inc(S["sLN"], 1)
                        for dc in range(DCN):
                            i = tt * 8 + dc
                            are = arP[i % 2][:, 0:TTA]
                            aim = arP[i % 2][:, TTA:2 * TTA]
                            a.wait_ge(S["sPE0"], i * 2 + 2)
                            if i >= 1:
                                a.wait_ge(S["sMG"], 4 * i)
                            act(a, m0, are, AF.Sign)
                            act(a, m1, aim, AF.Sign)
                            act(a, m0, m0, AF.Square)
                            act(a, m2, are, AF.Square)
                            act(a, m3, aim, AF.Square).then_inc(S["sMG"], 1)
                            a.wait_ge(S["sMG"], 4 * i + 2)
                            if i >= 2:
                                a.wait_ge(S["sSTa"], 48 * (i - 1))
                            act(a, m2, m2, AF.Sqrt, bias=epsc[:, 2:3])
                            act(a, m5, m4, AF.Sign)
                            act(a, m0, m3, AF.Arctan)
                            act(a, m2, m2, AF.Sigmoid)
                            act(a, rlg[i % 2][:, :], m2, AF.Ln
                                ).then_inc(S["sMG"], 1)

                @blk.tensor
                def _(t):
                    t.wait_ge(S["sQL"], cget("sQL"))
                    for tt in range(NTT):
                        t.wait_ge(S["sXN"], tt + 1)
                        for dc in range(DCN):
                            i = tt * 8 + dc
                            if i >= 2:
                                t.wait_ge(S["sTRe"], i - 1)
                            last = None
                            for s4 in range(4):
                                last = t.transpose(
                                    tr_ps[i % 2][:, s4 * 128:(s4 + 1) * 128],
                                    xn_tmb[tt % 2][:, s4 * D + dc * 128:
                                                   s4 * D + (dc + 1) * 128],
                                    ident[:, :])
                            last.then_inc(S["sTR"], 1)
                        t.wait_ge(S["sTRe"], 8 * (tt + 1))
                        xs = xnf_t[tt % 2]
                        for dc in range(DCN):
                            i = tt * 8 + dc
                            if i >= 1:
                                t.wait_ge(S["sKVe"], i)
                            last = None
                            for (ps, wsb) in ((kvP[i % 2][:, :], wk_sb),
                                              (kvP[(i + 1) % 2][:, :], wv_sb)):
                                for kc in range(8):
                                    last = t.matmul(
                                        ps,
                                        wsb[:, kc * D + dc * 128:
                                            kc * D + dc * 128 + 128],
                                        xs[:, kc * TTA:(kc + 1) * TTA],
                                        start=(kc == 0), stop=(kc == 7))
                            last.then_inc(S["sPE0"], 1)
                            if i >= 2:
                                t.wait_ge(S["sMG"], 4 * (i - 1))
                            last = None
                            for (cps, col0) in ((arP[i % 2][:, 0:TTA],
                                                 dc * 128),
                                                (arP[i % 2][:, TTA:2 * TTA],
                                                 D + dc * 128)):
                                for kc in range(8):
                                    last = t.matmul(
                                        cps,
                                        wa_sb[:, kc * 2 * D + col0:
                                              kc * 2 * D + col0 + 128],
                                        xs[:, kc * TTA:(kc + 1) * TTA],
                                        start=(kc == 0), stop=(kc == 7))
                            last.then_inc(S["sPE0"], 1)

        with ExitStack() as e0b, ExitStack() as e0bp:
            wq_sb = SB(e0b, "wq_sb", [128, 8 * D], BF16)
            wg_sb = SB(e0b, "wg_sb", [128, 8 * D], BF16)
            gwt_sb = SB(e0b, "gwt_sb", [128, 8 * E], BF16)
            xnb_t = [SB(e0b, f"xnb_t{i}", [128, DCN * TTA], BF16)
                     for i in range(2)]
            rq = [SB(e0b, f"rq{i}", [128, TTA], F32) for i in range(2)]
            rsg = [SB(e0b, f"rsg{i}", [128, TTA], F32) for i in range(2)]
            rt = SB(e0b, "rt", [128, 64], F32)
            pq = [e0bp.enter_context(
                nc.psum_tensor(f"pqQ{i}", [128, TTA], F32)) for i in range(2)]
            pg = [e0bp.enter_context(
                nc.psum_tensor(f"pgQ{i}", [128, TTA], F32)) for i in range(2)]
            pr = e0bp.enter_context(nc.psum_tensor("prQ", [128, 16], F32))

            with nc.Block() as blk:
                @blk.gpsimd
                def _(g):
                    g.wait_ge(S["sPE0"], 128)
                    g.wait_ge(S["sMG"], 256)
                    g.wait_ge(S["sSTa"], 48 * 64)
                    g.wait_ge(S["sSTx"], 16 * 64)
                    g.dma_start(wq_sb.ap().rearrange("p (c m) -> p c m", c=8),
                                wq_d.ap().rearrange("(c p) m -> p c m", p=128)
                                ).then_inc(S["sQL"], 16)
                    g.dma_start(wg_sb.ap().rearrange("p (c m) -> p c m", c=8),
                                wg_d.ap().rearrange("(c p) m -> p c m", p=128)
                                ).then_inc(S["sQL"], 16)
                    g.dma_start(gwt_sb.ap().rearrange("p (c e) -> p c e", c=8),
                                gwT_d.ap().rearrange("(c p) e -> p c e", p=128)
                                ).then_inc(S["sQL"], 16)
                    cadd("sQL", 48)
                    g.wait_ge(S["sQL"], cget("sQL"))

                @blk.sync
                def _(s):
                    for tt in range(NTT):
                        s.wait_ge(S["sSTx"], 16 * 8 * (tt + 1))
                        if tt >= 2:
                            s.wait_ge(S["sPQ"], 8 * (tt - 1))
                            s.wait_ge(S["sRT"], 8 * (tt - 1))
                        s.dma_start(
                            xnb_t[tt % 2].ap().rearrange(
                                "p (c t) -> p c t", c=8),
                            xnf_d[:, tt // 2, :,
                                  (tt % 2) * TTA:(tt % 2 + 1) * TTA]
                            .rearrange("c p t -> p c t")
                        ).then_inc(S["sZL"], 16)
                        cadd("sZL", 16)
                        for dc in range(DCN):
                            i = tt * 8 + dc
                            s.wait_ge(S["sQGe"], i * 2 + 2)
                            s.dma_start(stash_d[4, dc, :,
                                                tt * TTA:(tt + 1) * TTA],
                                        rq[i % 2][:, :]).then_inc(S["sSTb"], 16)
                            s.dma_start(stash_d[5, dc, :,
                                                tt * TTA:(tt + 1) * TTA],
                                        rsg[i % 2][:, :]).then_inc(S["sSTb"], 16)
                            cadd("sSTb", 32)

                @blk.tensor
                def _(t):
                    t.wait_ge(S["sQL"], cget("sQL"))
                    for tt in range(NTT):
                        t.wait_ge(S["sZL"], 16 * (tt + 1))
                        xs = xnb_t[tt % 2]
                        for dc in range(DCN):
                            i = tt * 8 + dc
                            if i >= 2:
                                t.wait_ge(S["sQGe"], (i - 1) * 2)
                            last = None
                            for (ps, wsb) in ((pq[i % 2], wq_sb),
                                              (pg[i % 2], wg_sb)):
                                for kc in range(8):
                                    last = t.matmul(
                                        ps[:, :],
                                        wsb[:, kc * D + dc * 128:
                                            kc * D + dc * 128 + 128],
                                        xs[:, kc * TTA:(kc + 1) * TTA],
                                        start=(kc == 0), stop=(kc == 7))
                            last.then_inc(S["sPQ"], 1)
                        for sp in range(2):
                            t.wait_ge(S["sRT"], tt * 8 + sp * 4)
                            last = None
                            for k in range(2):
                                s4 = sp * 2 + k
                                for kc in range(8):
                                    last = t.matmul(
                                        pr[:, k * 8:(k + 1) * 8],
                                        xs[:, kc * TTA + s4 * 128:
                                           kc * TTA + (s4 + 1) * 128],
                                        gwt_sb[:, kc * E:(kc + 1) * E],
                                        start=(kc == 0), stop=(kc == 7))
                            last.then_inc(S["sRT"], 1)

                @blk.vector
                def _(v):
                    for tt in range(NTT):
                        for dc in range(DCN):
                            i = tt * 8 + dc
                            v.wait_ge(S["sPQ"], min(i + 2, (i // 8) * 8 + 8))
                            if i >= 2:
                                v.wait_ge(S["sSTb"], 32 * (i - 1))
                            v.tensor_copy(rq[i % 2][:, :], pq[i % 2][:, :]
                                          ).then_inc(S["sQGe"], 1)
                        for sp in range(2):
                            v.wait_ge(S["sRT"], tt * 8 + sp * 4 + 1)
                            v.reduce_max(rt[:, 0:1], pr[:, 0:8], axis=AX,
                                         negate=True)
                            v.reduce_max(rt[:, 32:33], pr[:, 8:16], axis=AX,
                                         negate=True).then_inc(S["sRT"], 1)
                            v.wait_ge(S["sRT"], tt * 8 + sp * 4 + 3)
                            def ops(B, g128):
                                yield lambda: v.reduce_sum(
                                    rt[:, B + 1:B + 2], rt[:, B + 8:B + 16],
                                    axis=AX)
                                yield lambda: v.reciprocal(
                                    rt[:, B + 2:B + 3], rt[:, B + 1:B + 2])
                                yield lambda: v.tensor_scalar(
                                    rt[:, B + 8:B + 16], rt[:, B + 8:B + 16],
                                    rt[:, B + 2:B + 3], None, ALU.mult)
                                yield lambda: v.reduce_max(
                                    rt[:, B + 3:B + 4], rt[:, B + 8:B + 16],
                                    axis=AX)
                                yield lambda: v.tensor_scalar(
                                    rt[:, B + 16:B + 24], rt[:, B + 8:B + 16],
                                    rt[:, B + 3:B + 4], None, ALU.is_ge)
                                yield lambda: v.tensor_scalar(
                                    rt[:, B + 24:B + 32], rt[:, B + 16:B + 24],
                                    -1.0, 1.0, ALU.mult, ALU.add)
                                yield lambda: v.tensor_mul(
                                    rt[:, B + 24:B + 32], rt[:, B + 8:B + 16],
                                    rt[:, B + 24:B + 32])
                                yield lambda: v.reduce_max(
                                    rt[:, B + 4:B + 5], rt[:, B + 24:B + 32],
                                    axis=AX)
                                yield lambda: v.tensor_scalar(
                                    rt[:, B + 24:B + 32], rt[:, B + 24:B + 32],
                                    rt[:, B + 4:B + 5], None, ALU.is_ge)
                                yield lambda: v.tensor_add(
                                    rt[:, B + 5:B + 6], rt[:, B + 3:B + 4],
                                    rt[:, B + 4:B + 5])
                                yield lambda: v.reciprocal(
                                    rt[:, B + 6:B + 7], rt[:, B + 5:B + 6])
                                yield lambda: v.tensor_mul(
                                    rt[:, B + 3:B + 4], rt[:, B + 3:B + 4],
                                    rt[:, B + 6:B + 7])
                                yield lambda: v.tensor_mul(
                                    rt[:, B + 4:B + 5], rt[:, B + 4:B + 5],
                                    rt[:, B + 6:B + 7])
                                yield lambda: v.tensor_scalar(
                                    rt[:, B + 16:B + 24], rt[:, B + 16:B + 24],
                                    rt[:, B + 3:B + 4], None, ALU.mult)
                                yield lambda: v.tensor_scalar(
                                    rt[:, B + 24:B + 32], rt[:, B + 24:B + 32],
                                    rt[:, B + 4:B + 5], None, ALU.mult)
                                yield lambda: v.tensor_add(
                                    combine[:, g128 * 8:(g128 + 1) * 8],
                                    rt[:, B + 16:B + 24], rt[:, B + 24:B + 32])
                            g0 = tt * 4 + sp * 2
                            it0 = ops(0, g0)
                            it1 = ops(32, g0 + 1)
                            last = None
                            for o0, o1 in zip(it0, it1):
                                last = o0()
                                last = o1()
                            last.then_inc(S["sRT"], 1)

                @blk.scalar
                def _(a):
                    for tt in range(NTT):
                        for dc in range(DCN):
                            i = tt * 8 + dc
                            a.wait_ge(S["sPQ"], min(i + 2, (i // 8) * 8 + 8))
                            if i >= 2:
                                a.wait_ge(S["sSTb"], 32 * (i - 1))
                            act(a, rsg[i % 2][:, :], pg[i % 2][:, :],
                                AF.Silu).then_inc(S["sQGe"], 1)
                        for sp in range(2):
                            a.wait_ge(S["sRT"], tt * 8 + sp * 4 + 2)
                            act(a, rt[:, 8:16], pr[:, 0:8], AF.Exp,
                                bias=rt[:, 0:1])
                            act(a, rt[:, 40:48], pr[:, 8:16], AF.Exp,
                                bias=rt[:, 32:33]).then_inc(S["sRT"], 1)

        streams = {k: [] for k in "tvasg"}
        seqn = [0]
        lastest = {k: 0.0 for k in "tvasg"}

        def ADD(engs, est, fn):
            est = max(est, lastest[engs])
            lastest[engs] = est
            streams[engs].append((est, seqn[0], fn))
            seqn[0] += 1

        with ExitStack() as big, ExitStack() as bigp:
            uscr = SB(big, "uscr", [128, 12 * TTA], F32)
            _sl = [uscr.ap()[:, i * TTA:(i + 1) * TTA] for i in range(12)]
            _map = {0: 0, 1: 1, 3: 2, 4: 3, 5: 4, 6: 5, 7: 6, 8: 7,
                    9: 8, 10: 9, 11: 10, 12: 11, 2: 3}
            U = [_sl[_map[i]] for i in range(13)]
            b_kv = SB(big, "b_kv", [128, TTA], F32)
            csm = SB(big, "csm", [128, 120], F32)
            zt = SB(big, "zt", [128, TTA], BF16)
            agp = uscr.ap()[:, 0:256]
            prr = SB(big, "prr", [128, 64], F32)
            pri = SB(big, "pri", [128, 64], F32)
            mtmp = SB(big, "mtmp", [128, 32], F32)
            selb = SB(big, "selb", [128, 64], F32)
            sg43 = SB(big, "sg43", [4, 4], F32)
            sg43b = SB(big, "sg43b", [4, 4], F32)
            row0 = SB(big, "row0", [1, 128], F32)
            row1 = SB(big, "row1", [1, 128], F32)
            grw = SB(big, "grw", [1, 256], F32)
            gnsr = uscr.ap()[0:1, 0:D]
            gnbr = uscr.ap()[0:1, D:2 * D]
            c1row = uscr.ap()[0:1, 2 * D:3 * D]
            c2row = uscr.ap()[0:1, 3 * D:4 * D]
            xnq = [SB(big, f"xnq{i}", [128, 8 * TTA], BF16) for i in range(3)]
            slab = [SB(big, f"slab{i}", [128, 2 * 8 * TTA], BF16)
                    for i in range(2)]
            dwq = [SB(big, f"dwq{i}", [128, 8 * D], BF16) for i in range(2)]
            acc = SB(big, "acc1", [128, 8 * D], F32)
            acc1_ref = acc
            hls = [SB(big, f"hls{i}", [128, 8 * 128], BF16) for i in range(2)]
            gel = [SB(big, f"gel{i}", [128, TTA], BF16) for i in range(2)]
            ups = [SB(big, f"ups{i}", [128, TTA], BF16) for i in range(2)]
            hstg = [SB(big, f"hstg{i}", [128, TTA], BF16) for i in range(2)]
            gms = [SB(big, "gms0", [128, D], BF16)]
            mfl = [SB(big, "mfl0", [128, DCN * 512], BF16)]
            woo = [SB(big, "woo0", [128, D], F32)]
            wo_sb = SB(big, "wo_sb", [128, 8 * D], BF16)
            zl = SB(big, "zlw", [128, TTA], BF16)
            sgl = SB(big, "sglw", [128, TTA], F32)
            mfb = [SB(big, "mfb0", [128, TTA], BF16)]
            pA = [bigp.enter_context(nc.psum_tensor(f"pA{i}", [128, TTA], F32))
                  for i in range(2)]
            pB = [bigp.enter_context(nc.psum_tensor(f"pB{i}", [128, TTA], F32))
                  for i in range(2)]
            pD = [bigp.enter_context(nc.psum_tensor(f"pD{i}", [128, TTA], F32))
                  for i in range(2)]
            pW = [bigp.enter_context(nc.psum_tensor(f"pW{i}", [128, TTA], F32))
                  for i in range(2)]
            ps43 = pW[1].ap()[0:4, 0:4]

            E0DONE = 960.0

            achain = []

            def ach(eng, cost, fn):
                achain.append((eng, cost, fn))

            def emit_scan(tt, dc):
                i = tt * 8 + dc
                t0 = tt * TTA

                def f_load(s, dc=dc, t0=t0, i=i):
                    if i == 0:
                        s.wait_ge(S["sPE0"], 128)
                        s.wait_ge(S["sMG"], 256)
                        s.wait_ge(S["sQGe"], 128)
                        s.wait_ge(S["sRT"], 64)
                        s.wait_ge(S["sPQ"], 64)
                        s.wait_ge(S["sSTb"], 32 * 64)
                    s.wait_ge(S["sSTa"], 48 * (i + 1))
                    s.dma_start(b_kv[:, :], stash_d[6, dc, :, t0:t0 + TTA]
                                ).then_inc(S["sQL"], 16)
                    s.dma_start(U[0], stash_d[7, dc, :, t0:t0 + TTA]
                                ).then_inc(S["sQL"], 16)
                    s.dma_start(U[5], stash_d[8, dc, :, t0:t0 + TTA]
                                ).then_inc(S["sQL"], 16)
                    cadd("sQL", 48)
                    s.wait_ge(S["sQL"], cget("sQL"))
                    return None
                ach("s", 2.5, f_load)

                def f_d4(v):
                    v.tensor_tensor_scan(U[1], mask64[:, :], U[0],
                                         0.0, ALU.mult, ALU.add)
                    v.tensor_tensor_scan(U[3], mask64[:, :], U[5],
                                         0.0, ALU.mult, ALU.add)
                    v.tensor_scalar(U[4], U[3], INV2PI, MAGIC,
                                    ALU.mult, ALU.add)
                    v.tensor_scalar(U[4], U[4], -MAGIC, None, ALU.add)
                    return v.scalar_tensor_tensor(U[4], U[4], -TWOPI, U[3],
                                                  ALU.mult, ALU.add)
                ach("v", 4.0, f_d4)

                def f_a5(a):
                    act(a, U[6], U[4], AF.Abs)
                    act(a, U[7], U[1], AF.Exp, scale=-1.0)
                    return act(a, U[1], U[1], AF.Exp)
                ach("a", 2.2, f_a5)

                def f_d5(v):
                    return v.tensor_scalar(U[6], U[6], -1.0,
                                           PI / 2, ALU.mult, ALU.add)
                ach("v", 0.8, f_d5)

                def f_a6(a):
                    act(a, U[4], U[4], AF.Sin)
                    return act(a, U[6], U[6], AF.Sin)
                ach("a", 1.6, f_a6)

                def f_d6(v, dc=dc):
                    c = csm
                    v.tensor_mul(U[3], U[7], b_kv[:, :])
                    v.tensor_mul(U[8], U[3], U[6])
                    v.tensor_mul(U[9], U[3], U[4])
                    v.reduce_sum(c[:, 0:8],
                                 U[8].rearrange("p (c j) -> p c j", c=8),
                                 axis=AX)
                    v.reduce_sum(c[:, 8:16],
                                 U[9].rearrange("p (c j) -> p c j", c=8),
                                 axis=AX)
                    eL63 = U[1][:, SUB - 1:TTA:SUB]
                    cos63 = U[6][:, SUB - 1:TTA:SUB]
                    sin63 = U[4][:, SUB - 1:TTA:SUB]
                    v.tensor_mul(c[:, 16:24], eL63, cos63)
                    v.tensor_mul(c[:, 24:32], eL63, sin63)
                    v.tensor_copy(c[:, 48:49], summ[:, 16 + dc:17 + dc])
                    v.tensor_copy(c[:, 57:58], summ[:, 24 + dc:25 + dc])
                    v.tensor_copy(c[:, 66:67], summ[:, 0 + dc:1 + dc])
                    v.tensor_copy(c[:, 75:76], summ[:, 8 + dc:9 + dc])
                    v.tensor_mul(c[:, 84:92], c[:, 16:24], c[:, 0:8])
                    v.tensor_mul(c[:, 32:40], c[:, 24:32], c[:, 8:16])
                    v.tensor_mul(c[:, 96:104], c[:, 24:32], c[:, 0:8])
                    v.tensor_mul(c[:, 40:48], c[:, 16:24], c[:, 8:16])
                    v.tensor_add(c[:, 32:40], c[:, 84:92], c[:, 32:40])
                    v.tensor_sub(c[:, 40:48], c[:, 96:104], c[:, 40:48])
                    for cc in range(NSUB):
                        Aer = c[:, 16 + cc:17 + cc]
                        Aei = c[:, 24 + cc:25 + cc]
                        Hr = c[:, 48 + cc:49 + cc]
                        Hi = c[:, 57 + cc:58 + cc]
                        Pr = c[:, 66 + cc:67 + cc]
                        Pi = c[:, 75 + cc:76 + cc]
                        p1 = c[:, 104:105]; p2 = c[:, 105:106]
                        p3 = c[:, 106:107]; p4 = c[:, 107:108]
                        a1 = c[:, 108:109]; a2 = c[:, 109:110]
                        b1 = c[:, 110:111]; b2 = c[:, 111:112]
                        tA = c[:, 112:113]; tB = c[:, 113:114]
                        v.tensor_mul(p1, Pr, Aer)
                        v.tensor_mul(p2, Pi, Aei)
                        v.tensor_mul(p3, Pr, Aei)
                        v.tensor_mul(p4, Pi, Aer)
                        v.tensor_mul(a1, Aer, Hr)
                        v.tensor_mul(a2, Aei, Hi)
                        v.tensor_mul(b1, Aer, Hi)
                        v.tensor_mul(b2, Aei, Hr)
                        v.tensor_sub(c[:, 67 + cc:68 + cc], p1, p2)
                        v.tensor_add(c[:, 76 + cc:77 + cc], p3, p4)
                        v.tensor_sub(tA, a1, a2)
                        v.tensor_add(tB, b1, b2)
                        v.tensor_add(c[:, 49 + cc:50 + cc], tA,
                                     c[:, 32 + cc:33 + cc])
                        v.tensor_add(c[:, 58 + cc:59 + cc], tB,
                                     c[:, 40 + cc:41 + cc])
                    v.tensor_copy(summ[:, 0 + dc:1 + dc], c[:, 74:75])
                    v.tensor_copy(summ[:, 8 + dc:9 + dc], c[:, 83:84])
                    v.tensor_copy(summ[:, 16 + dc:17 + dc], c[:, 56:57])
                    v.tensor_copy(summ[:, 24 + dc:25 + dc], c[:, 65:66])
                    j0r = U[8][:, 0:TTA:SUB]
                    j0i = U[9][:, 0:TTA:SUB]
                    v.tensor_add(j0r, j0r, c[:, 48:56])
                    v.tensor_sub(j0i, j0i, c[:, 57:65])
                    v.tensor_tensor_scan(U[7], mask64[:, :], U[8],
                                         0.0, ALU.mult, ALU.add)
                    v.tensor_tensor_scan(U[3], mask64[:, :], U[9],
                                         0.0, ALU.mult, ALU.add)
                    v.tensor_mul(U[10], U[1], U[6])
                    v.tensor_mul(U[11], U[1], U[4])
                    v.tensor_mul(U[0], U[10], U[7])
                    v.tensor_mul(U[5], U[11], U[3])
                    v.tensor_add(U[0], U[0], U[5])
                    v.tensor_mul(U[5], U[11], U[7])
                    v.tensor_mul(U[6], U[10], U[3])
                    v.tensor_sub(U[5], U[5], U[6])
                    APrb = c[:, 66:74].unsqueeze(2).broadcast_to(
                        (128, 8, SUB))
                    APib = c[:, 75:83].unsqueeze(2).broadcast_to(
                        (128, 8, SUB))
                    Ar3 = U[10].rearrange("p (c j) -> p c j", c=8)
                    Ai3 = U[11].rearrange("p (c j) -> p c j", c=8)
                    t4v = U[4].rearrange("p (c j) -> p c j", c=8)
                    t6v = U[6].rearrange("p (c j) -> p c j", c=8)
                    t12v = U[12].rearrange("p (c j) -> p c j", c=8)
                    v.tensor_tensor(t4v, Ar3, APrb, ALU.mult)
                    v.tensor_tensor(t6v, Ai3, APib, ALU.mult)
                    v.tensor_sub(U[4], U[4], U[6])
                    v.tensor_tensor(t6v, Ar3, APib, ALU.mult)
                    v.tensor_tensor(t12v, Ai3, APrb, ALU.mult)
                    return v.tensor_add(U[6], U[6], U[12])
                ach("v", 20.0, f_d6)

                def f_st(s, tt=tt, dc=dc):
                    t0 = tt * TTA
                    s.dma_start(stash_d[0, dc, :, t0:t0 + TTA],
                                U[0]).then_inc(S["sSTsc"], 16)
                    s.dma_start(stash_d[1, dc, :, t0:t0 + TTA],
                                U[5]).then_inc(S["sSTsc"], 16)
                    s.dma_start(stash_d[2, dc, :, t0:t0 + TTA],
                                U[4]).then_inc(S["sSTsc"], 16)
                    s.dma_start(stash_d[3, dc, :, t0:t0 + TTA],
                                U[6]).then_inc(S["sSTsc"], 16)
                    cadd("sSTsc", 64)
                    s.wait_ge(S["sSTsc"], cget("sSTsc"))
                    return None
                ach("s", 2.5, f_st)

            for tt in range(NTT):
                for dc in range(DCN):
                    emit_scan(tt, dc)

            x1_marker = len(achain)
            X1EST = 3600.0

            def x1_store(s):
                s.dma_start(summ_in_d[:, :], summ[:, :]).then_inc(S["sQS"], 16)
                cadd("sQS", 16)
                s.wait_ge(S["sQS"], cget("sQS"))
                return None
            ach("s", 1.0, x1_store)

            def x1_coll(g):
                g.collective_compute(
                    "AllGather", ALU.bypass,
                    replica_groups=[list(range(8))],
                    ins=[summ_in_d.ap().opt()],
                    outs=[summ_ag_d.ap().opt()]).then_inc(S["sCC"], 1)
                cadd("sCC", 1)
                g.wait_ge(S["sCC"], cget("sCC"))
                return None
            ach("g", 60.0, x1_coll)

            def x1_load(s):
                s.dma_start(
                    agp.rearrange("p (r c) -> p r c", r=8),
                    summ_ag_d.ap().rearrange("(r p) c -> p r c", p=128)
                ).then_inc(S["sQL"], 16)
                cadd("sQL", 16)
                s.wait_ge(S["sQL"], cget("sQL"))
                return None
            ach("s", 1.0, x1_load)

            def x1_pref(v):
                v.memset(prr[:, :], 0.0)
                v.memset(pri[:, :], 0.0)
                for r in range(1, 8):
                    if r == 4:
                        continue
                    co = (r - 1) * 32
                    def colA(dc):
                        return agp[:, co + dc:co + dc + 1]
                    def colAi(dc):
                        return agp[:, co + 8 + dc:co + 8 + dc + 1]
                    def colH(dc):
                        return agp[:, co + 16 + dc:co + 16 + dc + 1]
                    def colHi(dc):
                        return agp[:, co + 24 + dc:co + 24 + dc + 1]
                    for dc in range(DCN):
                        v.tensor_mul(mtmp[:, dc:dc + 1], colA(dc),
                                     prr[:, dc * 8 + r - 1:dc * 8 + r])
                    for dc in range(DCN):
                        v.tensor_mul(mtmp[:, 8 + dc:9 + dc], colAi(dc),
                                     pri[:, dc * 8 + r - 1:dc * 8 + r])
                    for dc in range(DCN):
                        v.tensor_mul(mtmp[:, 16 + dc:17 + dc], colA(dc),
                                     pri[:, dc * 8 + r - 1:dc * 8 + r])
                    for dc in range(DCN):
                        v.tensor_mul(mtmp[:, 24 + dc:25 + dc], colAi(dc),
                                     prr[:, dc * 8 + r - 1:dc * 8 + r])
                    for dc in range(DCN):
                        v.tensor_sub(mtmp[:, dc:dc + 1], mtmp[:, dc:dc + 1],
                                     mtmp[:, 8 + dc:9 + dc])
                    for dc in range(DCN):
                        v.tensor_add(mtmp[:, 16 + dc:17 + dc],
                                     mtmp[:, 16 + dc:17 + dc],
                                     mtmp[:, 24 + dc:25 + dc])
                    for dc in range(DCN):
                        v.tensor_add(prr[:, dc * 8 + r:dc * 8 + r + 1],
                                     mtmp[:, dc:dc + 1], colH(dc))
                    for dc in range(DCN):
                        v.tensor_add(pri[:, dc * 8 + r:dc * 8 + r + 1],
                                     mtmp[:, 16 + dc:17 + dc], colHi(dc))
                for dc in range(DCN):
                    v.tensor_mul(selb[:, dc * 8:(dc + 1) * 8],
                                 prr[:, dc * 8:(dc + 1) * 8], onehot[:, :])
                for dc in range(DCN):
                    v.reduce_sum(hg[:, 2 * dc:2 * dc + 1],
                                 selb[:, dc * 8:(dc + 1) * 8], axis=AX)
                for dc in range(DCN):
                    v.tensor_mul(selb[:, dc * 8:(dc + 1) * 8],
                                 pri[:, dc * 8:(dc + 1) * 8], onehot[:, :])
                last = None
                for dc in range(DCN):
                    last = v.reduce_sum(hg[:, 2 * dc + 1:2 * dc + 2],
                                        selb[:, dc * 8:(dc + 1) * 8], axis=AX)
                return last
            ach("v", 25.0, x1_pref)

            l_h0r = U[1]; l_h0i = U[2]; l_acr = U[3]
            l_aci = U[7]; l_q = U[8]; l_sg = U[9]
            w_1 = U[10]; w_2 = U[11]; w_3 = U[12]
            acc4 = csm.ap()[:, 0:4]

            def emit_a2(tt, dc):
                t0 = tt * TTA
                i = tt * 8 + dc

                def f_l(s, dc=dc, t0=t0, i=i):
                    if i == 0:
                        s.wait_ge(S["sSTsc"], cget("sSTsc"))
                        s.wait_ge(S["sSTb"], 32 * 64)
                    for qi, dst in ((0, l_h0r), (1, l_h0i), (2, l_acr),
                                    (3, l_aci), (4, l_q), (5, l_sg)):
                        s.dma_start(dst, stash_d[qi, dc, :, t0:t0 + TTA]
                                    ).then_inc(S["sQL"], 16)
                        cadd("sQL", 16)
                    s.wait_ge(S["sQL"], cget("sQL"))
                    return None
                ach("s", 4.0, f_l)

                def f_d(v, dc=dc):
                    hgr = hg[:, 2 * dc:2 * dc + 1]
                    hgi = hg[:, 2 * dc + 1:2 * dc + 2]
                    v.tensor_scalar(w_1, l_acr, hgr, None, ALU.mult)
                    v.tensor_scalar(w_2, l_aci, hgi, None, ALU.mult)
                    v.tensor_sub(w_1, w_1, w_2)
                    v.tensor_add(w_1, w_1, l_h0r)
                    v.tensor_scalar(w_2, l_acr, hgi, None, ALU.mult)
                    v.tensor_scalar(w_3, l_aci, hgr, None, ALU.mult)
                    v.tensor_add(w_2, w_2, w_3)
                    v.tensor_add(w_2, w_2, l_h0i)
                    v.tensor_mul(w_1, l_q, w_1)
                    v.tensor_mul(w_2, l_q, w_2)
                    return v.tensor_mul(zt[:, :], w_1, l_sg)
                ach("v", 8.0, f_d)

                def f_a(a):
                    act(a, w_3, w_1, AF.Copy, accum_out=acc4[:, 0:1])
                    act(a, w_3, w_2, AF.Copy, accum_out=acc4[:, 1:2])
                    act(a, w_3, w_1, AF.Square, accum_out=acc4[:, 2:3])
                    return act(a, w_3, w_2, AF.Square, accum_out=acc4[:, 3:4])
                ach("a", 3.0, f_a)

                def f_d2(v, dc=dc):
                    return v.tensor_add(stats_acc[:, dc * 4:(dc + 1) * 4],
                                        stats_acc[:, dc * 4:(dc + 1) * 4],
                                        acc4[:, :])
                ach("v", 0.4, f_d2)

                def f_s(s, dc=dc, t0=t0):
                    s.dma_start(z_d[dc, :, t0:t0 + TTA], zt[:, :]
                                ).then_inc(S["sQS"], 16)
                    cadd("sQS", 16)
                    s.wait_ge(S["sQS"], cget("sQS"))
                    return None
                ach("s", 1.2, f_s)

            for tt in range(NTT):
                for dc in range(DCN):
                    emit_a2(tt, dc)

            X2EST = 6000.0

            def x2_mm(dc):
                def f_t(t, dc=dc):
                    return t.matmul(ps43, gmap[:, :],
                                    stats_acc[:, dc * 4:(dc + 1) * 4],
                                    start=True, stop=True)
                ach("t", 0.3, f_t)

                def f_v(v):
                    v.tensor_scalar(sg43[:, :], ps43, bsel[0:4, 0:1], None,
                                    ALU.mult)
                    return v.tensor_scalar(sg43b[:, :], ps43, bsel[0:4, 1:2],
                                           None, ALU.mult)
                ach("v", 0.3, f_v)

                def f_s(s, dc=dc):
                    s.dma_start(
                        stats_in_d.ap()[0:1, dc * 16:(dc + 1) * 16]
                        .rearrange("o (a b) -> a (o b)", a=4),
                        sg43[:, :]).then_inc(S["sQS"], 16)
                    s.dma_start(
                        stats_in_d.ap()[1:2, dc * 16:(dc + 1) * 16]
                        .rearrange("o (a b) -> a (o b)", a=4),
                        sg43b[:, :]).then_inc(S["sQS"], 16)
                    cadd("sQS", 32)
                    s.wait_ge(S["sQS"], cget("sQS"))
                    return None
                ach("s", 0.6, f_s)

            achain.append(("v", 0.0, None))
            achain.pop()
            x2_start_marker = len(achain)
            for dc in range(DCN):
                x2_mm(dc)

            def x2_coll(g):
                g.collective_compute(
                    "AllReduce", ALU.add,
                    replica_groups=[list(range(8))],
                    ins=[stats_in_d.ap().opt()],
                    outs=[stats_out_d.ap().opt()]).then_inc(S["sCC"], 1)
                cadd("sCC", 1)
                g.wait_ge(S["sCC"], cget("sCC"))
                return None
            ach("g", 60.0, x2_coll)

            def x2_load(s):
                s.dma_start(row0[:, :], stats_out_d[0:1, :]
                            ).then_inc(S["sQL"], 16)
                s.dma_start(row1[:, :], stats_out_d[1:2, :]
                            ).then_inc(S["sQL"], 16)
                s.dma_start(gnsr[:, :], gnsr_d[:, :]).then_inc(S["sQL"], 16)
                s.dma_start(gnbr[:, :], gnbr_d[:, :]).then_inc(S["sQL"], 16)
                cadd("sQL", 64)
                s.wait_ge(S["sQL"], cget("sQL"))
                return None
            ach("s", 2.0, x2_load)

            def x2_v1(v):
                v.tensor_scalar(row0[:, :], row0[:, :], bsel[0:1, 0:1],
                                None, ALU.mult)
                v.tensor_scalar(row1[:, :], row1[:, :], bsel[0:1, 1:2],
                                None, ALU.mult)
                v.tensor_add(row0[:, :], row0[:, :], row1[:, :])
                r4 = row0.ap().rearrange("o (g j) -> o g j", g=32)
                v.tensor_tensor(grw.ap()[0:1, 160:192].unsqueeze(2),
                                r4[:, :, 2:3], r4[:, :, 3:4], ALU.add)
                v.tensor_scalar(grw.ap()[0:1, 0:32].unsqueeze(2),
                                r4[:, :, 0:1], CNT_INV, None, ALU.mult)
                v.tensor_scalar(grw.ap()[0:1, 32:64].unsqueeze(2),
                                r4[:, :, 1:2], CNT_INV, None, ALU.mult)
                v.tensor_scalar(grw[0:1, 64:96], grw[0:1, 160:192],
                                CNT_INV, None, ALU.mult)
                v.tensor_mul(grw[0:1, 96:128], grw[0:1, 0:32],
                             grw[0:1, 0:32])
                v.tensor_mul(grw[0:1, 128:160], grw[0:1, 32:64],
                             grw[0:1, 32:64])
                v.tensor_sub(grw[0:1, 64:96], grw[0:1, 64:96],
                             grw[0:1, 96:128])
                v.memset(grw[0:1, 224:225], 0.0)
                return v.tensor_sub(grw[0:1, 64:96], grw[0:1, 64:96],
                                    grw[0:1, 128:160])
            ach("v", 3.0, x2_v1)

            def x2_a1(a):
                return act(a, grw[0:1, 96:128], grw[0:1, 64:96], AF.Sqrt,
                           bias=epsc[0:1, 1:2])
            ach("a", 1.0, x2_a1)

            def x2_v2(v):
                v.reciprocal(grw[0:1, 192:224], grw[0:1, 96:128])
                rstd_b = grw.ap()[0:1, 192:224].unsqueeze(2).broadcast_to(
                    (1, 32, 32))
                mre_b = grw.ap()[0:1, 0:32].unsqueeze(2).broadcast_to(
                    (1, 32, 32))
                g3 = gnsr.rearrange("o (g j) -> o g j", g=32)
                c13 = c1row.rearrange("o (g j) -> o g j", g=32)
                c23 = c2row.rearrange("o (g j) -> o g j", g=32)
                v.memset(grw[0:1, 224:225], 0.0)
                v.tensor_tensor(c13, g3, rstd_b, ALU.mult)
                v.memset(grw[0:1, 225:226], 0.0)
                v.tensor_tensor(c23, c13, mre_b, ALU.mult)
                v.memset(grw[0:1, 226:227], 0.0)
                return v.tensor_sub(c2row[:, :], gnbr[:, :], c2row[:, :])
            ach("v", 3.0, x2_v2)

            def x2_st(s):
                s.dma_start(crow_d[0:1, :], c1row[:, :]).then_inc(S["sQS"], 16)
                s.dma_start(crow_d[1:2, :], c2row[:, :]).then_inc(S["sQS"], 16)
                cadd("sQS", 32)
                s.wait_ge(S["sQS"], cget("sQS"))
                for dc in range(DCN):
                    s.dma_start(c1pp[:, dc:dc + 1],
                                crow_d[0:1, dc * 128:(dc + 1) * 128]
                                .rearrange("o p -> p o")).then_inc(S["sQL"], 16)
                    s.dma_start(c2pp[:, dc:dc + 1],
                                crow_d[1:2, dc * 128:(dc + 1) * 128]
                                .rearrange("o p -> p o")).then_inc(S["sQL"], 16)
                    cadd("sQL", 32)
                s.wait_ge(S["sQL"], cget("sQL"))
                return None
            ach("s", 3.0, x2_st)

            def emit_wop(tt):
                t0 = tt * TTA

                def f_one(s, dc, t0=t0):
                    s.dma_start(zl[:, :], z_d[dc, :, t0:t0 + TTA]
                                ).then_inc(S["sQL"], 16)
                    s.dma_start(sgl[:, :], stash_d[5, dc, :, t0:t0 + TTA]
                                ).then_inc(S["sQL"], 16)
                    cadd("sQL", 32)
                    s.wait_ge(S["sQL"], cget("sQL"))
                    return None

                for dc in range(DCN):
                    def f_ld(s, dc=dc):
                        return f_one(s, dc)
                    ach("s", 1.5, f_ld)

                    def f_m(v, dc=dc):
                        v.tensor_scalar(U[1], zl[:, :],
                                        c1pp[:, dc:dc + 1], None, ALU.mult)
                        v.tensor_scalar(U[2], sgl[:, :],
                                        c2pp[:, dc:dc + 1], None, ALU.mult)
                        return v.tensor_add(mfb[0][:, :], U[1], U[2])
                    ach("v", 2.2, f_m)

                    def f_st2(s, tt=tt, dc=dc):
                        s.dma_start(mfm_d[tt, dc, :, :], mfb[0][:, :]
                                    ).then_inc(S["sQS"], 16)
                        cadd("sQS", 16)
                        s.wait_ge(S["sQS"], cget("sQS"))
                        return None
                    ach("s", 0.8, f_st2)

            for tt in range(NTT):
                emit_wop(tt)

            aidx = [0]
            acursor = [E0DONE]

            def chain_stage(eng, fn, idx):
                def run(engine, fn=fn, idx=idx):
                    engine.wait_ge(S["sAC"], idx)
                    last = fn(engine)
                    if last is not None:
                        last.then_inc(S["sAC"], 1)
                    else:
                        engine.sem_inc(S["sAC"], 1)
                return run

            for n, (eng, cost, fn) in enumerate(achain):
                if n == x1_marker:
                    acursor[0] = max(acursor[0], X1EST)
                if n == x2_start_marker:
                    acursor[0] = max(acursor[0], X2EST)
                acursor[0] += cost
                ADD(eng, acursor[0], chain_stage(eng, fn, n))
            ACH_TOTAL = len(achain)

            E1T0 = 1000.0
            GUS = 2.3
            cnt_g = [0]
            cnt_sb = [0]
            cnt_hst = [0]
            cnt_hls = [0]
            cnt_dwq = [0]
            cnt_j2 = [0]
            cnt_xq = [0]
            pend_st = [None]
            est1 = [E1T0]

            pe_j2_at_hls = {}

            def gu_slab(tcb, e, fb):
                sb = cnt_sb[0]; cnt_sb[0] += 1
                est = est1[0] + (0 if sb < 2 else 0)

                def f_g(g, tcb=tcb, e=e, fb=fb, sb=sb):
                    if sb == 0:
                        g.wait_ge(S["sQGe"], 128)
                        g.wait_ge(S["sRT"], 64)
                    if sb >= 2:
                        g.wait_ge(S["sPE1"], 16 * (sb - 1))
                    g.dma_start(
                        slab[sb % 2].ap()[:, 0:8 * TTA]
                        .rearrange("p (c f) -> p c f", c=8),
                        guT_d[e, :, fb * TTA:(fb + 1) * TTA]
                        .rearrange("(c p) f -> p c f", p=128)
                        ).then_inc(S["sGW"], 16)
                    g.dma_start(
                        slab[sb % 2].ap()[:, 8 * TTA:16 * TTA]
                        .rearrange("p (c f) -> p c f", c=8),
                        guT_d[e, :, FF + fb * TTA:FF + (fb + 1) * TTA]
                        .rearrange("(c p) f -> p c f", p=128)
                        ).then_inc(S["sGW"], 16)
                    return None
                ADD("g", est - 2 * GUS * 16, f_g)
                return sb

            def gu_groups(tcb, e, fb, sb):
                for tc in range(2):
                    xq = tcb * 2 + tc
                    if xq >= cnt_xq[0]:
                        cnt_xq[0] = xq + 1

                        def f_xq(s, tcb=tcb, tc=tc, xq=xq):
                            if xq == 0:
                                s.wait_ge(S["sSTx"], 16 * 64)
                                s.wait_ge(S["sPQ"], 64)
                                s.wait_ge(S["sQGe"], 128)
                                s.wait_ge(S["sRT"], 64)
                            if xq >= 3:
                                s.wait_ge(S["sPE1"],
                                          1024 * ((xq - 3) // 2 + 1))
                            s.dma_start(
                                xnq[xq % 3].ap().rearrange(
                                    "p (c t) -> p c t", c=8),
                                xnf_d[:, tcb, :, tc * TTA:(tc + 1) * TTA]
                                .rearrange("c p t -> p c t")
                            ).then_inc(S["sZL"], 16)
                            cadd("sZL", 16)
                            return None
                        ADD("s", est1[0] - 2 * GUS * 16, f_xq)
                for tc in range(2):
                    xq = tcb * 2 + tc
                    for fc4 in range(4):
                        fc = fb * 4 + fc4
                        for gu in range(2):
                            gidx = cnt_g[0]; cnt_g[0] += 1
                            pair = gidx // 2
                            est1[0] += GUS

                            def f_t(t, sb=sb, tc=tc, fc4=fc4, gu=gu,
                                    gidx=gidx, pair=pair, xq=xq):
                                t.wait_ge(S["sGW"], 32 * (sb + 1))
                                t.wait_ge(S["sZL"],
                                          128 + 16 * (xq + 1))
                                ps = (pA if gu == 0 else pB)[pair % 2]
                                sem = S["sACg"] if gu == 0 else S["sACu"]
                                if pair >= 2:
                                    t.wait_ge(sem, pair - 1)
                                off = gu * 8 * TTA
                                last = None
                                for kc in range(8):
                                    last = t.matmul(
                                        ps[:, :],
                                        slab[sb % 2][:, off + kc * TTA +
                                                     fc4 * 128:
                                                     off + kc * TTA +
                                                     (fc4 + 1) * 128],
                                        xnq[xq % 3][:, kc * TTA:
                                                    (kc + 1) * TTA],
                                        start=(kc == 0), stop=(kc == 7))
                                last.then_inc(S["sPE1"], 1)
                                return None
                            ADD("t", est1[0], f_t)

                            def f_a(a, gu=gu, pair=pair, gidx=gidx):
                                a.wait_ge(S["sPE1"], min(gidx + 2, (gidx // 16 + 1) * 16))
                                if pair >= 2:
                                    a.wait_ge(S["sGM"], pair - 1)
                                if gu == 0:
                                    act(a, gel[pair % 2][:, :],
                                        pA[pair % 2][:, :],
                                        AF.Gelu).then_inc(S["sACg"], 1)
                                else:
                                    act(a, ups[pair % 2][:, :],
                                        pB[pair % 2][:, :],
                                        AF.Copy).then_inc(S["sACu"], 1)
                                return None
                            ADD("a", est1[0] + 0.3, f_a)

                        pair = (cnt_g[0] - 1) // 2
                        hst = cnt_hst[0]; cnt_hst[0] += 1

                        def f_gm(g, pair=pair, tcb=tcb, e=e, tc=tc, fc=fc,
                                 hst=hst):
                            cap = (pair // 8 + 1) * 8
                            g.wait_ge(S["sACg"], min(pair + 2, cap))
                            g.wait_ge(S["sACu"], min(pair + 2, cap))
                            if pair >= 2:
                                g.wait_ge(S["sHS"], 16 * (pair - 1))
                            g.tensor_mul(hstg[pair % 2][:, :],
                                         ups[pair % 2][:, :],
                                         gel[pair % 2][:, :]
                                         ).then_inc(S["sGM"], 1)
                            if pend_st[0] is not None:
                                pend_st[0](g)
                                pend_st[0] = None

                            def st(g2, pair=pair, e=e, tc=tc, fc=fc):
                                g2.wait_ge(S["sGM"], pair + 1)
                                g2.dma_start(
                                    hact_d[e % 2, fc // 8,
                                           tc * 4:(tc + 1) * 4,
                                           fc % 8, :, :]
                                    .rearrange("t8 p t -> p t8 t"),
                                    hstg[pair % 2].ap().rearrange(
                                        "p (t8 t) -> p t8 t", t8=4)
                                    ).then_inc(S["sHS"], 16)
                            pend_st[0] = st
                            return None
                        ADD("g", est1[0] + 0.6, f_gm)

            def down_pass(tcb, e, q):
                dq = cnt_dwq[0]; cnt_dwq[0] += 1

                def f_dw(g, e=e, q=q, dq=dq):
                    if pend_st[0] is not None:
                        pend_st[0](g)
                        pend_st[0] = None
                    if dq >= 2:
                        g.wait_ge(S["sPE2"], (dq - 1) * 16)
                    g.dma_start(
                        dwq[dq % 2].ap().rearrange("p (c h) -> p c h", c=8),
                        dwq_d[e, q * 8:(q + 1) * 8, :, :]
                        .rearrange("c p h -> p c h")
                        ).then_inc(S["sDWQ"], 16)
                    return None
                ADD("g", est1[0] - GUS * 30, f_dw)

                stneed = (tcb * 8 + e) * 64 + (q + 1) * 8 * 2

                for t8 in range(8):
                    hl = cnt_hls[0]; cnt_hls[0] += 1

                    def f_hl(s, tcb=tcb, e=e, q=q, t8=t8, hl=hl,
                             stneed=stneed):
                        if hl >= 2:
                            s.wait_ge(S["sPE2"], (hl - 1) * 2)
                        s.wait_ge(S["sHS"], 16 * stneed)
                        s.dma_start(
                            hls[hl % 2].ap().rearrange(
                                "p (c t) -> p c t", c=8),
                            hact_d[e % 2, q, t8, :, :, :]
                            .rearrange("c p t -> p c t")
                            ).then_inc(S["sDH"], 16)
                        return None
                    ADD("s", est1[0] - GUS * 2, f_hl)

                    est1[0] += 2.9
                    for h2 in range(2):
                        j2 = cnt_j2[0]; cnt_j2[0] += 1

                        def f_t(t, hl=hl, h2=h2, j2=j2, dq=dq):
                            t.wait_ge(S["sDH"], 16 * (hl + 1))
                            t.wait_ge(S["sDWQ"], 16 * (dq + 1))
                            if j2 >= 2:
                                t.wait_ge(S["sDV2"], j2 - 1)
                            last = None
                            for fcm in range(8):
                                last = t.matmul(
                                    pD[j2 % 2][:, :],
                                    hls[hl % 2][:, fcm * 128:(fcm + 1) * 128],
                                    dwq[dq % 2][:, fcm * D + h2 * TTA:
                                                fcm * D + (h2 + 1) * TTA],
                                    start=(fcm == 0), stop=(fcm == 7))
                            last.then_inc(S["sPE2"], 1)
                            return None
                        ADD("t", est1[0], f_t)

                        def f_v(v, tcb=tcb, e=e, q=q, t8=t8, h2=h2, j2=j2):
                            v.wait_ge(S["sPE2"], min(j2 + 2, (j2 // 512 + 1) * 512))
                            first = (e == 0 and q == 0)
                            cslice = combine[:, (tcb * 8 + t8) * 8 + e:
                                             (tcb * 8 + t8) * 8 + e + 1]
                            dst = acc[:, t8 * D + h2 * TTA:
                                      t8 * D + (h2 + 1) * TTA]
                            if first:
                                if tcb > 0:
                                    v.wait_ge(S["sSIL"], 8 * tcb)
                                v.tensor_scalar(dst, pD[j2 % 2][:, :],
                                                cslice, None, ALU.mult
                                                ).then_inc(S["sDV2"], 1)
                            else:
                                v.scalar_tensor_tensor(
                                    dst, pD[j2 % 2][:, :], cslice, dst,
                                    ALU.mult, ALU.add).then_inc(S["sDV2"], 1)
                            return None
                        ADD("v", est1[0] + 0.3, f_v)

            def quarter_silu(tcb):
                for t8 in range(8):
                    def f_sil(a, tcb=tcb, t8=t8):
                        if t8 == 0:
                            a.wait_ge(S["sDV2"], 512 * (tcb + 1))
                        if tcb * 8 + t8 >= 1:
                            a.wait_ge(S["sGMS"], 16 * (tcb * 8 + t8))
                        act(a, gms[0][:, :], acc[:, t8 * D:(t8 + 1) * D],
                            AF.Silu).then_inc(S["sSIL"], 1)
                        a.wait_ge(S["sSIL"], tcb * 8 + t8 + 1)
                        a.dma_start(gmoe_d[tcb * 8 + t8, :, :],
                                    gms[0][:, :]).then_inc(S["sGMS"], 16)
                        return None
                    ADD("a", est1[0] + 2.0 + t8 * 1.2, f_sil)

            def f_warm(t):
                for ps in (pA[0], pA[1], pB[0], pB[1], pD[0], pD[1],
                           pW[0], pW[1]):
                    for c4 in range(4):
                        t.matmul(ps[:, c4 * 128:(c4 + 1) * 128],
                                 ident[:, :], ident[:, :],
                                 start=True, stop=True)
                return None
            ADD("t", E0DONE + 2.0, f_warm)

            for tcb in range(NQT):
                for e in range(E):
                    sbs = []
                    plan = ["s0", "s1", "s2", "s3", "p0", "s4", "s5", "p1",
                            "s6", "s7", "p2", "p3"]
                    for item in plan:
                        k = int(item[1])
                        if item[0] == "s":
                            sb = gu_slab(tcb, e, k)
                            gu_groups(tcb, e, k, sb)
                        else:
                            down_pass(tcb, e, k)
                quarter_silu(tcb)

            GU_TOTAL = cnt_g[0]
            J2_TOTAL = cnt_j2[0]

            woest = [6400.0]

            def f_wo_w(g):
                g.wait_ge(S["sQGe"], 128)
                g.wait_ge(S["sRT"], 64)
                g.dma_start(wo_sb.ap().rearrange("p (c m) -> p c m", c=8),
                            wo_d.ap().rearrange("(c p) m -> p c m", p=128)
                            ).then_inc(S["sWL"], 16)
                return None
            ADD("g", 1500.0, f_wo_w)

            cnt_wo = [0]
            for tt in range(NTT):
                def f_ml(s, tt=tt):
                    if tt == 0:
                        s.wait_ge(S["sAC"], ACH_TOTAL)
                    if tt >= 1:
                        s.wait_ge(S["sWPE"], tt * 8)
                    s.dma_start(
                        mfl[0].ap().rearrange("p (c t) -> p c t", c=8),
                        mfm_d[tt].rearrange("c p t -> p c t")
                    ).then_inc(S["sWL"], 16)
                    return None
                ADD("s", woest[0], f_ml)
                woest[0] += 4.0

                for s4 in range(4):
                    for half in range(2):
                        w8 = cnt_wo[0]; cnt_wo[0] += 1

                        def f_pe(t, tt=tt, s4=s4, half=half, w8=w8):
                            t.wait_ge(S["sWL"], 16 * (tt + 2))
                            if w8 >= 2:
                                t.wait_ge(S["sWEV"], w8 - 1)
                            last = None
                            for dc in range(DCN):
                                last = t.matmul(
                                    pW[w8 % 2][:, :],
                                    mfl[0][:, dc * 512 + s4 * 128:
                                           dc * 512 + (s4 + 1) * 128],
                                    wo_sb[:, dc * D + half * 512:
                                          dc * D + (half + 1) * 512],
                                    start=(dc == 0), stop=(dc == 7))
                            last.then_inc(S["sWPE"], 1)
                            return None
                        woest[0] += 2.4
                        ADD("t", woest[0], f_pe)

                        def f_ev(v, s4=s4, half=half, w8=w8):
                            v.wait_ge(S["sWPE"], w8 + 1)
                            if w8 >= 2:
                                v.wait_ge(S["sWST"], 16 * (w8 // 2))
                            v.tensor_copy(
                                woo[0][:, half * 512:(half + 1) * 512],
                                pW[w8 % 2][:, :]).then_inc(S["sWEV"], 1)
                            return None
                        ADD("v", woest[0] + 0.4, f_ev)

                    def f_st(s, tt=tt, s4=s4):
                        s.wait_ge(S["sWEV"], tt * 8 + (s4 + 1) * 2)
                        s.dma_start(wo_out_d[tt * 4 + s4, :, :],
                                    woo[0][:, :]).then_inc(S["sWST"], 16)
                        return None
                    ADD("s", woest[0] + 0.8, f_st)

            for k in streams:
                streams[k].sort(key=lambda x: (x[0], x[1]))

            with nc.Block() as blk:
                @blk.tensor
                def _(t):
                    for _, _, fn in streams["t"]:
                        fn(t)

                @blk.vector
                def _(v):
                    for _, _, fn in streams["v"]:
                        fn(v)

                @blk.scalar
                def _(a):
                    for _, _, fn in streams["a"]:
                        fn(a)

                @blk.sync
                def _(s):
                    for _, _, fn in streams["s"]:
                        fn(s)

                @blk.gpsimd
                def _(g):
                    for _, _, fn in streams["g"]:
                        fn(g)

        with ExitStack() as bs, ExitStack() as bsp:
            w1_sb = SB(bs, "w1_sb", [128, 8 * FF], BF16)
            w2_sb = SB(bs, "w2_sb", [128, 32 * D], BF16)
            b1c = SB(bs, "b1c_sb", [128, 32], F32)
            b2b = SB(bs, "b2b_sb", [128, D], BF16)
            l1sb = SB(bs, "l1sb_sb", [128, D], BF16)
            l1bb = SB(bs, "l1bb_sb", [128, D], BF16)
            l2sb = SB(bs, "l2sb_sb", [128, D], BF16)
            l2bb = SB(bs, "l2bb_sb", [128, D], BF16)
            xb_tm = SB(bs, "xb_tm", [128, 2 * D], F32)
            wol = SB(bs, "wol", [128, 2 * D], F32)
            gml = SB(bs, "gml", [128, 2 * D], BF16)
            t1b = SB(bs, "t1b", [128, 2 * D], F32)
            xh_tm = SB(bs, "xh_tm", [128, 2 * D], F32)
            xh_bf = SB(bs, "xh_bf", [128, 2 * D], BF16)
            xh_fm = SB(bs, "xh_fm", [128, 8 * TTB], BF16)
            hh_fm = SB(bs, "hh_fm", [128, 32 * TTB], BF16)
            stb = SB(bs, "stb", [128, 32], F32)
            eps2 = SB(bs, "eps2", [128, 2], F32)
            ps_big = bsp.enter_context(
                nc.psum_tensor("ps_big", [128, 2 * D], F32))
            trb_ps = bsp.enter_context(
                nc.psum_tensor("trb_ps", [128, 2 * D], BF16))
            p1 = [bsp.enter_context(
                nc.psum_tensor(f"p1_{i}", [128, TTB], F32)) for i in range(2)]

            sched = []
            bq0 = cget("sQL")
            cadd("sQL", 128)

            def f_w(g):
                g.wait_ge(S["sPE1"], GU_TOTAL)
                g.wait_ge(S["sPE2"], J2_TOTAL)
                g.wait_ge(S["sWPE"], 64)
                g.wait_ge(S["sSIL"], 32)
                g.wait_ge(S["sWST"], 16 * 32)
                g.wait_ge(S["sGMS"], 16 * 32)
                g.wait_ge(S["sHS"], 16 * 2048)
                g.wait_ge(S["sGM"], 2048)
                g.wait_ge(S["sAC"], ACH_TOTAL)
                g.dma_start(w1_sb.ap().rearrange("p (c m) -> p c m", c=8),
                            w1_d.ap().rearrange("(c p) m -> p c m", p=128)
                            ).then_inc(S["sQL"], 16)
                g.dma_start(w2_sb.ap().rearrange("p (c m) -> p c m", c=32),
                            w2_d.ap().rearrange("(c p) m -> p c m", p=128)
                            ).then_inc(S["sQL"], 16)
                g.dma_start(b1c[:, :], b1c_d[:, :]).then_inc(S["sQL"], 16)
                g.dma_start(b2b[:, :], b2b_d[:, :]).then_inc(S["sQL"], 16)
                g.dma_start(l1sb[:, :], l1sb_d[:, :]).then_inc(S["sQL"], 16)
                g.dma_start(l1bb[:, :], l1bb_d[:, :]).then_inc(S["sQL"], 16)
                g.dma_start(l2sb[:, :], l2sb_d[:, :]).then_inc(S["sQL"], 16)
                g.dma_start(l2bb[:, :], l2bb_d[:, :]).then_inc(S["sQL"], 16)
                g.wait_ge(S["sQL"], bq0 + 128)
                return None
            sched.append(("g", f_w))

            def f_eps(v):
                v.memset(eps2[:, 0:1], 1e-5)
                v.memset(eps2[:, 1:2], 1e-6)
                return None
            sched.append(("v", f_eps))

            def emit_ln_tm(sched, src, dst, epscol, scale_t, bias_t):
                nsub = 2

                def f1(v, src=src):
                    return v.reduce_sum(
                        stb[:, 0:2],
                        src.ap().rearrange("p (s d) -> p s d", s=nsub),
                        axis=AX)
                sched.append(("v", f1))

                def f2(a, src=src, dst=dst):
                    last = None
                    for s4 in range(nsub):
                        last = act(a, dst[:, s4 * D:(s4 + 1) * D],
                                   src[:, s4 * D:(s4 + 1) * D], AF.Square,
                                   accum_out=stb[:, 4 + s4:5 + s4])
                    return last
                sched.append(("a", f2))

                def f3(v):
                    v.tensor_scalar(stb[:, 2:4], stb[:, 0:2], 1.0 / D, None,
                                    ALU.mult)
                    v.tensor_scalar(stb[:, 6:8], stb[:, 4:6], 1.0 / D, None,
                                    ALU.mult)
                    v.tensor_mul(stb[:, 8:10], stb[:, 2:4], stb[:, 2:4])
                    v.memset(stb[:, 30:31], 0.0)
                    return v.tensor_sub(stb[:, 8:10], stb[:, 6:8],
                                        stb[:, 8:10])
                sched.append(("v", f3))

                def f4(a, epscol=epscol):
                    return act(a, stb[:, 10:12], stb[:, 8:10], AF.Sqrt,
                               bias=eps2[:, epscol:epscol + 1])
                sched.append(("a", f4))

                def f5(v, src=src, dst=dst, scale_t=scale_t, bias_t=bias_t):
                    v.reciprocal(stb[:, 12:14], stb[:, 10:12])
                    v.memset(stb[:, 30:31], 0.0)
                    last = None
                    for s4 in range(nsub):
                        last = v.tensor_scalar(dst[:, s4 * D:(s4 + 1) * D],
                                               src[:, s4 * D:(s4 + 1) * D],
                                               stb[:, 2 + s4:3 + s4],
                                               stb[:, 12 + s4:13 + s4],
                                               ALU.subtract, ALU.mult)
                    if scale_t is not None:
                        for s4 in range(nsub):
                            v.tensor_mul(dst[:, s4 * D:(s4 + 1) * D],
                                         dst[:, s4 * D:(s4 + 1) * D],
                                         scale_t[:, :])
                            last = v.tensor_add(dst[:, s4 * D:(s4 + 1) * D],
                                                dst[:, s4 * D:(s4 + 1) * D],
                                                bias_t[:, :])
                    return last
                sched.append(("v", f5))

            def emit_b(ttb):
                t0 = ttb * TTB

                def f_l(s, ttb=ttb, t0=t0):
                    s.dma_start(
                        xb_tm.ap().rearrange("p (s d) -> p s d", s=2),
                        x_d[t0:t0 + TTB, :].rearrange("(s p) d -> p s d",
                                                      p=128)
                    ).then_inc(S["sQL"], 16)
                    cadd("sQL", 16)
                    for s2 in range(2):
                        s.dma_start(wol[:, s2 * D:(s2 + 1) * D],
                                    wo_out_d[ttb * 2 + s2, :, :]
                                    ).then_inc(S["sQL"], 16)
                        s.dma_start(gml[:, s2 * D:(s2 + 1) * D],
                                    gmoe_d[ttb * 2 + s2, :, :]
                                    ).then_inc(S["sQL"], 16)
                        cadd("sQL", 32)
                    s.wait_ge(S["sQL"], cget("sQL"))
                    return None
                sched.append(("s", f_l))

                emit_ln_tm(sched, xb_tm, t1b, 0, None, None)

                def f_t1(v):
                    return v.tensor_add(t1b[:, :], t1b[:, :], wol[:, :])
                sched.append(("v", f_t1))

                emit_ln_tm(sched, t1b, xh_tm, 1, l1sb, l1bb)

                def f_cp(v):
                    return v.tensor_copy(xh_bf[:, :], xh_tm[:, :])
                sched.append(("v", f_cp))

                def f_tr(t):
                    last = None
                    for s4 in range(2):
                        for dc in range(DCN):
                            last = t.transpose(
                                trb_ps[:, dc * TTB + s4 * 128:
                                       dc * TTB + (s4 + 1) * 128],
                                xh_bf[:, s4 * D + dc * 128:
                                      s4 * D + (dc + 1) * 128],
                                ident[:, :])
                    return last
                sched.append(("t", f_tr))

                def f_tre(v):
                    return v.tensor_copy(xh_fm[:, :], trb_ps[:, :])
                sched.append(("v", f_tre))

                for fcp in range(16):
                    def f_w1(t, fcp=fcp):
                        last = None
                        for i in range(2):
                            fc = fcp * 2 + i
                            for kc in range(8):
                                last = t.matmul(
                                    p1[i][:, :],
                                    w1_sb[:, kc * FF + fc * 128:
                                          kc * FF + (fc + 1) * 128],
                                    xh_fm[:, kc * TTB:(kc + 1) * TTB],
                                    start=(kc == 0), stop=(kc == 7))
                        return last
                    sched.append(("t", f_w1))

                    def f_g1(a, fcp=fcp):
                        last = None
                        for i in range(2):
                            fc = fcp * 2 + i
                            last = act(a, hh_fm[:, fc * TTB:(fc + 1) * TTB],
                                       p1[i][:, :], AF.Gelu_apprx_tanh,
                                       bias=b1c[:, fc:fc + 1])
                        return last
                    sched.append(("a", f_g1))

                for s4 in range(2):
                    def f_w2(t, s4=s4):
                        last = None
                        for half in range(2):
                            for fc in range(32):
                                last = t.matmul(
                                    ps_big[:, s4 * D + half * 512:
                                           s4 * D + (half + 1) * 512],
                                    hh_fm[:, fc * TTB + s4 * 128:
                                          fc * TTB + (s4 + 1) * 128],
                                    w2_sb[:, fc * D + half * 512:
                                          fc * D + (half + 1) * 512],
                                    start=(fc == 0), stop=(fc == 31))
                        return last
                    sched.append(("t", f_w2))

                def f_res(v):
                    for s4 in range(2):
                        v.scalar_tensor_tensor(
                            wol[:, s4 * D:(s4 + 1) * D],
                            ps_big[:, s4 * D:(s4 + 1) * D], 1.0, b2b[:, :],
                            ALU.mult, ALU.add)
                    return v.tensor_add(wol[:, :], wol[:, :], xh_tm[:, :])
                sched.append(("v", f_res))

                emit_ln_tm(sched, wol, t1b, 1, l2sb, l2bb)

                def f_fin(v):
                    v.tensor_mul(t1b[:, :], t1b[:, :], gml[:, :])
                    return v.tensor_add(wol[:, :], xb_tm[:, :], t1b[:, :])
                sched.append(("v", f_fin))

                def f_out(g, t0=t0):
                    g.dma_start(
                        y_d[t0:t0 + TTB, :].rearrange("(s p) d -> p s d",
                                                      p=128),
                        wol.ap().rearrange("p (s d) -> p s d", s=2)
                    ).then_inc(S["sQO"], 16)
                    cadd("sQO", 16)
                    return None
                sched.append(("g", f_out))

            for ttb in range(NTTB):
                emit_b(ttb)

            with nc.Block() as blk:
                def body(eng_name):
                    def _(eng):
                        for i, (en, fn) in enumerate(sched):
                            if en != eng_name:
                                continue
                            eng.wait_ge(S["sB"], i)
                            last = fn(eng)
                            if last is not None:
                                last.then_inc(S["sB"], 1)
                            else:
                                eng.sem_inc(S["sB"], 1)
                    return _
                blk.sync(body("s"))
                blk.tensor(body("t"))
                blk.vector(body("v"))
                blk.scalar(body("a"))
                blk.gpsimd(body("g"))

        with nc.Block() as blk:
            @blk.sync
            def _(s):
                s.wait_ge(S["sQO"], cget("sQO"))
        if dbg:
            with nc.Block() as blk:
                @blk.gpsimd
                def _(g):
                    g.dma_start(dbg_stash.ap(), stash_d.ap()).then_inc(
                        S["sQO"], 16)
                    g.dma_start(dbg_z.ap(), z_d.ap()).then_inc(S["sQO"], 16)
                    g.dma_start(dbg_xnf.ap(), xnf_d.ap()).then_inc(
                        S["sQO"], 16)
                    g.dma_start(dbg_crow.ap(), crow_d.ap()).then_inc(
                        S["sQO"], 16)
                    g.dma_start(dbg_summ.ap(), summ_ag_d.ap()).then_inc(
                        S["sQO"], 16)
                    g.dma_start(dbg_wo.ap(), wo_out_d.ap()).then_inc(
                        S["sQO"], 16)
                    g.dma_start(dbg_gm.ap(), gmoe_d.ap()).then_inc(
                        S["sQO"], 16)
                    g.dma_start(dbg_mfm.ap(), mfm_d.ap()).then_inc(
                        S["sQO"], 16)
                    g.dma_start(dbg_hact.ap(), hact_d.ap()).then_inc(
                        S["sQO"], 16)
                    g.dma_start(dbg_comb[:, :], combine[:, :]).then_inc(
                        S["sQO"], 16)
                    g.dma_start(dbg_acc[:, :], acc1_ref[:, :]).then_inc(
                        S["sQO"], 16)
                    cadd("sQO", 176)
                    g.wait_ge(S["sQO"], cget("sQO"))

        for k, vv in C.items():
            assert vv < 60000, (k, vv)
    return nc


def kernel(**inputs):
    x = np.asarray(inputs["L0_attn_x"])
    bf = ml_dtypes.bfloat16

    def f32c(a):
        return np.ascontiguousarray(np.asarray(a), dtype=np.float32)

    wq = f32c(inputs["L0_attn_Wq"]); wk = f32c(inputs["L0_attn_Wk"])
    wv = f32c(inputs["L0_attn_Wv"]); wa = f32c(inputs["L0_attn_Wa"])
    wg = f32c(inputs["L0_attn_Wg"]); wo = f32c(inputs["L0_attn_Wo"])
    w1 = f32c(inputs["L0_attn_W1"]); w2 = f32c(inputs["L0_attn_W2"])
    b1 = f32c(inputs["L0_attn_b1"]); b2 = f32c(inputs["L0_attn_b2"])
    l1s = f32c(inputs["L0_attn_ln1_scale"]); l1b = f32c(inputs["L0_attn_ln1_bias"])
    l2s = f32c(inputs["L0_attn_ln2_scale"]); l2b = f32c(inputs["L0_attn_ln2_bias"])
    gns = f32c(inputs["L0_attn_gn_scale"]); gnb = f32c(inputs["L0_attn_gn_bias"])
    gate_w = f32c(inputs["L0_ffn_gate_w"])
    gu = f32c(inputs["L0_ffn_gate_up_w"])
    dw = f32c(inputs["L0_ffn_down_w"])

    guT = np.ascontiguousarray(gu.transpose(0, 2, 1))
    dwq = np.ascontiguousarray(
        dw.transpose(0, 2, 1).reshape(8, 32, 128, 1024))
    gwT = np.ascontiguousarray(gate_w.T)
    b1c = np.ascontiguousarray(b1.reshape(32, 128).T)
    b2b = np.ascontiguousarray(np.broadcast_to(b2, (128, D)))
    l1sb = np.ascontiguousarray(np.broadcast_to(l1s, (128, D)))
    l1bb = np.ascontiguousarray(np.broadcast_to(l1b, (128, D)))
    l2sb = np.ascontiguousarray(np.broadcast_to(l2s, (128, D)))
    l2bb = np.ascontiguousarray(np.broadcast_to(l2b, (128, D)))
    gnsr = gns.reshape(1, D)
    gnbr = gnb.reshape(1, D)
    ident = np.eye(128, dtype=bf)
    gmap = np.zeros((128, 4), np.float32)
    for p in range(128):
        gmap[p, p // 32] = 1.0

    in_maps = []
    for c in range(8):
        b = c // 4
        t0 = (c % 4) * T
        onehot = np.zeros((128, 8), np.float32)
        onehot[:, c] = 1.0
        bsel = np.zeros((128, 2), np.float32)
        bsel[:, b] = 1.0
        m = {
            "x": np.ascontiguousarray(x[b, t0:t0 + T, :], dtype=np.float32),
            "wq": wq, "wk": wk, "wv": wv, "wa": wa, "wg": wg, "wo": wo,
            "w1": w1, "w2": w2, "b1c": b1c, "b2b": b2b,
            "l1sb": l1sb, "l1bb": l1bb, "l2sb": l2sb, "l2bb": l2bb,
            "gnsr": gnsr, "gnbr": gnbr, "gwT": gwT, "guT": guT, "dwq": dwq,
            "ident": ident, "gmap": gmap, "onehot": onehot, "bsel": bsel,
        }
        in_maps.append(m)

    dbg = bool(inputs.get("_dbg", False))
    nc = build_nc(dbg=dbg)
    res = run_bass_kernel_spmd(nc, in_maps, core_ids=list(range(8)))
    out = np.empty((2, 16384, D), np.float32)
    for c in range(8):
        b = c // 4
        t0 = (c % 4) * T
        out[b, t0:t0 + T, :] = res.results[c]["y"]
    if dbg:
        return out, res
    return out


if __name__ == "__main__":
    print("building...")
    nc = build_nc()
    print("built ok; instructions:", nc.n_instructions())


# version 07
# speedup vs baseline: 1.2941x; 1.2941x over previous
import heapq
import numpy as np
import ml_dtypes
from contextlib import ExitStack

import concourse.bass as bass
import concourse.mybir as mybir
from concourse.bass_utils import run_bass_kernel_spmd

F32 = mybir.dt.float32
BF16 = mybir.dt.bfloat16
AF = mybir.ActivationFunctionType
ALU = mybir.AluOpType
AX = mybir.AxisListType.X

PI = float(np.pi)
MAGIC = 12582912.0
INV2PI = float(1.0 / (2.0 * np.pi))
TWOPI = float(2.0 * np.pi)

D = 1024
FF = 4096
E = 8
T = 4096
TTA = 512
NTT = T // TTA
TTB = 256
NTTB = T // TTB
DCN = 8
SUB = 64
NSUB = TTA // SUB
QT = 1024
NQT = 4
CNT_INV = float(1.0 / (16384.0 * 32.0))


def _act_setids():
    from concourse.hw_specs import get_activation_tables
    tabs = list(get_activation_tables("gen3").values())
    def find(*funcs):
        for i, fs in enumerate(tabs):
            if all(f in fs for f in funcs):
                return i
        raise ValueError(funcs)
    return {
        AF.Sqrt: find(AF.Sqrt, AF.Square, AF.Sign),
        AF.Sigmoid: find(AF.Sigmoid, AF.Arctan, AF.Sign),
        AF.Arctan: find(AF.Sigmoid, AF.Arctan, AF.Sign),
        AF.Ln: find(AF.Ln, AF.Exp),
        AF.Exp: find(AF.Ln, AF.Exp),
        AF.Sin: find(AF.Sin, AF.Silu),
        AF.Silu: find(AF.Sin, AF.Silu),
        AF.Gelu: find(AF.Gelu),
        AF.Gelu_apprx_tanh: find(AF.Gelu_apprx_tanh),
    }


_ACT_WILD = {AF.Square, AF.Sign, AF.Abs, AF.Copy, AF.Identity, AF.Relu,
             AF.Is_finite}


def build_nc(dbg=False):
    nc = bass.Bass("TRN2", target_bir_lowering=False)
    _setid = _act_setids()
    _att = {"cur": None}

    def _tload(a, sid):
        a.add_instruction(mybir.InstLoadActFuncSet(
            name=nc.get_next_instruction_name(),
            act_func_set_id=sid, ins=[], outs=[]))
        _att["cur"] = sid

    def act(a, out, in_, func, **kw):
        if func in _ACT_WILD:
            if _att["cur"] is None:
                _tload(a, _setid[AF.Sqrt])
        elif _setid[func] != _att["cur"]:
            _tload(a, _setid[func])
        return a.activation(out, in_, func, **kw)

    x_d = nc.dram_tensor("x", [T, D], F32, kind="ExternalInput")
    wq_d = nc.dram_tensor("wq", [D, D], F32, kind="ExternalInput")
    wk_d = nc.dram_tensor("wk", [D, D], F32, kind="ExternalInput")
    wv_d = nc.dram_tensor("wv", [D, D], F32, kind="ExternalInput")
    wa_d = nc.dram_tensor("wa", [D, 2 * D], F32, kind="ExternalInput")
    wg_d = nc.dram_tensor("wg", [D, D], F32, kind="ExternalInput")
    wo_d = nc.dram_tensor("wo", [D, D], F32, kind="ExternalInput")
    w1_d = nc.dram_tensor("w1", [D, FF], F32, kind="ExternalInput")
    w2_d = nc.dram_tensor("w2", [FF, D], F32, kind="ExternalInput")
    b1c_d = nc.dram_tensor("b1c", [128, 32], F32, kind="ExternalInput")
    b2b_d = nc.dram_tensor("b2b", [128, D], F32, kind="ExternalInput")
    l1sb_d = nc.dram_tensor("l1sb", [128, D], F32, kind="ExternalInput")
    l1bb_d = nc.dram_tensor("l1bb", [128, D], F32, kind="ExternalInput")
    l2sb_d = nc.dram_tensor("l2sb", [128, D], F32, kind="ExternalInput")
    l2bb_d = nc.dram_tensor("l2bb", [128, D], F32, kind="ExternalInput")
    gnsr_d = nc.dram_tensor("gnsr", [1, D], F32, kind="ExternalInput")
    gnbr_d = nc.dram_tensor("gnbr", [1, D], F32, kind="ExternalInput")
    gwT_d = nc.dram_tensor("gwT", [D, E], F32, kind="ExternalInput")
    guT_d = nc.dram_tensor("guT", [E, D, 2 * FF], F32, kind="ExternalInput")
    dwq_d = nc.dram_tensor("dwq", [E, 32, 128, D], F32, kind="ExternalInput")
    ident_d = nc.dram_tensor("ident", [128, 128], BF16, kind="ExternalInput")
    gmap_d = nc.dram_tensor("gmap", [128, 4], F32, kind="ExternalInput")
    onehot_d = nc.dram_tensor("onehot", [128, 8], F32, kind="ExternalInput")
    bsel_d = nc.dram_tensor("bsel", [128, 2], F32, kind="ExternalInput")

    y_d = nc.dram_tensor("y", [T, D], F32, kind="ExternalOutput")

    stash_d = nc.dram_tensor("stash", [9, DCN, 128, T], F32)
    z_d = nc.dram_tensor("zst", [DCN, 128, T], BF16)
    xnf_d = nc.dram_tensor("xnfd", [DCN, NQT, 128, QT], BF16)
    hact_d = nc.dram_tensor("hact", [2, 4, 8, 8, 128, 128], BF16)
    mfm_d = nc.dram_tensor("mfmd", [NTT, DCN, 128, TTA], BF16)
    gmoe_d = nc.dram_tensor("gmoe", [32, 128, D], BF16)
    wo_out_d = nc.dram_tensor("wo_out", [32, 128, D], F32)
    summ_in_d = nc.dram_tensor("summ_in", [128, 32], F32)
    summ_ag_d = nc.dram_tensor("summ_ag", [1024, 32], F32)
    stats_in_d = nc.dram_tensor("stats_in", [2, 128], F32)
    stats_out_d = nc.dram_tensor("stats_out", [2, 128], F32)
    crow_d = nc.dram_tensor("crow", [2, D], F32)

    if dbg:
        dbg_stash = nc.dram_tensor("dbg_stash", [9, DCN, 128, T], F32,
                                   kind="ExternalOutput")
        dbg_z = nc.dram_tensor("dbg_z", [DCN, 128, T], BF16,
                               kind="ExternalOutput")
        dbg_xnf = nc.dram_tensor("dbg_xnf", [DCN, NQT, 128, QT], BF16,
                                 kind="ExternalOutput")
        dbg_crow = nc.dram_tensor("dbg_crow", [2, D], F32,
                                  kind="ExternalOutput")
        dbg_summ = nc.dram_tensor("dbg_summ", [1024, 32], F32,
                                  kind="ExternalOutput")
        dbg_wo = nc.dram_tensor("dbg_wo", [32, 128, D], F32,
                                kind="ExternalOutput")
        dbg_gm = nc.dram_tensor("dbg_gm", [32, 128, D], BF16,
                                kind="ExternalOutput")
        dbg_mfm = nc.dram_tensor("dbg_mfm", [NTT, DCN, 128, TTA], BF16,
                                 kind="ExternalOutput")
        dbg_hact = nc.dram_tensor("dbg_hact", [2, 4, 8, 8, 128, 128], BF16,
                                  kind="ExternalOutput")
        dbg_xn = nc.dram_tensor("dbg_xn", [NTT, 128, 4 * D], BF16,
                                kind="ExternalOutput")
        dbg_st4 = nc.dram_tensor("dbg_st4", [NTT, 128, 24], F32,
                                 kind="ExternalOutput")
        dbg_comb = nc.dram_tensor("dbg_comb", [128, 256], F32,
                                  kind="ExternalOutput")
        dbg_acc = nc.dram_tensor("dbg_acc", [128, 8 * D], F32,
                                 kind="ExternalOutput")

    es = ExitStack()
    with es:
        def SEM(name):
            return es.enter_context(nc.semaphore(name))
        C = {}
        def cadd(k, n):
            C[k] = C.get(k, 0) + n
            return C[k]
        def cget(k):
            return C.get(k, 0)

        SN = ("sXL sXF sLN sXN sTRd sTR sTRe sPE0 sKVe sMG sSTa "
              "sPQ sQGe sRT sSTb sSTx "
              "sAC sSTsc sCC "
              "sGW sPE1 sACg sACu sGM sHS sDH sDWQ sPE2 sDV2 sSIL sGMS "
              "sWL sWPE sWEV sWST sZL "
              "sB sQL sQS sQO sP0").split()
        S = {n: SEM(n) for n in SN}

        def SB(scope, name, shape, dt):
            return scope.enter_context(nc.sbuf_tensor(name, shape, dt))

        mask64 = SB(es, "mask64", [128, TTA], F32)
        ident = SB(es, "identsb", [128, 128], BF16)
        gmap = SB(es, "gmapsb", [128, 4], F32)
        onehot = SB(es, "onehotsb", [128, 8], F32)
        bsel = SB(es, "bselsb", [128, 2], F32)
        combine = SB(es, "combine", [128, 32 * E], F32)
        summ = SB(es, "summ", [128, 32], F32)
        hg = SB(es, "hg", [128, 16], F32)
        stats_acc = SB(es, "stats_acc", [128, 32], F32)
        c1pp = SB(es, "c1pp", [128, 8], F32)
        c2pp = SB(es, "c2pp", [128, 8], F32)
        epsc = SB(es, "epsc", [128, 4], F32)

        with nc.Block() as blk:
            @blk.gpsimd
            def _(g):
                g.dma_start(ident[:, :], ident_d[:, :]).then_inc(S["sQL"], 16)
                g.dma_start(gmap[:, :], gmap_d[:, :]).then_inc(S["sQL"], 16)
                g.dma_start(onehot[:, :], onehot_d[:, :]).then_inc(S["sQL"], 16)
                g.dma_start(bsel[:, :], bsel_d[:, :]).then_inc(S["sQL"], 16)
                cadd("sQL", 64)
                g.wait_ge(S["sQL"], cget("sQL"))
                g.sem_inc(S["sP0"], 1)

            @blk.vector
            def _(v):
                v.memset(mask64[:, :], 1.0)
                v.memset(mask64.ap()[:, 0:TTA:SUB], 0.0)
                v.memset(summ[:, :], 0.0)
                v.memset(summ[:, 0:8], 1.0)
                v.memset(stats_acc[:, :], 0.0)
                v.memset(epsc[:, 0:1], 1e-5)
                v.memset(epsc[:, 1:2], 1e-6)
                v.memset(epsc[:, 2:3], 1e-30)
                v.sem_inc(S["sP0"], 1)

        with ExitStack() as e0a, ExitStack() as e0ap:
            wk_sb = SB(e0a, "wk_sb", [128, 8 * D], BF16)
            wv_sb = SB(e0a, "wv_sb", [128, 8 * D], BF16)
            wa_sb = SB(e0a, "wa_sb", [128, 8 * 2 * D], BF16)
            x_tm = SB(e0a, "x_tm", [128, 4 * D], F32)
            xn_tmb = [SB(e0a, f"xn_tmb{i}", [128, 4 * D], BF16)
                      for i in range(2)]
            st4 = SB(e0a, "st4", [128, 24], F32)
            lnscr = SB(e0a, "lnscr", [128, D], F32)
            xnf_t = [SB(e0a, f"xnf_t{i}", [128, DCN * TTA], BF16)
                     for i in range(2)]
            rk = SB(e0a, "rk", [128, TTA], F32)
            rkv = [SB(e0a, f"rkv{i}", [128, TTA], F32) for i in range(2)]
            rlg = [SB(e0a, f"rlg{i}", [128, TTA], F32) for i in range(2)]
            rth = [SB(e0a, f"rth{i}", [128, TTA], F32) for i in range(2)]
            mtp = SB(e0a, "mtp", [128, 6 * TTA], F32)
            tr_ps = [e0ap.enter_context(
                nc.psum_tensor(f"tr_ps{i}", [128, TTA], BF16))
                for i in range(2)]
            kvP = [e0ap.enter_context(
                nc.psum_tensor(f"kvP{i}", [128, TTA], F32))
                for i in range(2)]
            arP = [e0ap.enter_context(
                nc.psum_tensor(f"arP{i}", [128, 2 * TTA], F32))
                for i in range(2)]
            m0 = mtp.ap()[:, 0:TTA]
            m1 = mtp.ap()[:, TTA:2 * TTA]
            m2 = mtp.ap()[:, 2 * TTA:3 * TTA]
            m3 = mtp.ap()[:, 3 * TTA:4 * TTA]
            m4 = mtp.ap()[:, 4 * TTA:5 * TTA]
            m5 = mtp.ap()[:, 5 * TTA:6 * TTA]

            with nc.Block() as blk:
                @blk.gpsimd
                def _(g):
                    g.dma_start(wk_sb.ap().rearrange("p (c m) -> p c m", c=8),
                                wk_d.ap().rearrange("(c p) m -> p c m", p=128)
                                ).then_inc(S["sQL"], 16)
                    g.dma_start(wv_sb.ap().rearrange("p (c m) -> p c m", c=8),
                                wv_d.ap().rearrange("(c p) m -> p c m", p=128)
                                ).then_inc(S["sQL"], 16)
                    g.dma_start(wa_sb.ap().rearrange("p (c m) -> p c m", c=8),
                                wa_d.ap().rearrange("(c p) m -> p c m", p=128)
                                ).then_inc(S["sQL"], 16)
                    cadd("sQL", 48)
                    g.wait_ge(S["sQL"], cget("sQL"))

                @blk.sync
                def _(s):
                    for tt in range(NTT):
                        if tt > 0:
                            s.wait_ge(S["sXF"], tt)
                        s.dma_start(
                            x_tm.ap().rearrange("p (s d) -> p s d", s=4),
                            x_d[tt * TTA:(tt + 1) * TTA, :]
                            .rearrange("(s p) d -> p s d", p=128)
                        ).then_inc(S["sXL"], 16)
                        cadd("sXL", 16)
                        if dbg:
                            s.wait_ge(S["sXN"], tt + 1)
                            s.dma_start(dbg_xn[tt], xn_tmb[tt % 2][:, :]
                                        ).then_inc(S["sQS"], 16)
                            s.dma_start(dbg_st4[tt], st4[:, :]
                                        ).then_inc(S["sQS"], 16)
                            cadd("sQS", 32)
                        for dc in range(DCN):
                            i = tt * 8 + dc
                            s.wait_ge(S["sTRe"], i + 1)
                            s.dma_start(
                                xnf_d[dc, tt // 2, :,
                                      (tt % 2) * TTA:(tt % 2 + 1) * TTA],
                                xnf_t[tt % 2][:, dc * TTA:(dc + 1) * TTA]
                            ).then_inc(S["sSTx"], 16)
                            cadd("sSTx", 16)
                            s.wait_ge(S["sKVe"], i + 1)
                            s.dma_start(stash_d[6, dc, :,
                                                tt * TTA:(tt + 1) * TTA],
                                        rkv[i % 2][:, :]).then_inc(S["sSTa"], 16)
                            s.wait_ge(S["sMG"], 4 * i + 4)
                            s.dma_start(stash_d[7, dc, :,
                                                tt * TTA:(tt + 1) * TTA],
                                        rlg[i % 2][:, :]).then_inc(S["sSTa"], 16)
                            s.dma_start(stash_d[8, dc, :,
                                                tt * TTA:(tt + 1) * TTA],
                                        rth[i % 2][:, :]).then_inc(S["sSTa"], 16)
                            cadd("sSTa", 48)

                @blk.vector
                def _(v):
                    for tt in range(NTT):
                        if tt == 0:
                            v.wait_ge(S["sP0"], 1)
                        v.wait_ge(S["sLN"], 4 * tt)
                        v.wait_ge(S["sXL"], 16 * (tt + 1))
                        v.reduce_sum(
                            st4[:, 0:4],
                            x_tm.ap().rearrange("p (s d) -> p s d", s=4),
                            axis=AX).then_inc(S["sLN"], 1)
                        v.wait_ge(S["sLN"], 4 * tt + 2)
                        v.tensor_scalar(st4[:, 4:8], st4[:, 0:4], 1.0 / D,
                                        None, ALU.mult)
                        v.tensor_scalar(st4[:, 12:16], st4[:, 8:12], 1.0 / D,
                                        None, ALU.mult)
                        v.tensor_mul(st4[:, 16:20], st4[:, 4:8], st4[:, 4:8])
                        v.memset(st4[:, 22:23], 0.0)
                        v.tensor_sub(st4[:, 16:20], st4[:, 12:16],
                                     st4[:, 16:20]).then_inc(S["sLN"], 1)
                        v.wait_ge(S["sLN"], 4 * tt + 4)
                        v.reciprocal(st4[:, 12:16], st4[:, 20:24])
                        v.memset(st4[:, 22:23], 0.0)
                        if tt >= 2:
                            v.wait_ge(S["sTRd"], tt - 1)
                        last = None
                        for s4 in range(4):
                            last = v.tensor_scalar(
                                xn_tmb[tt % 2][:, s4 * D:(s4 + 1) * D],
                                x_tm[:, s4 * D:(s4 + 1) * D],
                                st4[:, 4 + s4:5 + s4], st4[:, 12 + s4:13 + s4],
                                ALU.subtract, ALU.mult)
                        last.then_inc(S["sXN"], 1)
                        v.sem_inc(S["sXF"], 1)
                        for dc in range(DCN):
                            i = tt * 8 + dc
                            v.wait_ge(S["sTR"], i + 1)
                            if tt >= 2:
                                v.wait_ge(S["sSTx"],
                                          16 * (8 * (tt - 2) + dc + 1))
                            ev = v.tensor_copy(
                                xnf_t[tt % 2][:, dc * TTA:(dc + 1) * TTA],
                                tr_ps[i % 2][:, :])
                            ev.then_inc(S["sTRe"], 1)
                            if dc == DCN - 1:
                                v.sem_inc(S["sTRd"], 1)
                        for dc in range(DCN):
                            i = tt * 8 + dc
                            v.wait_ge(S["sPE0"], i * 2 + 1)
                            if i >= 2:
                                v.wait_ge(S["sSTa"], 48 * (i - 1))
                            v.tensor_copy(rk[:, :], kvP[i % 2][:, :])
                            v.tensor_mul(rkv[i % 2][:, :], rk[:, :],
                                         kvP[(i + 1) % 2][:, :]
                                         ).then_inc(S["sKVe"], 1)
                            v.wait_ge(S["sMG"], 4 * i + 1)
                            are = arP[i % 2][:, 0:TTA]
                            aim = arP[i % 2][:, TTA:2 * TTA]
                            v.tensor_add(m2, m2, m3)
                            v.tensor_scalar(m0, m0, -1e-37, 1e-37,
                                            ALU.mult, ALU.add)
                            v.tensor_tensor(m4, are, m0, ALU.add)
                            v.reciprocal(m0, m4)
                            v.tensor_tensor(m3, aim, m0, ALU.mult
                                            ).then_inc(S["sMG"], 1)
                            v.wait_ge(S["sMG"], 4 * i + 3)
                            if i >= 2:
                                v.wait_ge(S["sSTa"], 48 * (i - 1))
                            v.tensor_scalar(m5, m5, -PI / 2, PI / 2,
                                            ALU.mult, ALU.add)
                            v.tensor_mul(m5, m5, m1)
                            v.tensor_add(rth[i % 2][:, :], m0, m5
                                         ).then_inc(S["sMG"], 1)

                @blk.scalar
                def _(a):
                    for tt in range(NTT):
                        if tt == 0:
                            a.wait_ge(S["sP0"], 2)
                        a.wait_ge(S["sLN"], 4 * tt + 1)
                        if tt >= 1:
                            a.wait_ge(S["sMG"], 4 * (8 * tt))
                        last = None
                        for s4 in range(4):
                            last = act(a, lnscr[:, :],
                                       x_tm[:, s4 * D:(s4 + 1) * D],
                                       AF.Square,
                                       accum_out=st4[:, 8 + s4:9 + s4])
                        last.then_inc(S["sLN"], 1)
                        a.wait_ge(S["sLN"], 4 * tt + 3)
                        act(a, st4[:, 20:24], st4[:, 16:20], AF.Sqrt,
                            bias=epsc[:, 0:1]).then_inc(S["sLN"], 1)
                        for dc in range(DCN):
                            i = tt * 8 + dc
                            are = arP[i % 2][:, 0:TTA]
                            aim = arP[i % 2][:, TTA:2 * TTA]
                            a.wait_ge(S["sPE0"], i * 2 + 2)
                            if i >= 1:
                                a.wait_ge(S["sMG"], 4 * i)
                            act(a, m0, are, AF.Sign)
                            act(a, m1, aim, AF.Sign)
                            act(a, m0, m0, AF.Square)
                            act(a, m2, are, AF.Square)
                            act(a, m3, aim, AF.Square).then_inc(S["sMG"], 1)
                            a.wait_ge(S["sMG"], 4 * i + 2)
                            if i >= 2:
                                a.wait_ge(S["sSTa"], 48 * (i - 1))
                            act(a, m2, m2, AF.Sqrt, bias=epsc[:, 2:3])
                            act(a, m5, m4, AF.Sign)
                            act(a, m0, m3, AF.Arctan)
                            act(a, m2, m2, AF.Sigmoid)
                            act(a, rlg[i % 2][:, :], m2, AF.Ln
                                ).then_inc(S["sMG"], 1)

                @blk.tensor
                def _(t):
                    t.wait_ge(S["sQL"], cget("sQL"))
                    for tt in range(NTT):
                        t.wait_ge(S["sXN"], tt + 1)
                        for dc in range(DCN):
                            i = tt * 8 + dc
                            if i >= 2:
                                t.wait_ge(S["sTRe"], i - 1)
                            last = None
                            for s4 in range(4):
                                last = t.transpose(
                                    tr_ps[i % 2][:, s4 * 128:(s4 + 1) * 128],
                                    xn_tmb[tt % 2][:, s4 * D + dc * 128:
                                                   s4 * D + (dc + 1) * 128],
                                    ident[:, :])
                            last.then_inc(S["sTR"], 1)
                        t.wait_ge(S["sTRe"], 8 * (tt + 1))
                        xs = xnf_t[tt % 2]
                        for dc in range(DCN):
                            i = tt * 8 + dc
                            if i >= 1:
                                t.wait_ge(S["sKVe"], i)
                            last = None
                            for (ps, wsb) in ((kvP[i % 2][:, :], wk_sb),
                                              (kvP[(i + 1) % 2][:, :], wv_sb)):
                                for kc in range(8):
                                    last = t.matmul(
                                        ps,
                                        wsb[:, kc * D + dc * 128:
                                            kc * D + dc * 128 + 128],
                                        xs[:, kc * TTA:(kc + 1) * TTA],
                                        start=(kc == 0), stop=(kc == 7))
                            last.then_inc(S["sPE0"], 1)
                            if i >= 2:
                                t.wait_ge(S["sMG"], 4 * (i - 1))
                            last = None
                            for (cps, col0) in ((arP[i % 2][:, 0:TTA],
                                                 dc * 128),
                                                (arP[i % 2][:, TTA:2 * TTA],
                                                 D + dc * 128)):
                                for kc in range(8):
                                    last = t.matmul(
                                        cps,
                                        wa_sb[:, kc * 2 * D + col0:
                                              kc * 2 * D + col0 + 128],
                                        xs[:, kc * TTA:(kc + 1) * TTA],
                                        start=(kc == 0), stop=(kc == 7))
                            last.then_inc(S["sPE0"], 1)

        with ExitStack() as e0b, ExitStack() as e0bp:
            wq_sb = SB(e0b, "wq_sb", [128, 8 * D], BF16)
            wg_sb = SB(e0b, "wg_sb", [128, 8 * D], BF16)
            gwt_sb = SB(e0b, "gwt_sb", [128, 8 * E], BF16)
            xnb_t = [SB(e0b, f"xnb_t{i}", [128, DCN * TTA], BF16)
                     for i in range(2)]
            rq = [SB(e0b, f"rq{i}", [128, TTA], F32) for i in range(2)]
            rsg = [SB(e0b, f"rsg{i}", [128, TTA], F32) for i in range(2)]
            rt = SB(e0b, "rt", [128, 64], F32)
            pq = [e0bp.enter_context(
                nc.psum_tensor(f"pqQ{i}", [128, TTA], F32)) for i in range(2)]
            pg = [e0bp.enter_context(
                nc.psum_tensor(f"pgQ{i}", [128, TTA], F32)) for i in range(2)]
            pr = e0bp.enter_context(nc.psum_tensor("prQ", [128, 16], F32))

            with nc.Block() as blk:
                @blk.gpsimd
                def _(g):
                    g.wait_ge(S["sPE0"], 128)
                    g.wait_ge(S["sMG"], 256)
                    g.wait_ge(S["sSTa"], 48 * 64)
                    g.wait_ge(S["sSTx"], 16 * 64)
                    g.dma_start(wq_sb.ap().rearrange("p (c m) -> p c m", c=8),
                                wq_d.ap().rearrange("(c p) m -> p c m", p=128)
                                ).then_inc(S["sQL"], 16)
                    g.dma_start(wg_sb.ap().rearrange("p (c m) -> p c m", c=8),
                                wg_d.ap().rearrange("(c p) m -> p c m", p=128)
                                ).then_inc(S["sQL"], 16)
                    g.dma_start(gwt_sb.ap().rearrange("p (c e) -> p c e", c=8),
                                gwT_d.ap().rearrange("(c p) e -> p c e", p=128)
                                ).then_inc(S["sQL"], 16)
                    cadd("sQL", 48)
                    g.wait_ge(S["sQL"], cget("sQL"))

                @blk.sync
                def _(s):
                    for tt in range(NTT):
                        s.wait_ge(S["sSTx"], 16 * 8 * (tt + 1))
                        if tt >= 2:
                            s.wait_ge(S["sPQ"], 8 * (tt - 1))
                            s.wait_ge(S["sRT"], 8 * (tt - 1))
                        s.dma_start(
                            xnb_t[tt % 2].ap().rearrange(
                                "p (c t) -> p c t", c=8),
                            xnf_d[:, tt // 2, :,
                                  (tt % 2) * TTA:(tt % 2 + 1) * TTA]
                            .rearrange("c p t -> p c t")
                        ).then_inc(S["sZL"], 16)
                        cadd("sZL", 16)
                        for dc in range(DCN):
                            i = tt * 8 + dc
                            s.wait_ge(S["sQGe"], i * 2 + 2)
                            s.dma_start(stash_d[4, dc, :,
                                                tt * TTA:(tt + 1) * TTA],
                                        rq[i % 2][:, :]).then_inc(S["sSTb"], 16)
                            s.dma_start(stash_d[5, dc, :,
                                                tt * TTA:(tt + 1) * TTA],
                                        rsg[i % 2][:, :]).then_inc(S["sSTb"], 16)
                            cadd("sSTb", 32)

                @blk.tensor
                def _(t):
                    t.wait_ge(S["sQL"], cget("sQL"))
                    for tt in range(NTT):
                        t.wait_ge(S["sZL"], 16 * (tt + 1))
                        xs = xnb_t[tt % 2]
                        for dc in range(DCN):
                            i = tt * 8 + dc
                            if i >= 2:
                                t.wait_ge(S["sQGe"], (i - 1) * 2)
                            last = None
                            for (ps, wsb) in ((pq[i % 2], wq_sb),
                                              (pg[i % 2], wg_sb)):
                                for kc in range(8):
                                    last = t.matmul(
                                        ps[:, :],
                                        wsb[:, kc * D + dc * 128:
                                            kc * D + dc * 128 + 128],
                                        xs[:, kc * TTA:(kc + 1) * TTA],
                                        start=(kc == 0), stop=(kc == 7))
                            last.then_inc(S["sPQ"], 1)
                        for sp in range(2):
                            t.wait_ge(S["sRT"], tt * 8 + sp * 4)
                            last = None
                            for k in range(2):
                                s4 = sp * 2 + k
                                for kc in range(8):
                                    last = t.matmul(
                                        pr[:, k * 8:(k + 1) * 8],
                                        xs[:, kc * TTA + s4 * 128:
                                           kc * TTA + (s4 + 1) * 128],
                                        gwt_sb[:, kc * E:(kc + 1) * E],
                                        start=(kc == 0), stop=(kc == 7))
                            last.then_inc(S["sRT"], 1)

                @blk.vector
                def _(v):
                    for tt in range(NTT):
                        for dc in range(DCN):
                            i = tt * 8 + dc
                            v.wait_ge(S["sPQ"], i + 1)
                            if i >= 2:
                                v.wait_ge(S["sSTb"], 32 * (i - 1))
                            v.tensor_copy(rq[i % 2][:, :], pq[i % 2][:, :]
                                          ).then_inc(S["sQGe"], 1)
                        for sp in range(2):
                            v.wait_ge(S["sRT"], tt * 8 + sp * 4 + 1)
                            v.reduce_max(rt[:, 0:1], pr[:, 0:8], axis=AX,
                                         negate=True)
                            v.reduce_max(rt[:, 32:33], pr[:, 8:16], axis=AX,
                                         negate=True).then_inc(S["sRT"], 1)
                            v.wait_ge(S["sRT"], tt * 8 + sp * 4 + 3)
                            def ops(B, g128):
                                yield lambda: v.reduce_sum(
                                    rt[:, B + 1:B + 2], rt[:, B + 8:B + 16],
                                    axis=AX)
                                yield lambda: v.reciprocal(
                                    rt[:, B + 2:B + 3], rt[:, B + 1:B + 2])
                                yield lambda: v.tensor_scalar(
                                    rt[:, B + 8:B + 16], rt[:, B + 8:B + 16],
                                    rt[:, B + 2:B + 3], None, ALU.mult)
                                yield lambda: v.reduce_max(
                                    rt[:, B + 3:B + 4], rt[:, B + 8:B + 16],
                                    axis=AX)
                                yield lambda: v.tensor_scalar(
                                    rt[:, B + 16:B + 24], rt[:, B + 8:B + 16],
                                    rt[:, B + 3:B + 4], None, ALU.is_ge)
                                yield lambda: v.tensor_scalar(
                                    rt[:, B + 24:B + 32], rt[:, B + 16:B + 24],
                                    -1.0, 1.0, ALU.mult, ALU.add)
                                yield lambda: v.tensor_mul(
                                    rt[:, B + 24:B + 32], rt[:, B + 8:B + 16],
                                    rt[:, B + 24:B + 32])
                                yield lambda: v.reduce_max(
                                    rt[:, B + 4:B + 5], rt[:, B + 24:B + 32],
                                    axis=AX)
                                yield lambda: v.tensor_scalar(
                                    rt[:, B + 24:B + 32], rt[:, B + 24:B + 32],
                                    rt[:, B + 4:B + 5], None, ALU.is_ge)
                                yield lambda: v.tensor_add(
                                    rt[:, B + 5:B + 6], rt[:, B + 3:B + 4],
                                    rt[:, B + 4:B + 5])
                                yield lambda: v.reciprocal(
                                    rt[:, B + 6:B + 7], rt[:, B + 5:B + 6])
                                yield lambda: v.tensor_mul(
                                    rt[:, B + 3:B + 4], rt[:, B + 3:B + 4],
                                    rt[:, B + 6:B + 7])
                                yield lambda: v.tensor_mul(
                                    rt[:, B + 4:B + 5], rt[:, B + 4:B + 5],
                                    rt[:, B + 6:B + 7])
                                yield lambda: v.tensor_scalar(
                                    rt[:, B + 16:B + 24], rt[:, B + 16:B + 24],
                                    rt[:, B + 3:B + 4], None, ALU.mult)
                                yield lambda: v.tensor_scalar(
                                    rt[:, B + 24:B + 32], rt[:, B + 24:B + 32],
                                    rt[:, B + 4:B + 5], None, ALU.mult)
                                yield lambda: v.tensor_add(
                                    combine[:, g128 * 8:(g128 + 1) * 8],
                                    rt[:, B + 16:B + 24], rt[:, B + 24:B + 32])
                            g0 = tt * 4 + sp * 2
                            it0 = ops(0, g0)
                            it1 = ops(32, g0 + 1)
                            last = None
                            for o0, o1 in zip(it0, it1):
                                last = o0()
                                last = o1()
                            last.then_inc(S["sRT"], 1)

                @blk.scalar
                def _(a):
                    for tt in range(NTT):
                        for dc in range(DCN):
                            i = tt * 8 + dc
                            a.wait_ge(S["sPQ"], i + 1)
                            if i >= 2:
                                a.wait_ge(S["sSTb"], 32 * (i - 1))
                            act(a, rsg[i % 2][:, :], pg[i % 2][:, :],
                                AF.Silu).then_inc(S["sQGe"], 1)
                        for sp in range(2):
                            a.wait_ge(S["sRT"], tt * 8 + sp * 4 + 2)
                            act(a, rt[:, 8:16], pr[:, 0:8], AF.Exp,
                                bias=rt[:, 0:1])
                            act(a, rt[:, 40:48], pr[:, 8:16], AF.Exp,
                                bias=rt[:, 32:33]).then_inc(S["sRT"], 1)

        streams = {k: [] for k in "tvasg"}
        seqn = [0]
        lastest = {k: 0.0 for k in "tvasg"}

        def ADD(engs, est, fn):
            streams[engs].append((est, seqn[0], fn))
            seqn[0] += 1

        with ExitStack() as big, ExitStack() as bigp:
            uscr = SB(big, "uscr", [128, 12 * TTA], F32)
            _sl = [uscr.ap()[:, i * TTA:(i + 1) * TTA] for i in range(12)]
            _map = {0: 0, 1: 1, 3: 2, 4: 3, 5: 4, 6: 5, 7: 6, 8: 7,
                    9: 8, 10: 9, 11: 10, 12: 11, 2: 3}
            U = [_sl[_map[i]] for i in range(13)]
            b_kv = SB(big, "b_kv", [128, TTA], F32)
            csm = SB(big, "csm", [128, 120], F32)
            zt = SB(big, "zt", [128, TTA], BF16)
            agp = uscr.ap()[:, 0:256]
            prr = SB(big, "prr", [128, 64], F32)
            pri = SB(big, "pri", [128, 64], F32)
            mtmp = SB(big, "mtmp", [128, 32], F32)
            selb = SB(big, "selb", [128, 64], F32)
            sg43 = SB(big, "sg43", [4, 4], F32)
            sg43b = SB(big, "sg43b", [4, 4], F32)
            row0 = SB(big, "row0", [1, 128], F32)
            row1 = SB(big, "row1", [1, 128], F32)
            grw = SB(big, "grw", [1, 256], F32)
            gnsr = uscr.ap()[0:1, 0:D]
            gnbr = uscr.ap()[0:1, D:2 * D]
            c1row = uscr.ap()[0:1, 2 * D:3 * D]
            c2row = uscr.ap()[0:1, 3 * D:4 * D]
            xnq = [SB(big, f"xnq{i}", [128, 8 * TTA], BF16) for i in range(3)]
            slab = [SB(big, f"slab{i}", [128, 2 * 8 * TTA], BF16)
                    for i in range(2)]
            dwq = [SB(big, f"dwq{i}", [128, 8 * D], BF16) for i in range(2)]
            acc = SB(big, "acc1", [128, 8 * D], F32)
            acc1_ref = acc
            hls = [SB(big, f"hls{i}", [128, 8 * 128], BF16) for i in range(2)]
            gel = [SB(big, f"gel{i}", [128, TTA], BF16) for i in range(2)]
            ups = [SB(big, f"ups{i}", [128, TTA], BF16) for i in range(2)]
            hstg = [SB(big, f"hstg{i}", [128, TTA], BF16) for i in range(2)]
            gms = [SB(big, "gms0", [128, D], BF16)]
            mfl = [SB(big, "mfl0", [128, DCN * 512], BF16)]
            woo = [SB(big, "woo0", [128, D], F32)]
            wo_sb = SB(big, "wo_sb", [128, 8 * D], BF16)
            zl = SB(big, "zlw", [128, TTA], BF16)
            sgl = SB(big, "sglw", [128, TTA], F32)
            mfb = [SB(big, "mfb0", [128, TTA], BF16)]
            pA = [bigp.enter_context(nc.psum_tensor(f"pA{i}", [128, TTA], F32))
                  for i in range(2)]
            pB = [bigp.enter_context(nc.psum_tensor(f"pB{i}", [128, TTA], F32))
                  for i in range(2)]
            pD = [bigp.enter_context(nc.psum_tensor(f"pD{i}", [128, TTA], F32))
                  for i in range(2)]
            pW = [bigp.enter_context(nc.psum_tensor(f"pW{i}", [128, TTA], F32))
                  for i in range(2)]
            ps43 = pW[1].ap()[0:4, 0:4]

            E0DONE = 960.0

            achain = []

            def ach(eng, cost, fn):
                achain.append((eng, cost, fn))

            def emit_scan(tt, dc):
                i = tt * 8 + dc
                t0 = tt * TTA

                def f_load(s, dc=dc, t0=t0, i=i):
                    if i == 0:
                        s.wait_ge(S["sPE0"], 128)
                        s.wait_ge(S["sMG"], 256)
                        s.wait_ge(S["sQGe"], 128)
                        s.wait_ge(S["sRT"], 64)
                        s.wait_ge(S["sPQ"], 64)
                        s.wait_ge(S["sSTb"], 32 * 64)
                    s.wait_ge(S["sSTa"], 48 * (i + 1))
                    s.dma_start(b_kv[:, :], stash_d[6, dc, :, t0:t0 + TTA]
                                ).then_inc(S["sQL"], 16)
                    s.dma_start(U[0], stash_d[7, dc, :, t0:t0 + TTA]
                                ).then_inc(S["sQL"], 16)
                    s.dma_start(U[5], stash_d[8, dc, :, t0:t0 + TTA]
                                ).then_inc(S["sQL"], 16)
                    cadd("sQL", 48)
                    s.wait_ge(S["sQL"], cget("sQL"))
                    return None
                ach("s", 2.5, f_load)

                def f_d4(v):
                    v.tensor_tensor_scan(U[1], mask64[:, :], U[0],
                                         0.0, ALU.mult, ALU.add)
                    v.tensor_tensor_scan(U[3], mask64[:, :], U[5],
                                         0.0, ALU.mult, ALU.add)
                    v.tensor_scalar(U[4], U[3], INV2PI, MAGIC,
                                    ALU.mult, ALU.add)
                    v.tensor_scalar(U[4], U[4], -MAGIC, None, ALU.add)
                    return v.scalar_tensor_tensor(U[4], U[4], -TWOPI, U[3],
                                                  ALU.mult, ALU.add)
                ach("v", 4.0, f_d4)

                def f_a5(a):
                    act(a, U[6], U[4], AF.Abs)
                    act(a, U[7], U[1], AF.Exp, scale=-1.0)
                    return act(a, U[1], U[1], AF.Exp)
                ach("a", 2.2, f_a5)

                def f_d5(v):
                    return v.tensor_scalar(U[6], U[6], -1.0,
                                           PI / 2, ALU.mult, ALU.add)
                ach("v", 0.8, f_d5)

                def f_a6(a):
                    act(a, U[4], U[4], AF.Sin)
                    return act(a, U[6], U[6], AF.Sin)
                ach("a", 1.6, f_a6)

                def f_d6(v, dc=dc):
                    c = csm
                    v.tensor_mul(U[3], U[7], b_kv[:, :])
                    v.tensor_mul(U[8], U[3], U[6])
                    v.tensor_mul(U[9], U[3], U[4])
                    v.reduce_sum(c[:, 0:8],
                                 U[8].rearrange("p (c j) -> p c j", c=8),
                                 axis=AX)
                    v.reduce_sum(c[:, 8:16],
                                 U[9].rearrange("p (c j) -> p c j", c=8),
                                 axis=AX)
                    eL63 = U[1][:, SUB - 1:TTA:SUB]
                    cos63 = U[6][:, SUB - 1:TTA:SUB]
                    sin63 = U[4][:, SUB - 1:TTA:SUB]
                    v.tensor_mul(c[:, 16:24], eL63, cos63)
                    v.tensor_mul(c[:, 24:32], eL63, sin63)
                    v.tensor_copy(c[:, 48:49], summ[:, 16 + dc:17 + dc])
                    v.tensor_copy(c[:, 57:58], summ[:, 24 + dc:25 + dc])
                    v.tensor_copy(c[:, 66:67], summ[:, 0 + dc:1 + dc])
                    v.tensor_copy(c[:, 75:76], summ[:, 8 + dc:9 + dc])
                    v.tensor_mul(c[:, 84:92], c[:, 16:24], c[:, 0:8])
                    v.tensor_mul(c[:, 32:40], c[:, 24:32], c[:, 8:16])
                    v.tensor_mul(c[:, 96:104], c[:, 24:32], c[:, 0:8])
                    v.tensor_mul(c[:, 40:48], c[:, 16:24], c[:, 8:16])
                    v.tensor_add(c[:, 32:40], c[:, 84:92], c[:, 32:40])
                    v.tensor_sub(c[:, 40:48], c[:, 96:104], c[:, 40:48])
                    for cc in range(NSUB):
                        Aer = c[:, 16 + cc:17 + cc]
                        Aei = c[:, 24 + cc:25 + cc]
                        Hr = c[:, 48 + cc:49 + cc]
                        Hi = c[:, 57 + cc:58 + cc]
                        Pr = c[:, 66 + cc:67 + cc]
                        Pi = c[:, 75 + cc:76 + cc]
                        p1 = c[:, 104:105]; p2 = c[:, 105:106]
                        p3 = c[:, 106:107]; p4 = c[:, 107:108]
                        a1 = c[:, 108:109]; a2 = c[:, 109:110]
                        b1 = c[:, 110:111]; b2 = c[:, 111:112]
                        tA = c[:, 112:113]; tB = c[:, 113:114]
                        v.tensor_mul(p1, Pr, Aer)
                        v.tensor_mul(p2, Pi, Aei)
                        v.tensor_mul(p3, Pr, Aei)
                        v.tensor_mul(p4, Pi, Aer)
                        v.tensor_mul(a1, Aer, Hr)
                        v.tensor_mul(a2, Aei, Hi)
                        v.tensor_mul(b1, Aer, Hi)
                        v.tensor_mul(b2, Aei, Hr)
                        v.tensor_sub(c[:, 67 + cc:68 + cc], p1, p2)
                        v.tensor_add(c[:, 76 + cc:77 + cc], p3, p4)
                        v.tensor_sub(tA, a1, a2)
                        v.tensor_add(tB, b1, b2)
                        v.tensor_add(c[:, 49 + cc:50 + cc], tA,
                                     c[:, 32 + cc:33 + cc])
                        v.tensor_add(c[:, 58 + cc:59 + cc], tB,
                                     c[:, 40 + cc:41 + cc])
                    v.tensor_copy(summ[:, 0 + dc:1 + dc], c[:, 74:75])
                    v.tensor_copy(summ[:, 8 + dc:9 + dc], c[:, 83:84])
                    v.tensor_copy(summ[:, 16 + dc:17 + dc], c[:, 56:57])
                    v.tensor_copy(summ[:, 24 + dc:25 + dc], c[:, 65:66])
                    j0r = U[8][:, 0:TTA:SUB]
                    j0i = U[9][:, 0:TTA:SUB]
                    v.tensor_add(j0r, j0r, c[:, 48:56])
                    v.tensor_sub(j0i, j0i, c[:, 57:65])
                    v.tensor_tensor_scan(U[7], mask64[:, :], U[8],
                                         0.0, ALU.mult, ALU.add)
                    v.tensor_tensor_scan(U[3], mask64[:, :], U[9],
                                         0.0, ALU.mult, ALU.add)
                    v.tensor_mul(U[10], U[1], U[6])
                    v.tensor_mul(U[11], U[1], U[4])
                    v.tensor_mul(U[0], U[10], U[7])
                    v.tensor_mul(U[5], U[11], U[3])
                    v.tensor_add(U[0], U[0], U[5])
                    v.tensor_mul(U[5], U[11], U[7])
                    v.tensor_mul(U[6], U[10], U[3])
                    v.tensor_sub(U[5], U[5], U[6])
                    APrb = c[:, 66:74].unsqueeze(2).broadcast_to(
                        (128, 8, SUB))
                    APib = c[:, 75:83].unsqueeze(2).broadcast_to(
                        (128, 8, SUB))
                    Ar3 = U[10].rearrange("p (c j) -> p c j", c=8)
                    Ai3 = U[11].rearrange("p (c j) -> p c j", c=8)
                    t4v = U[4].rearrange("p (c j) -> p c j", c=8)
                    t6v = U[6].rearrange("p (c j) -> p c j", c=8)
                    t12v = U[12].rearrange("p (c j) -> p c j", c=8)
                    v.tensor_tensor(t4v, Ar3, APrb, ALU.mult)
                    v.tensor_tensor(t6v, Ai3, APib, ALU.mult)
                    v.tensor_sub(U[4], U[4], U[6])
                    v.tensor_tensor(t6v, Ar3, APib, ALU.mult)
                    v.tensor_tensor(t12v, Ai3, APrb, ALU.mult)
                    return v.tensor_add(U[6], U[6], U[12])
                ach("v", 20.0, f_d6)

                def f_st(s, tt=tt, dc=dc):
                    t0 = tt * TTA
                    s.dma_start(stash_d[0, dc, :, t0:t0 + TTA],
                                U[0]).then_inc(S["sSTsc"], 16)
                    s.dma_start(stash_d[1, dc, :, t0:t0 + TTA],
                                U[5]).then_inc(S["sSTsc"], 16)
                    s.dma_start(stash_d[2, dc, :, t0:t0 + TTA],
                                U[4]).then_inc(S["sSTsc"], 16)
                    s.dma_start(stash_d[3, dc, :, t0:t0 + TTA],
                                U[6]).then_inc(S["sSTsc"], 16)
                    cadd("sSTsc", 64)
                    s.wait_ge(S["sSTsc"], cget("sSTsc"))
                    return None
                ach("s", 2.5, f_st)

            for tt in range(NTT):
                for dc in range(DCN):
                    emit_scan(tt, dc)

            x1_marker = len(achain)
            X1EST = 3600.0

            def x1_store(s):
                s.dma_start(summ_in_d[:, :], summ[:, :]).then_inc(S["sQS"], 16)
                cadd("sQS", 16)
                s.wait_ge(S["sQS"], cget("sQS"))
                return None
            ach("s", 1.0, x1_store)

            def x1_coll(g):
                g.collective_compute(
                    "AllGather", ALU.bypass,
                    replica_groups=[list(range(8))],
                    ins=[summ_in_d.ap().opt()],
                    outs=[summ_ag_d.ap().opt()]).then_inc(S["sCC"], 1)
                cadd("sCC", 1)
                g.wait_ge(S["sCC"], cget("sCC"))
                return None
            ach("g", 60.0, x1_coll)

            def x1_load(s):
                s.dma_start(
                    agp.rearrange("p (r c) -> p r c", r=8),
                    summ_ag_d.ap().rearrange("(r p) c -> p r c", p=128)
                ).then_inc(S["sQL"], 16)
                cadd("sQL", 16)
                s.wait_ge(S["sQL"], cget("sQL"))
                return None
            ach("s", 1.0, x1_load)

            def x1_pref(v):
                v.memset(prr[:, :], 0.0)
                v.memset(pri[:, :], 0.0)
                for r in range(1, 8):
                    if r == 4:
                        continue
                    co = (r - 1) * 32
                    def colA(dc):
                        return agp[:, co + dc:co + dc + 1]
                    def colAi(dc):
                        return agp[:, co + 8 + dc:co + 8 + dc + 1]
                    def colH(dc):
                        return agp[:, co + 16 + dc:co + 16 + dc + 1]
                    def colHi(dc):
                        return agp[:, co + 24 + dc:co + 24 + dc + 1]
                    for dc in range(DCN):
                        v.tensor_mul(mtmp[:, dc:dc + 1], colA(dc),
                                     prr[:, dc * 8 + r - 1:dc * 8 + r])
                    for dc in range(DCN):
                        v.tensor_mul(mtmp[:, 8 + dc:9 + dc], colAi(dc),
                                     pri[:, dc * 8 + r - 1:dc * 8 + r])
                    for dc in range(DCN):
                        v.tensor_mul(mtmp[:, 16 + dc:17 + dc], colA(dc),
                                     pri[:, dc * 8 + r - 1:dc * 8 + r])
                    for dc in range(DCN):
                        v.tensor_mul(mtmp[:, 24 + dc:25 + dc], colAi(dc),
                                     prr[:, dc * 8 + r - 1:dc * 8 + r])
                    for dc in range(DCN):
                        v.tensor_sub(mtmp[:, dc:dc + 1], mtmp[:, dc:dc + 1],
                                     mtmp[:, 8 + dc:9 + dc])
                    for dc in range(DCN):
                        v.tensor_add(mtmp[:, 16 + dc:17 + dc],
                                     mtmp[:, 16 + dc:17 + dc],
                                     mtmp[:, 24 + dc:25 + dc])
                    for dc in range(DCN):
                        v.tensor_add(prr[:, dc * 8 + r:dc * 8 + r + 1],
                                     mtmp[:, dc:dc + 1], colH(dc))
                    for dc in range(DCN):
                        v.tensor_add(pri[:, dc * 8 + r:dc * 8 + r + 1],
                                     mtmp[:, 16 + dc:17 + dc], colHi(dc))
                for dc in range(DCN):
                    v.tensor_mul(selb[:, dc * 8:(dc + 1) * 8],
                                 prr[:, dc * 8:(dc + 1) * 8], onehot[:, :])
                for dc in range(DCN):
                    v.reduce_sum(hg[:, 2 * dc:2 * dc + 1],
                                 selb[:, dc * 8:(dc + 1) * 8], axis=AX)
                for dc in range(DCN):
                    v.tensor_mul(selb[:, dc * 8:(dc + 1) * 8],
                                 pri[:, dc * 8:(dc + 1) * 8], onehot[:, :])
                last = None
                for dc in range(DCN):
                    last = v.reduce_sum(hg[:, 2 * dc + 1:2 * dc + 2],
                                        selb[:, dc * 8:(dc + 1) * 8], axis=AX)
                return last
            ach("v", 25.0, x1_pref)

            l_h0r = U[1]; l_h0i = U[2]; l_acr = U[3]
            l_aci = U[7]; l_q = U[8]; l_sg = U[9]
            w_1 = U[10]; w_2 = U[11]; w_3 = U[12]
            acc4 = csm.ap()[:, 0:4]

            def emit_a2(tt, dc):
                t0 = tt * TTA
                i = tt * 8 + dc

                def f_l(s, dc=dc, t0=t0, i=i):
                    if i == 0:
                        s.wait_ge(S["sSTsc"], cget("sSTsc"))
                        s.wait_ge(S["sSTb"], 32 * 64)
                    for qi, dst in ((0, l_h0r), (1, l_h0i), (2, l_acr),
                                    (3, l_aci), (4, l_q), (5, l_sg)):
                        s.dma_start(dst, stash_d[qi, dc, :, t0:t0 + TTA]
                                    ).then_inc(S["sQL"], 16)
                        cadd("sQL", 16)
                    s.wait_ge(S["sQL"], cget("sQL"))
                    return None
                ach("s", 4.0, f_l)

                def f_d(v, dc=dc):
                    hgr = hg[:, 2 * dc:2 * dc + 1]
                    hgi = hg[:, 2 * dc + 1:2 * dc + 2]
                    v.tensor_scalar(w_1, l_acr, hgr, None, ALU.mult)
                    v.tensor_scalar(w_2, l_aci, hgi, None, ALU.mult)
                    v.tensor_sub(w_1, w_1, w_2)
                    v.tensor_add(w_1, w_1, l_h0r)
                    v.tensor_scalar(w_2, l_acr, hgi, None, ALU.mult)
                    v.tensor_scalar(w_3, l_aci, hgr, None, ALU.mult)
                    v.tensor_add(w_2, w_2, w_3)
                    v.tensor_add(w_2, w_2, l_h0i)
                    v.tensor_mul(w_1, l_q, w_1)
                    v.tensor_mul(w_2, l_q, w_2)
                    return v.tensor_mul(zt[:, :], w_1, l_sg)
                ach("v", 8.0, f_d)

                def f_a(a):
                    act(a, w_3, w_1, AF.Copy, accum_out=acc4[:, 0:1])
                    act(a, w_3, w_2, AF.Copy, accum_out=acc4[:, 1:2])
                    act(a, w_3, w_1, AF.Square, accum_out=acc4[:, 2:3])
                    return act(a, w_3, w_2, AF.Square, accum_out=acc4[:, 3:4])
                ach("a", 3.0, f_a)

                def f_d2(v, dc=dc):
                    return v.tensor_add(stats_acc[:, dc * 4:(dc + 1) * 4],
                                        stats_acc[:, dc * 4:(dc + 1) * 4],
                                        acc4[:, :])
                ach("v", 0.4, f_d2)

                def f_s(s, dc=dc, t0=t0):
                    s.dma_start(z_d[dc, :, t0:t0 + TTA], zt[:, :]
                                ).then_inc(S["sQS"], 16)
                    cadd("sQS", 16)
                    s.wait_ge(S["sQS"], cget("sQS"))
                    return None
                ach("s", 1.2, f_s)

            for tt in range(NTT):
                for dc in range(DCN):
                    emit_a2(tt, dc)

            X2EST = 20000.0

            def x2_mm(dc):
                def f_t(t, dc=dc):
                    return t.matmul(ps43, gmap[:, :],
                                    stats_acc[:, dc * 4:(dc + 1) * 4],
                                    start=True, stop=True)
                ach("t", 0.3, f_t)

                def f_v(v):
                    v.tensor_scalar(sg43[:, :], ps43, bsel[0:4, 0:1], None,
                                    ALU.mult)
                    return v.tensor_scalar(sg43b[:, :], ps43, bsel[0:4, 1:2],
                                           None, ALU.mult)
                ach("v", 0.3, f_v)

                def f_s(s, dc=dc):
                    s.dma_start(
                        stats_in_d.ap()[0:1, dc * 16:(dc + 1) * 16]
                        .rearrange("o (a b) -> a (o b)", a=4),
                        sg43[:, :]).then_inc(S["sQS"], 16)
                    s.dma_start(
                        stats_in_d.ap()[1:2, dc * 16:(dc + 1) * 16]
                        .rearrange("o (a b) -> a (o b)", a=4),
                        sg43b[:, :]).then_inc(S["sQS"], 16)
                    cadd("sQS", 32)
                    s.wait_ge(S["sQS"], cget("sQS"))
                    return None
                ach("s", 0.6, f_s)

            achain.append(("v", 0.0, None))
            achain.pop()
            x2_start_marker = len(achain)
            for dc in range(DCN):
                x2_mm(dc)

            def x2_coll(g):
                g.collective_compute(
                    "AllReduce", ALU.add,
                    replica_groups=[list(range(8))],
                    ins=[stats_in_d.ap().opt()],
                    outs=[stats_out_d.ap().opt()]).then_inc(S["sCC"], 1)
                cadd("sCC", 1)
                g.wait_ge(S["sCC"], cget("sCC"))
                return None
            ach("g", 60.0, x2_coll)

            def x2_load(s):
                s.dma_start(row0[:, :], stats_out_d[0:1, :]
                            ).then_inc(S["sQL"], 16)
                s.dma_start(row1[:, :], stats_out_d[1:2, :]
                            ).then_inc(S["sQL"], 16)
                s.dma_start(gnsr[:, :], gnsr_d[:, :]).then_inc(S["sQL"], 16)
                s.dma_start(gnbr[:, :], gnbr_d[:, :]).then_inc(S["sQL"], 16)
                cadd("sQL", 64)
                s.wait_ge(S["sQL"], cget("sQL"))
                return None
            ach("s", 2.0, x2_load)

            def x2_v1(v):
                v.tensor_scalar(row0[:, :], row0[:, :], bsel[0:1, 0:1],
                                None, ALU.mult)
                v.tensor_scalar(row1[:, :], row1[:, :], bsel[0:1, 1:2],
                                None, ALU.mult)
                v.tensor_add(row0[:, :], row0[:, :], row1[:, :])
                r4 = row0.ap().rearrange("o (g j) -> o g j", g=32)
                v.tensor_tensor(grw.ap()[0:1, 160:192].unsqueeze(2),
                                r4[:, :, 2:3], r4[:, :, 3:4], ALU.add)
                v.tensor_scalar(grw.ap()[0:1, 0:32].unsqueeze(2),
                                r4[:, :, 0:1], CNT_INV, None, ALU.mult)
                v.tensor_scalar(grw.ap()[0:1, 32:64].unsqueeze(2),
                                r4[:, :, 1:2], CNT_INV, None, ALU.mult)
                v.tensor_scalar(grw[0:1, 64:96], grw[0:1, 160:192],
                                CNT_INV, None, ALU.mult)
                v.tensor_mul(grw[0:1, 96:128], grw[0:1, 0:32],
                             grw[0:1, 0:32])
                v.tensor_mul(grw[0:1, 128:160], grw[0:1, 32:64],
                             grw[0:1, 32:64])
                v.tensor_sub(grw[0:1, 64:96], grw[0:1, 64:96],
                             grw[0:1, 96:128])
                v.memset(grw[0:1, 224:225], 0.0)
                return v.tensor_sub(grw[0:1, 64:96], grw[0:1, 64:96],
                                    grw[0:1, 128:160])
            ach("v", 3.0, x2_v1)

            def x2_a1(a):
                return act(a, grw[0:1, 96:128], grw[0:1, 64:96], AF.Sqrt,
                           bias=epsc[0:1, 1:2])
            ach("a", 1.0, x2_a1)

            def x2_v2(v):
                v.reciprocal(grw[0:1, 192:224], grw[0:1, 96:128])
                rstd_b = grw.ap()[0:1, 192:224].unsqueeze(2).broadcast_to(
                    (1, 32, 32))
                mre_b = grw.ap()[0:1, 0:32].unsqueeze(2).broadcast_to(
                    (1, 32, 32))
                g3 = gnsr.rearrange("o (g j) -> o g j", g=32)
                c13 = c1row.rearrange("o (g j) -> o g j", g=32)
                c23 = c2row.rearrange("o (g j) -> o g j", g=32)
                v.memset(grw[0:1, 224:225], 0.0)
                v.tensor_tensor(c13, g3, rstd_b, ALU.mult)
                v.memset(grw[0:1, 225:226], 0.0)
                v.tensor_tensor(c23, c13, mre_b, ALU.mult)
                v.memset(grw[0:1, 226:227], 0.0)
                return v.tensor_sub(c2row[:, :], gnbr[:, :], c2row[:, :])
            ach("v", 3.0, x2_v2)

            def x2_st(s):
                s.dma_start(crow_d[0:1, :], c1row[:, :]).then_inc(S["sQS"], 16)
                s.dma_start(crow_d[1:2, :], c2row[:, :]).then_inc(S["sQS"], 16)
                cadd("sQS", 32)
                s.wait_ge(S["sQS"], cget("sQS"))
                for dc in range(DCN):
                    s.dma_start(c1pp[:, dc:dc + 1],
                                crow_d[0:1, dc * 128:(dc + 1) * 128]
                                .rearrange("o p -> p o")).then_inc(S["sQL"], 16)
                    s.dma_start(c2pp[:, dc:dc + 1],
                                crow_d[1:2, dc * 128:(dc + 1) * 128]
                                .rearrange("o p -> p o")).then_inc(S["sQL"], 16)
                    cadd("sQL", 32)
                s.wait_ge(S["sQL"], cget("sQL"))
                return None
            ach("s", 3.0, x2_st)

            def emit_wop(tt):
                t0 = tt * TTA

                def f_one(s, dc, t0=t0):
                    s.dma_start(zl[:, :], z_d[dc, :, t0:t0 + TTA]
                                ).then_inc(S["sQL"], 16)
                    s.dma_start(sgl[:, :], stash_d[5, dc, :, t0:t0 + TTA]
                                ).then_inc(S["sQL"], 16)
                    cadd("sQL", 32)
                    s.wait_ge(S["sQL"], cget("sQL"))
                    return None

                for dc in range(DCN):
                    def f_ld(s, dc=dc):
                        return f_one(s, dc)
                    ach("s", 1.5, f_ld)

                    def f_m(v, dc=dc):
                        v.tensor_scalar(U[1], zl[:, :],
                                        c1pp[:, dc:dc + 1], None, ALU.mult)
                        v.tensor_scalar(U[2], sgl[:, :],
                                        c2pp[:, dc:dc + 1], None, ALU.mult)
                        return v.tensor_add(mfb[0][:, :], U[1], U[2])
                    ach("v", 2.2, f_m)

                    def f_st2(s, tt=tt, dc=dc):
                        s.dma_start(mfm_d[tt, dc, :, :], mfb[0][:, :]
                                    ).then_inc(S["sQS"], 16)
                        cadd("sQS", 16)
                        s.wait_ge(S["sQS"], cget("sQS"))
                        return None
                    ach("s", 0.8, f_st2)

            for tt in range(NTT):
                emit_wop(tt)

            aidx = [0]
            acursor = [E0DONE]

            def chain_stage(eng, fn, idx):
                def run(engine, fn=fn, idx=idx):
                    engine.wait_ge(S["sAC"], idx)
                    last = fn(engine)
                    if last is not None:
                        last.then_inc(S["sAC"], 1)
                    else:
                        engine.sem_inc(S["sAC"], 1)
                return run

            for n, (eng, cost, fn) in enumerate(achain):
                if n == x1_marker:
                    acursor[0] = max(acursor[0], X1EST)
                if n == x2_start_marker:
                    acursor[0] = max(acursor[0], X2EST)
                acursor[0] += cost
                ADD(eng, acursor[0], chain_stage(eng, fn, n))
            ACH_TOTAL = len(achain)

            E1T0 = 1000.0
            GUS = 2.3
            cnt_g = [0]
            cnt_sb = [0]
            cnt_hst = [0]
            cnt_hls = [0]
            cnt_dwq = [0]
            cnt_j2 = [0]
            cnt_xq = [0]
            pend_st = [None]
            est1 = [E1T0]

            pe_j2_at_hls = {}

            def gu_slab(tcb, e, fb):
                sb = cnt_sb[0]; cnt_sb[0] += 1
                est = est1[0] + (0 if sb < 2 else 0)

                def f_g(g, tcb=tcb, e=e, fb=fb, sb=sb):
                    if sb == 0:
                        g.wait_ge(S["sQGe"], 128)
                        g.wait_ge(S["sRT"], 64)
                    if sb >= 2:
                        g.wait_ge(S["sPE1"], 16 * (sb - 1))
                    g.dma_start(
                        slab[sb % 2].ap()[:, 0:8 * TTA]
                        .rearrange("p (c f) -> p c f", c=8),
                        guT_d[e, :, fb * TTA:(fb + 1) * TTA]
                        .rearrange("(c p) f -> p c f", p=128)
                        ).then_inc(S["sGW"], 16)
                    g.dma_start(
                        slab[sb % 2].ap()[:, 8 * TTA:16 * TTA]
                        .rearrange("p (c f) -> p c f", c=8),
                        guT_d[e, :, FF + fb * TTA:FF + (fb + 1) * TTA]
                        .rearrange("(c p) f -> p c f", p=128)
                        ).then_inc(S["sGW"], 16)
                    return None
                ADD("g", est - GUS * 16, f_g)
                return sb

            def gu_groups(tcb, e, fb, sb):
                for tc in range(2):
                    xq = tcb * 2 + tc
                    if xq >= cnt_xq[0]:
                        cnt_xq[0] = xq + 1

                        def f_xq(s, tcb=tcb, tc=tc, xq=xq):
                            if xq == 0:
                                s.wait_ge(S["sSTx"], 16 * 64)
                                s.wait_ge(S["sPQ"], 64)
                                s.wait_ge(S["sQGe"], 128)
                                s.wait_ge(S["sRT"], 64)
                            if xq >= 3:
                                s.wait_ge(S["sPE1"],
                                          1024 * ((xq - 3) // 2 + 1))
                            s.dma_start(
                                xnq[xq % 3].ap().rearrange(
                                    "p (c t) -> p c t", c=8),
                                xnf_d[:, tcb, :, tc * TTA:(tc + 1) * TTA]
                                .rearrange("c p t -> p c t")
                            ).then_inc(S["sZL"], 16)
                            cadd("sZL", 16)
                            return None
                        ADD("s", est1[0] - GUS * 20, f_xq)
                for tc in range(2):
                    xq = tcb * 2 + tc
                    for fc4 in range(4):
                        fc = fb * 4 + fc4
                        for gu in range(2):
                            gidx = cnt_g[0]; cnt_g[0] += 1
                            pair = gidx // 2
                            est1[0] += GUS

                            def f_t(t, sb=sb, tc=tc, fc4=fc4, gu=gu,
                                    gidx=gidx, pair=pair, xq=xq):
                                t.wait_ge(S["sGW"], 32 * (sb + 1))
                                t.wait_ge(S["sZL"],
                                          128 + 16 * (xq + 1))
                                ps = (pA if gu == 0 else pB)[pair % 2]
                                sem = S["sACg"] if gu == 0 else S["sACu"]
                                if pair >= 2:
                                    t.wait_ge(sem, pair - 1)
                                off = gu * 8 * TTA
                                last = None
                                for kc in range(8):
                                    last = t.matmul(
                                        ps[:, :],
                                        slab[sb % 2][:, off + kc * TTA +
                                                     fc4 * 128:
                                                     off + kc * TTA +
                                                     (fc4 + 1) * 128],
                                        xnq[xq % 3][:, kc * TTA:
                                                    (kc + 1) * TTA],
                                        start=(kc == 0), stop=(kc == 7))
                                last.then_inc(S["sPE1"], 1)
                                return None
                            ADD("t", est1[0], f_t)

                            def f_a(a, gu=gu, pair=pair, gidx=gidx):
                                a.wait_ge(S["sPE1"], gidx + 1)
                                if pair >= 2:
                                    a.wait_ge(S["sGM"], pair - 1)
                                if gu == 0:
                                    act(a, gel[pair % 2][:, :],
                                        pA[pair % 2][:, :],
                                        AF.Gelu).then_inc(S["sACg"], 1)
                                else:
                                    act(a, ups[pair % 2][:, :],
                                        pB[pair % 2][:, :],
                                        AF.Copy).then_inc(S["sACu"], 1)
                                return None
                            ADD("a", est1[0] + 0.3, f_a)

                        pair = (cnt_g[0] - 1) // 2
                        hst = cnt_hst[0]; cnt_hst[0] += 1

                        def f_gm(g, pair=pair, tcb=tcb, e=e, tc=tc, fc=fc,
                                 hst=hst):
                            g.wait_ge(S["sACg"], pair + 1)
                            g.wait_ge(S["sACu"], pair + 1)
                            if pair >= 2:
                                g.wait_ge(S["sHS"], 16 * (pair - 1))
                            g.tensor_mul(hstg[pair % 2][:, :],
                                         ups[pair % 2][:, :],
                                         gel[pair % 2][:, :]
                                         ).then_inc(S["sGM"], 1)
                            if pend_st[0] is not None:
                                pend_st[0](g)
                                pend_st[0] = None

                            def st(g2, pair=pair, e=e, tc=tc, fc=fc):
                                g2.wait_ge(S["sGM"], pair + 1)
                                g2.dma_start(
                                    hact_d[e % 2, fc // 8,
                                           tc * 4:(tc + 1) * 4,
                                           fc % 8, :, :]
                                    .rearrange("t8 p t -> p t8 t"),
                                    hstg[pair % 2].ap().rearrange(
                                        "p (t8 t) -> p t8 t", t8=4)
                                    ).then_inc(S["sHS"], 16)
                            pend_st[0] = st
                            return None
                        ADD("g", est1[0] + 0.6, f_gm)

            def down_pass(tcb, e, q):
                dq = cnt_dwq[0]; cnt_dwq[0] += 1

                def f_dw(g, e=e, q=q, dq=dq):
                    if pend_st[0] is not None:
                        pend_st[0](g)
                        pend_st[0] = None
                    if dq >= 2:
                        g.wait_ge(S["sPE2"], (dq - 1) * 16)
                    g.dma_start(
                        dwq[dq % 2].ap().rearrange("p (c h) -> p c h", c=8),
                        dwq_d[e, q * 8:(q + 1) * 8, :, :]
                        .rearrange("c p h -> p c h")
                        ).then_inc(S["sDWQ"], 16)
                    return None
                ADD("g", est1[0] - GUS * 16, f_dw)

                stneed = (tcb * 8 + e) * 64 + (q + 1) * 8 * 2

                for t8 in range(8):
                    hl = cnt_hls[0]; cnt_hls[0] += 1

                    def f_hl(s, tcb=tcb, e=e, q=q, t8=t8, hl=hl,
                             stneed=stneed):
                        if hl >= 2:
                            s.wait_ge(S["sPE2"], (hl - 1) * 2)
                        s.wait_ge(S["sHS"], 16 * stneed)
                        s.dma_start(
                            hls[hl % 2].ap().rearrange(
                                "p (c t) -> p c t", c=8),
                            hact_d[e % 2, q, t8, :, :, :]
                            .rearrange("c p t -> p c t")
                            ).then_inc(S["sDH"], 16)
                        return None
                    ADD("s", est1[0] - GUS * 2, f_hl)

                    est1[0] += 2.9
                    for h2 in range(2):
                        j2 = cnt_j2[0]; cnt_j2[0] += 1

                        def f_t(t, hl=hl, h2=h2, j2=j2, dq=dq):
                            t.wait_ge(S["sDH"], 16 * (hl + 1))
                            t.wait_ge(S["sDWQ"], 16 * (dq + 1))
                            if j2 >= 2:
                                t.wait_ge(S["sDV2"], j2 - 1)
                            last = None
                            for fcm in range(8):
                                last = t.matmul(
                                    pD[j2 % 2][:, :],
                                    hls[hl % 2][:, fcm * 128:(fcm + 1) * 128],
                                    dwq[dq % 2][:, fcm * D + h2 * TTA:
                                                fcm * D + (h2 + 1) * TTA],
                                    start=(fcm == 0), stop=(fcm == 7))
                            last.then_inc(S["sPE2"], 1)
                            return None
                        ADD("t", est1[0], f_t)

                        def f_v(v, tcb=tcb, e=e, q=q, t8=t8, h2=h2, j2=j2):
                            v.wait_ge(S["sPE2"], j2 + 1)
                            first = (e == 0 and q == 0)
                            cslice = combine[:, (tcb * 8 + t8) * 8 + e:
                                             (tcb * 8 + t8) * 8 + e + 1]
                            dst = acc[:, t8 * D + h2 * TTA:
                                      t8 * D + (h2 + 1) * TTA]
                            if first:
                                if tcb > 0:
                                    v.wait_ge(S["sSIL"], 8 * tcb)
                                v.tensor_scalar(dst, pD[j2 % 2][:, :],
                                                cslice, None, ALU.mult
                                                ).then_inc(S["sDV2"], 1)
                            else:
                                v.scalar_tensor_tensor(
                                    dst, pD[j2 % 2][:, :], cslice, dst,
                                    ALU.mult, ALU.add).then_inc(S["sDV2"], 1)
                            return None
                        ADD("v", est1[0] + 0.3, f_v)

            def quarter_silu(tcb):
                for t8 in range(8):
                    def f_sil(a, tcb=tcb, t8=t8):
                        if t8 == 0:
                            a.wait_ge(S["sDV2"], 512 * (tcb + 1))
                        if tcb * 8 + t8 >= 1:
                            a.wait_ge(S["sGMS"], 16 * (tcb * 8 + t8))
                        act(a, gms[0][:, :], acc[:, t8 * D:(t8 + 1) * D],
                            AF.Silu).then_inc(S["sSIL"], 1)
                        a.wait_ge(S["sSIL"], tcb * 8 + t8 + 1)
                        a.dma_start(gmoe_d[tcb * 8 + t8, :, :],
                                    gms[0][:, :]).then_inc(S["sGMS"], 16)
                        return None
                    ADD("a", est1[0] + 2.0 + t8 * 1.2, f_sil)

            def f_warm(t):
                for ps in (pA[0], pA[1], pB[0], pB[1], pD[0], pD[1],
                           pW[0], pW[1]):
                    for c4 in range(4):
                        t.matmul(ps[:, c4 * 128:(c4 + 1) * 128],
                                 ident[:, :], ident[:, :],
                                 start=True, stop=True)
                return None
            ADD("t", E0DONE + 2.0, f_warm)

            for tcb in range(NQT):
                for e in range(E):
                    sbs = []
                    plan = ["s0", "s1", "s2", "s3", "p0", "s4", "s5", "p1",
                            "s6", "s7", "p2", "p3"]
                    for item in plan:
                        k = int(item[1])
                        if item[0] == "s":
                            sb = gu_slab(tcb, e, k)
                            gu_groups(tcb, e, k, sb)
                            if k == 7:
                                def f_fl(g):
                                    if pend_st[0] is not None:
                                        pend_st[0](g)
                                        pend_st[0] = None
                                    return None
                                ADD("g", est1[0] + 0.7, f_fl)
                        else:
                            down_pass(tcb, e, k)
                quarter_silu(tcb)

            GU_TOTAL = cnt_g[0]
            J2_TOTAL = cnt_j2[0]

            woest = [20600.0]

            def f_wo_w(g):
                g.wait_ge(S["sQGe"], 128)
                g.wait_ge(S["sRT"], 64)
                g.dma_start(wo_sb.ap().rearrange("p (c m) -> p c m", c=8),
                            wo_d.ap().rearrange("(c p) m -> p c m", p=128)
                            ).then_inc(S["sWL"], 16)
                return None
            ADD("g", 1500.0, f_wo_w)

            cnt_wo = [0]
            for tt in range(NTT):
                def f_ml(s, tt=tt):
                    if tt == 0:
                        s.wait_ge(S["sAC"], ACH_TOTAL)
                    if tt >= 1:
                        s.wait_ge(S["sWPE"], tt * 8)
                    s.dma_start(
                        mfl[0].ap().rearrange("p (c t) -> p c t", c=8),
                        mfm_d[tt].rearrange("c p t -> p c t")
                    ).then_inc(S["sWL"], 16)
                    return None
                ADD("s", woest[0], f_ml)
                woest[0] += 4.0

                for s4 in range(4):
                    for half in range(2):
                        w8 = cnt_wo[0]; cnt_wo[0] += 1

                        def f_pe(t, tt=tt, s4=s4, half=half, w8=w8):
                            t.wait_ge(S["sWL"], 16 * (tt + 2))
                            if w8 >= 2:
                                t.wait_ge(S["sWEV"], w8 - 1)
                            last = None
                            for dc in range(DCN):
                                last = t.matmul(
                                    pW[w8 % 2][:, :],
                                    mfl[0][:, dc * 512 + s4 * 128:
                                           dc * 512 + (s4 + 1) * 128],
                                    wo_sb[:, dc * D + half * 512:
                                          dc * D + (half + 1) * 512],
                                    start=(dc == 0), stop=(dc == 7))
                            last.then_inc(S["sWPE"], 1)
                            return None
                        woest[0] += 2.4
                        ADD("t", woest[0], f_pe)

                        def f_ev(v, s4=s4, half=half, w8=w8):
                            v.wait_ge(S["sWPE"], w8 + 1)
                            if w8 >= 2:
                                v.wait_ge(S["sWST"], 16 * (w8 // 2))
                            v.tensor_copy(
                                woo[0][:, half * 512:(half + 1) * 512],
                                pW[w8 % 2][:, :]).then_inc(S["sWEV"], 1)
                            return None
                        ADD("v", woest[0] + 0.4, f_ev)

                    def f_st(s, tt=tt, s4=s4):
                        s.wait_ge(S["sWEV"], tt * 8 + (s4 + 1) * 2)
                        s.dma_start(wo_out_d[tt * 4 + s4, :, :],
                                    woo[0][:, :]).then_inc(S["sWST"], 16)
                        return None
                    ADD("s", woest[0] + 0.8, f_st)

            for k in streams:
                streams[k].sort(key=lambda x: (x[0], x[1]))

            with nc.Block() as blk:
                @blk.tensor
                def _(t):
                    for _, _, fn in streams["t"]:
                        fn(t)

                @blk.vector
                def _(v):
                    for _, _, fn in streams["v"]:
                        fn(v)

                @blk.scalar
                def _(a):
                    for _, _, fn in streams["a"]:
                        fn(a)

                @blk.sync
                def _(s):
                    for _, _, fn in streams["s"]:
                        fn(s)

                @blk.gpsimd
                def _(g):
                    for _, _, fn in streams["g"]:
                        fn(g)

        with ExitStack() as bs, ExitStack() as bsp:
            w1_sb = SB(bs, "w1_sb", [128, 8 * FF], BF16)
            w2_sb = SB(bs, "w2_sb", [128, 32 * D], BF16)
            b1c = SB(bs, "b1c_sb", [128, 32], F32)
            b2b = SB(bs, "b2b_sb", [128, D], BF16)
            l1sb = SB(bs, "l1sb_sb", [128, D], BF16)
            l1bb = SB(bs, "l1bb_sb", [128, D], BF16)
            l2sb = SB(bs, "l2sb_sb", [128, D], BF16)
            l2bb = SB(bs, "l2bb_sb", [128, D], BF16)
            xb_tm = SB(bs, "xb_tm", [128, 2 * D], F32)
            wol = SB(bs, "wol", [128, 2 * D], F32)
            gml = SB(bs, "gml", [128, 2 * D], BF16)
            t1b = SB(bs, "t1b", [128, 2 * D], F32)
            xh_tm = SB(bs, "xh_tm", [128, 2 * D], F32)
            xh_bf = SB(bs, "xh_bf", [128, 2 * D], BF16)
            xh_fm = SB(bs, "xh_fm", [128, 8 * TTB], BF16)
            hh_fm = SB(bs, "hh_fm", [128, 32 * TTB], BF16)
            stb = SB(bs, "stb", [128, 32], F32)
            eps2 = SB(bs, "eps2", [128, 2], F32)
            ps_big = bsp.enter_context(
                nc.psum_tensor("ps_big", [128, 2 * D], F32))
            trb_ps = bsp.enter_context(
                nc.psum_tensor("trb_ps", [128, 2 * D], BF16))
            p1 = [bsp.enter_context(
                nc.psum_tensor(f"p1_{i}", [128, TTB], F32)) for i in range(2)]

            sched = []
            bq0 = cget("sQL")
            cadd("sQL", 128)

            def f_w(g):
                g.wait_ge(S["sPE1"], GU_TOTAL)
                g.wait_ge(S["sPE2"], J2_TOTAL)
                g.wait_ge(S["sWPE"], 64)
                g.wait_ge(S["sSIL"], 32)
                g.wait_ge(S["sWST"], 16 * 32)
                g.wait_ge(S["sGMS"], 16 * 32)
                g.wait_ge(S["sHS"], 16 * 2048)
                g.wait_ge(S["sGM"], 2048)
                g.wait_ge(S["sAC"], ACH_TOTAL)
                g.dma_start(w1_sb.ap().rearrange("p (c m) -> p c m", c=8),
                            w1_d.ap().rearrange("(c p) m -> p c m", p=128)
                            ).then_inc(S["sQL"], 16)
                g.dma_start(w2_sb.ap().rearrange("p (c m) -> p c m", c=32),
                            w2_d.ap().rearrange("(c p) m -> p c m", p=128)
                            ).then_inc(S["sQL"], 16)
                g.dma_start(b1c[:, :], b1c_d[:, :]).then_inc(S["sQL"], 16)
                g.dma_start(b2b[:, :], b2b_d[:, :]).then_inc(S["sQL"], 16)
                g.dma_start(l1sb[:, :], l1sb_d[:, :]).then_inc(S["sQL"], 16)
                g.dma_start(l1bb[:, :], l1bb_d[:, :]).then_inc(S["sQL"], 16)
                g.dma_start(l2sb[:, :], l2sb_d[:, :]).then_inc(S["sQL"], 16)
                g.dma_start(l2bb[:, :], l2bb_d[:, :]).then_inc(S["sQL"], 16)
                g.wait_ge(S["sQL"], bq0 + 128)
                return None
            sched.append(("g", f_w))

            def f_eps(v):
                v.memset(eps2[:, 0:1], 1e-5)
                v.memset(eps2[:, 1:2], 1e-6)
                return None
            sched.append(("v", f_eps))

            def emit_ln_tm(sched, src, dst, epscol, scale_t, bias_t):
                nsub = 2

                def f1(v, src=src):
                    return v.reduce_sum(
                        stb[:, 0:2],
                        src.ap().rearrange("p (s d) -> p s d", s=nsub),
                        axis=AX)
                sched.append(("v", f1))

                def f2(a, src=src, dst=dst):
                    last = None
                    for s4 in range(nsub):
                        last = act(a, dst[:, s4 * D:(s4 + 1) * D],
                                   src[:, s4 * D:(s4 + 1) * D], AF.Square,
                                   accum_out=stb[:, 4 + s4:5 + s4])
                    return last
                sched.append(("a", f2))

                def f3(v):
                    v.tensor_scalar(stb[:, 2:4], stb[:, 0:2], 1.0 / D, None,
                                    ALU.mult)
                    v.tensor_scalar(stb[:, 6:8], stb[:, 4:6], 1.0 / D, None,
                                    ALU.mult)
                    v.tensor_mul(stb[:, 8:10], stb[:, 2:4], stb[:, 2:4])
                    v.memset(stb[:, 30:31], 0.0)
                    return v.tensor_sub(stb[:, 8:10], stb[:, 6:8],
                                        stb[:, 8:10])
                sched.append(("v", f3))

                def f4(a, epscol=epscol):
                    return act(a, stb[:, 10:12], stb[:, 8:10], AF.Sqrt,
                               bias=eps2[:, epscol:epscol + 1])
                sched.append(("a", f4))

                def f5(v, src=src, dst=dst, scale_t=scale_t, bias_t=bias_t):
                    v.reciprocal(stb[:, 12:14], stb[:, 10:12])
                    v.memset(stb[:, 30:31], 0.0)
                    last = None
                    for s4 in range(nsub):
                        last = v.tensor_scalar(dst[:, s4 * D:(s4 + 1) * D],
                                               src[:, s4 * D:(s4 + 1) * D],
                                               stb[:, 2 + s4:3 + s4],
                                               stb[:, 12 + s4:13 + s4],
                                               ALU.subtract, ALU.mult)
                    if scale_t is not None:
                        for s4 in range(nsub):
                            v.tensor_mul(dst[:, s4 * D:(s4 + 1) * D],
                                         dst[:, s4 * D:(s4 + 1) * D],
                                         scale_t[:, :])
                            last = v.tensor_add(dst[:, s4 * D:(s4 + 1) * D],
                                                dst[:, s4 * D:(s4 + 1) * D],
                                                bias_t[:, :])
                    return last
                sched.append(("v", f5))

            def emit_b(ttb):
                t0 = ttb * TTB

                def f_l(s, ttb=ttb, t0=t0):
                    s.dma_start(
                        xb_tm.ap().rearrange("p (s d) -> p s d", s=2),
                        x_d[t0:t0 + TTB, :].rearrange("(s p) d -> p s d",
                                                      p=128)
                    ).then_inc(S["sQL"], 16)
                    cadd("sQL", 16)
                    for s2 in range(2):
                        s.dma_start(wol[:, s2 * D:(s2 + 1) * D],
                                    wo_out_d[ttb * 2 + s2, :, :]
                                    ).then_inc(S["sQL"], 16)
                        s.dma_start(gml[:, s2 * D:(s2 + 1) * D],
                                    gmoe_d[ttb * 2 + s2, :, :]
                                    ).then_inc(S["sQL"], 16)
                        cadd("sQL", 32)
                    s.wait_ge(S["sQL"], cget("sQL"))
                    return None
                sched.append(("s", f_l))

                emit_ln_tm(sched, xb_tm, t1b, 0, None, None)

                def f_t1(v):
                    return v.tensor_add(t1b[:, :], t1b[:, :], wol[:, :])
                sched.append(("v", f_t1))

                emit_ln_tm(sched, t1b, xh_tm, 1, l1sb, l1bb)

                def f_cp(v):
                    return v.tensor_copy(xh_bf[:, :], xh_tm[:, :])
                sched.append(("v", f_cp))

                def f_tr(t):
                    last = None
                    for s4 in range(2):
                        for dc in range(DCN):
                            last = t.transpose(
                                trb_ps[:, dc * TTB + s4 * 128:
                                       dc * TTB + (s4 + 1) * 128],
                                xh_bf[:, s4 * D + dc * 128:
                                      s4 * D + (dc + 1) * 128],
                                ident[:, :])
                    return last
                sched.append(("t", f_tr))

                def f_tre(v):
                    return v.tensor_copy(xh_fm[:, :], trb_ps[:, :])
                sched.append(("v", f_tre))

                for fcp in range(16):
                    def f_w1(t, fcp=fcp):
                        last = None
                        for i in range(2):
                            fc = fcp * 2 + i
                            for kc in range(8):
                                last = t.matmul(
                                    p1[i][:, :],
                                    w1_sb[:, kc * FF + fc * 128:
                                          kc * FF + (fc + 1) * 128],
                                    xh_fm[:, kc * TTB:(kc + 1) * TTB],
                                    start=(kc == 0), stop=(kc == 7))
                        return last
                    sched.append(("t", f_w1))

                    def f_g1(a, fcp=fcp):
                        last = None
                        for i in range(2):
                            fc = fcp * 2 + i
                            last = act(a, hh_fm[:, fc * TTB:(fc + 1) * TTB],
                                       p1[i][:, :], AF.Gelu_apprx_tanh,
                                       bias=b1c[:, fc:fc + 1])
                        return last
                    sched.append(("a", f_g1))

                for s4 in range(2):
                    def f_w2(t, s4=s4):
                        last = None
                        for half in range(2):
                            for fc in range(32):
                                last = t.matmul(
                                    ps_big[:, s4 * D + half * 512:
                                           s4 * D + (half + 1) * 512],
                                    hh_fm[:, fc * TTB + s4 * 128:
                                          fc * TTB + (s4 + 1) * 128],
                                    w2_sb[:, fc * D + half * 512:
                                          fc * D + (half + 1) * 512],
                                    start=(fc == 0), stop=(fc == 31))
                        return last
                    sched.append(("t", f_w2))

                def f_res(v):
                    for s4 in range(2):
                        v.scalar_tensor_tensor(
                            wol[:, s4 * D:(s4 + 1) * D],
                            ps_big[:, s4 * D:(s4 + 1) * D], 1.0, b2b[:, :],
                            ALU.mult, ALU.add)
                    return v.tensor_add(wol[:, :], wol[:, :], xh_tm[:, :])
                sched.append(("v", f_res))

                emit_ln_tm(sched, wol, t1b, 1, l2sb, l2bb)

                def f_fin(v):
                    v.tensor_mul(t1b[:, :], t1b[:, :], gml[:, :])
                    return v.tensor_add(wol[:, :], xb_tm[:, :], t1b[:, :])
                sched.append(("v", f_fin))

                def f_out(g, t0=t0):
                    g.dma_start(
                        y_d[t0:t0 + TTB, :].rearrange("(s p) d -> p s d",
                                                      p=128),
                        wol.ap().rearrange("p (s d) -> p s d", s=2)
                    ).then_inc(S["sQO"], 16)
                    cadd("sQO", 16)
                    return None
                sched.append(("g", f_out))

            for ttb in range(NTTB):
                emit_b(ttb)

            with nc.Block() as blk:
                def body(eng_name):
                    def _(eng):
                        for i, (en, fn) in enumerate(sched):
                            if en != eng_name:
                                continue
                            eng.wait_ge(S["sB"], i)
                            last = fn(eng)
                            if last is not None:
                                last.then_inc(S["sB"], 1)
                            else:
                                eng.sem_inc(S["sB"], 1)
                    return _
                blk.sync(body("s"))
                blk.tensor(body("t"))
                blk.vector(body("v"))
                blk.scalar(body("a"))
                blk.gpsimd(body("g"))

        with nc.Block() as blk:
            @blk.sync
            def _(s):
                s.wait_ge(S["sQO"], cget("sQO"))
        if dbg:
            with nc.Block() as blk:
                @blk.gpsimd
                def _(g):
                    g.dma_start(dbg_stash.ap(), stash_d.ap()).then_inc(
                        S["sQO"], 16)
                    g.dma_start(dbg_z.ap(), z_d.ap()).then_inc(S["sQO"], 16)
                    g.dma_start(dbg_xnf.ap(), xnf_d.ap()).then_inc(
                        S["sQO"], 16)
                    g.dma_start(dbg_crow.ap(), crow_d.ap()).then_inc(
                        S["sQO"], 16)
                    g.dma_start(dbg_summ.ap(), summ_ag_d.ap()).then_inc(
                        S["sQO"], 16)
                    g.dma_start(dbg_wo.ap(), wo_out_d.ap()).then_inc(
                        S["sQO"], 16)
                    g.dma_start(dbg_gm.ap(), gmoe_d.ap()).then_inc(
                        S["sQO"], 16)
                    g.dma_start(dbg_mfm.ap(), mfm_d.ap()).then_inc(
                        S["sQO"], 16)
                    g.dma_start(dbg_hact.ap(), hact_d.ap()).then_inc(
                        S["sQO"], 16)
                    g.dma_start(dbg_comb[:, :], combine[:, :]).then_inc(
                        S["sQO"], 16)
                    g.dma_start(dbg_acc[:, :], acc1_ref[:, :]).then_inc(
                        S["sQO"], 16)
                    cadd("sQO", 176)
                    g.wait_ge(S["sQO"], cget("sQO"))

        for k, vv in C.items():
            assert vv < 60000, (k, vv)
    return nc


def kernel(**inputs):
    x = np.asarray(inputs["L0_attn_x"])
    bf = ml_dtypes.bfloat16

    def f32c(a):
        return np.ascontiguousarray(np.asarray(a), dtype=np.float32)

    wq = f32c(inputs["L0_attn_Wq"]); wk = f32c(inputs["L0_attn_Wk"])
    wv = f32c(inputs["L0_attn_Wv"]); wa = f32c(inputs["L0_attn_Wa"])
    wg = f32c(inputs["L0_attn_Wg"]); wo = f32c(inputs["L0_attn_Wo"])
    w1 = f32c(inputs["L0_attn_W1"]); w2 = f32c(inputs["L0_attn_W2"])
    b1 = f32c(inputs["L0_attn_b1"]); b2 = f32c(inputs["L0_attn_b2"])
    l1s = f32c(inputs["L0_attn_ln1_scale"]); l1b = f32c(inputs["L0_attn_ln1_bias"])
    l2s = f32c(inputs["L0_attn_ln2_scale"]); l2b = f32c(inputs["L0_attn_ln2_bias"])
    gns = f32c(inputs["L0_attn_gn_scale"]); gnb = f32c(inputs["L0_attn_gn_bias"])
    gate_w = f32c(inputs["L0_ffn_gate_w"])
    gu = f32c(inputs["L0_ffn_gate_up_w"])
    dw = f32c(inputs["L0_ffn_down_w"])

    guT = np.ascontiguousarray(gu.transpose(0, 2, 1))
    dwq = np.ascontiguousarray(
        dw.transpose(0, 2, 1).reshape(8, 32, 128, 1024))
    gwT = np.ascontiguousarray(gate_w.T)
    b1c = np.ascontiguousarray(b1.reshape(32, 128).T)
    b2b = np.ascontiguousarray(np.broadcast_to(b2, (128, D)))
    l1sb = np.ascontiguousarray(np.broadcast_to(l1s, (128, D)))
    l1bb = np.ascontiguousarray(np.broadcast_to(l1b, (128, D)))
    l2sb = np.ascontiguousarray(np.broadcast_to(l2s, (128, D)))
    l2bb = np.ascontiguousarray(np.broadcast_to(l2b, (128, D)))
    gnsr = gns.reshape(1, D)
    gnbr = gnb.reshape(1, D)
    ident = np.eye(128, dtype=bf)
    gmap = np.zeros((128, 4), np.float32)
    for p in range(128):
        gmap[p, p // 32] = 1.0

    in_maps = []
    for c in range(8):
        b = c // 4
        t0 = (c % 4) * T
        onehot = np.zeros((128, 8), np.float32)
        onehot[:, c] = 1.0
        bsel = np.zeros((128, 2), np.float32)
        bsel[:, b] = 1.0
        m = {
            "x": np.ascontiguousarray(x[b, t0:t0 + T, :], dtype=np.float32),
            "wq": wq, "wk": wk, "wv": wv, "wa": wa, "wg": wg, "wo": wo,
            "w1": w1, "w2": w2, "b1c": b1c, "b2b": b2b,
            "l1sb": l1sb, "l1bb": l1bb, "l2sb": l2sb, "l2bb": l2bb,
            "gnsr": gnsr, "gnbr": gnbr, "gwT": gwT, "guT": guT, "dwq": dwq,
            "ident": ident, "gmap": gmap, "onehot": onehot, "bsel": bsel,
        }
        in_maps.append(m)

    dbg = bool(inputs.get("_dbg", False))
    nc = build_nc(dbg=dbg)
    res = run_bass_kernel_spmd(nc, in_maps, core_ids=list(range(8)))
    out = np.empty((2, 16384, D), np.float32)
    for c in range(8):
        b = c // 4
        t0 = (c % 4) * T
        out[b, t0:t0 + T, :] = res.results[c]["y"]
    if dbg:
        return out, res
    return out


if __name__ == "__main__":
    print("building...")
    nc = build_nc()
    print("built ok; instructions:", nc.n_instructions())


# version 08
# speedup vs baseline: 1.5041x; 1.1623x over previous
import heapq
import numpy as np
import ml_dtypes
from contextlib import ExitStack

import concourse.bass as bass
import concourse.mybir as mybir
from concourse.bass_utils import run_bass_kernel_spmd

F32 = mybir.dt.float32
BF16 = mybir.dt.bfloat16
AF = mybir.ActivationFunctionType
ALU = mybir.AluOpType
AX = mybir.AxisListType.X

PI = float(np.pi)
MAGIC = 12582912.0
INV2PI = float(1.0 / (2.0 * np.pi))
TWOPI = float(2.0 * np.pi)

D = 1024
FF = 4096
E = 8
T = 4096
TTA = 512
NTT = T // TTA
TTB = 256
NTTB = T // TTB
DCN = 8
SUB = 64
NSUB = TTA // SUB
QT = 1024
NQT = 4
CNT_INV = float(1.0 / (16384.0 * 32.0))


def _act_setids():
    from concourse.hw_specs import get_activation_tables
    tabs = list(get_activation_tables("gen3").values())
    def find(*funcs):
        for i, fs in enumerate(tabs):
            if all(f in fs for f in funcs):
                return i
        raise ValueError(funcs)
    return {
        AF.Sqrt: find(AF.Sqrt, AF.Square, AF.Sign),
        AF.Sigmoid: find(AF.Sigmoid, AF.Arctan, AF.Sign),
        AF.Arctan: find(AF.Sigmoid, AF.Arctan, AF.Sign),
        AF.Ln: find(AF.Ln, AF.Exp),
        AF.Exp: find(AF.Ln, AF.Exp),
        AF.Sin: find(AF.Sin, AF.Silu),
        AF.Silu: find(AF.Sin, AF.Silu),
        AF.Gelu: find(AF.Gelu),
        AF.Gelu_apprx_tanh: find(AF.Gelu_apprx_tanh),
    }


_ACT_WILD = {AF.Square, AF.Sign, AF.Abs, AF.Copy, AF.Identity, AF.Relu,
             AF.Is_finite}


def build_nc(dbg=False):
    nc = bass.Bass("TRN2", target_bir_lowering=False)
    _setid = _act_setids()
    _att = {"cur": None}

    def _tload(a, sid):
        a.add_instruction(mybir.InstLoadActFuncSet(
            name=nc.get_next_instruction_name(),
            act_func_set_id=sid, ins=[], outs=[]))
        _att["cur"] = sid

    def act(a, out, in_, func, **kw):
        if func in _ACT_WILD:
            if _att["cur"] is None:
                _tload(a, _setid[AF.Sqrt])
        elif _setid[func] != _att["cur"]:
            _tload(a, _setid[func])
        return a.activation(out, in_, func, **kw)

    x_d = nc.dram_tensor("x", [T, D], F32, kind="ExternalInput")
    wq_d = nc.dram_tensor("wq", [D, D], F32, kind="ExternalInput")
    wk_d = nc.dram_tensor("wk", [D, D], F32, kind="ExternalInput")
    wv_d = nc.dram_tensor("wv", [D, D], F32, kind="ExternalInput")
    wa_d = nc.dram_tensor("wa", [D, 2 * D], F32, kind="ExternalInput")
    wg_d = nc.dram_tensor("wg", [D, D], F32, kind="ExternalInput")
    wo_d = nc.dram_tensor("wo", [D, D], F32, kind="ExternalInput")
    w1_d = nc.dram_tensor("w1", [D, FF], F32, kind="ExternalInput")
    w2_d = nc.dram_tensor("w2", [FF, D], F32, kind="ExternalInput")
    b1c_d = nc.dram_tensor("b1c", [128, 32], F32, kind="ExternalInput")
    b2b_d = nc.dram_tensor("b2b", [128, D], F32, kind="ExternalInput")
    l1sb_d = nc.dram_tensor("l1sb", [128, D], F32, kind="ExternalInput")
    l1bb_d = nc.dram_tensor("l1bb", [128, D], F32, kind="ExternalInput")
    l2sb_d = nc.dram_tensor("l2sb", [128, D], F32, kind="ExternalInput")
    l2bb_d = nc.dram_tensor("l2bb", [128, D], F32, kind="ExternalInput")
    gnsr_d = nc.dram_tensor("gnsr", [1, D], F32, kind="ExternalInput")
    gnbr_d = nc.dram_tensor("gnbr", [1, D], F32, kind="ExternalInput")
    gwT_d = nc.dram_tensor("gwT", [D, E], F32, kind="ExternalInput")
    guT_d = nc.dram_tensor("guT", [E, D, 2 * FF], F32, kind="ExternalInput")
    dwq_d = nc.dram_tensor("dwq", [E, 32, 128, D], F32, kind="ExternalInput")
    ident_d = nc.dram_tensor("ident", [128, 128], BF16, kind="ExternalInput")
    gmap_d = nc.dram_tensor("gmap", [128, 4], F32, kind="ExternalInput")
    onehot_d = nc.dram_tensor("onehot", [128, 8], F32, kind="ExternalInput")
    bsel_d = nc.dram_tensor("bsel", [128, 2], F32, kind="ExternalInput")

    y_d = nc.dram_tensor("y", [T, D], F32, kind="ExternalOutput")

    stash_d = nc.dram_tensor("stash", [9, DCN, 128, T], F32)
    z_d = nc.dram_tensor("zst", [DCN, 128, T], BF16)
    xnf_d = nc.dram_tensor("xnfd", [DCN, NQT, 128, QT], BF16)
    hact_d = nc.dram_tensor("hact", [2, 4, 8, 8, 128, 128], BF16)
    mfm_d = nc.dram_tensor("mfmd", [NTT, DCN, 128, TTA], BF16)
    gmoe_d = nc.dram_tensor("gmoe", [32, 128, D], BF16)
    wo_out_d = nc.dram_tensor("wo_out", [32, 128, D], F32)
    summ_in_d = nc.dram_tensor("summ_in", [128, 32], F32)
    summ_ag_d = nc.dram_tensor("summ_ag", [1024, 32], F32)
    stats_in_d = nc.dram_tensor("stats_in", [2, 128], F32)
    stats_out_d = nc.dram_tensor("stats_out", [2, 128], F32)
    crow_d = nc.dram_tensor("crow", [2, D], F32)

    if dbg:
        dbg_stash = nc.dram_tensor("dbg_stash", [9, DCN, 128, T], F32,
                                   kind="ExternalOutput")
        dbg_z = nc.dram_tensor("dbg_z", [DCN, 128, T], BF16,
                               kind="ExternalOutput")
        dbg_xnf = nc.dram_tensor("dbg_xnf", [DCN, NQT, 128, QT], BF16,
                                 kind="ExternalOutput")
        dbg_crow = nc.dram_tensor("dbg_crow", [2, D], F32,
                                  kind="ExternalOutput")
        dbg_summ = nc.dram_tensor("dbg_summ", [1024, 32], F32,
                                  kind="ExternalOutput")
        dbg_wo = nc.dram_tensor("dbg_wo", [32, 128, D], F32,
                                kind="ExternalOutput")
        dbg_gm = nc.dram_tensor("dbg_gm", [32, 128, D], BF16,
                                kind="ExternalOutput")
        dbg_mfm = nc.dram_tensor("dbg_mfm", [NTT, DCN, 128, TTA], BF16,
                                 kind="ExternalOutput")
        dbg_hact = nc.dram_tensor("dbg_hact", [2, 4, 8, 8, 128, 128], BF16,
                                  kind="ExternalOutput")
        dbg_xn = nc.dram_tensor("dbg_xn", [NTT, 128, 4 * D], BF16,
                                kind="ExternalOutput")
        dbg_st4 = nc.dram_tensor("dbg_st4", [NTT, 128, 24], F32,
                                 kind="ExternalOutput")
        dbg_comb = nc.dram_tensor("dbg_comb", [128, 256], F32,
                                  kind="ExternalOutput")
        dbg_acc = nc.dram_tensor("dbg_acc", [128, 8 * D], F32,
                                 kind="ExternalOutput")

    es = ExitStack()
    with es:
        def SEM(name):
            return es.enter_context(nc.semaphore(name))
        C = {}
        def cadd(k, n):
            C[k] = C.get(k, 0) + n
            return C[k]
        def cget(k):
            return C.get(k, 0)

        SN = ("sXL sXF sLN sXN sTRd sTR sTRe sPE0 sKVe sMG sSTa "
              "sPQ sQGe sRT sSTb sSTx "
              "sAC sSTsc sCC "
              "sGW sPE1 sACg sACu sGM sHS sDH sDWQ sPE2 sDV2 sSIL sGMS "
              "sWL sWPE sWEV sWST sZL "
              "sB sQL sQS sQO sP0").split()
        S = {n: SEM(n) for n in SN}

        def SB(scope, name, shape, dt):
            return scope.enter_context(nc.sbuf_tensor(name, shape, dt))

        mask64 = SB(es, "mask64", [128, TTA], F32)
        ident = SB(es, "identsb", [128, 128], BF16)
        gmap = SB(es, "gmapsb", [128, 4], F32)
        onehot = SB(es, "onehotsb", [128, 8], F32)
        bsel = SB(es, "bselsb", [128, 2], F32)
        combine = SB(es, "combine", [128, 32 * E], F32)
        summ = SB(es, "summ", [128, 32], F32)
        hg = SB(es, "hg", [128, 16], F32)
        stats_acc = SB(es, "stats_acc", [128, 32], F32)
        c1pp = SB(es, "c1pp", [128, 8], F32)
        c2pp = SB(es, "c2pp", [128, 8], F32)
        epsc = SB(es, "epsc", [128, 4], F32)

        with nc.Block() as blk:
            @blk.gpsimd
            def _(g):
                g.dma_start(ident[:, :], ident_d[:, :]).then_inc(S["sQL"], 16)
                g.dma_start(gmap[:, :], gmap_d[:, :]).then_inc(S["sQL"], 16)
                g.dma_start(onehot[:, :], onehot_d[:, :]).then_inc(S["sQL"], 16)
                g.dma_start(bsel[:, :], bsel_d[:, :]).then_inc(S["sQL"], 16)
                cadd("sQL", 64)
                g.wait_ge(S["sQL"], cget("sQL"))
                g.sem_inc(S["sP0"], 1)

            @blk.vector
            def _(v):
                v.memset(mask64[:, :], 1.0)
                v.memset(mask64.ap()[:, 0:TTA:SUB], 0.0)
                v.memset(summ[:, :], 0.0)
                v.memset(summ[:, 0:8], 1.0)
                v.memset(stats_acc[:, :], 0.0)
                v.memset(epsc[:, 0:1], 1e-5)
                v.memset(epsc[:, 1:2], 1e-6)
                v.memset(epsc[:, 2:3], 1e-30)
                v.sem_inc(S["sP0"], 1)

        with ExitStack() as e0a, ExitStack() as e0ap:
            wk_sb = SB(e0a, "wk_sb", [128, 8 * D], BF16)
            wv_sb = SB(e0a, "wv_sb", [128, 8 * D], BF16)
            wa_sb = SB(e0a, "wa_sb", [128, 8 * 2 * D], BF16)
            x_tm = SB(e0a, "x_tm", [128, 4 * D], F32)
            xn_tmb = [SB(e0a, f"xn_tmb{i}", [128, 4 * D], BF16)
                      for i in range(2)]
            st4 = SB(e0a, "st4", [128, 24], F32)
            lnscr = SB(e0a, "lnscr", [128, D], F32)
            xnf_t = [SB(e0a, f"xnf_t{i}", [128, DCN * TTA], BF16)
                     for i in range(2)]
            rk = SB(e0a, "rk", [128, TTA], F32)
            rkv = [SB(e0a, f"rkv{i}", [128, TTA], F32) for i in range(2)]
            rlg = [SB(e0a, f"rlg{i}", [128, TTA], F32) for i in range(2)]
            rth = [SB(e0a, f"rth{i}", [128, TTA], F32) for i in range(2)]
            mtp = SB(e0a, "mtp", [128, 6 * TTA], F32)
            tr_ps = [e0ap.enter_context(
                nc.psum_tensor(f"tr_ps{i}", [128, TTA], BF16))
                for i in range(2)]
            kvP = [e0ap.enter_context(
                nc.psum_tensor(f"kvP{i}", [128, TTA], F32))
                for i in range(2)]
            arP = [e0ap.enter_context(
                nc.psum_tensor(f"arP{i}", [128, 2 * TTA], F32))
                for i in range(2)]
            m0 = mtp.ap()[:, 0:TTA]
            m1 = mtp.ap()[:, TTA:2 * TTA]
            m2 = mtp.ap()[:, 2 * TTA:3 * TTA]
            m3 = mtp.ap()[:, 3 * TTA:4 * TTA]
            m4 = mtp.ap()[:, 4 * TTA:5 * TTA]
            m5 = mtp.ap()[:, 5 * TTA:6 * TTA]

            with nc.Block() as blk:
                @blk.gpsimd
                def _(g):
                    g.dma_start(wk_sb.ap().rearrange("p (c m) -> p c m", c=8),
                                wk_d.ap().rearrange("(c p) m -> p c m", p=128)
                                ).then_inc(S["sQL"], 16)
                    g.dma_start(wv_sb.ap().rearrange("p (c m) -> p c m", c=8),
                                wv_d.ap().rearrange("(c p) m -> p c m", p=128)
                                ).then_inc(S["sQL"], 16)
                    g.dma_start(wa_sb.ap().rearrange("p (c m) -> p c m", c=8),
                                wa_d.ap().rearrange("(c p) m -> p c m", p=128)
                                ).then_inc(S["sQL"], 16)
                    cadd("sQL", 48)
                    g.wait_ge(S["sQL"], cget("sQL"))

                @blk.sync
                def _(s):
                    for tt in range(NTT):
                        if tt > 0:
                            s.wait_ge(S["sXF"], tt)
                        s.dma_start(
                            x_tm.ap().rearrange("p (s d) -> p s d", s=4),
                            x_d[tt * TTA:(tt + 1) * TTA, :]
                            .rearrange("(s p) d -> p s d", p=128)
                        ).then_inc(S["sXL"], 16)
                        cadd("sXL", 16)
                        if dbg:
                            s.wait_ge(S["sXN"], tt + 1)
                            s.dma_start(dbg_xn[tt], xn_tmb[tt % 2][:, :]
                                        ).then_inc(S["sQS"], 16)
                            s.dma_start(dbg_st4[tt], st4[:, :]
                                        ).then_inc(S["sQS"], 16)
                            cadd("sQS", 32)
                        for dc in range(DCN):
                            i = tt * 8 + dc
                            s.wait_ge(S["sTRe"], i + 1)
                            s.dma_start(
                                xnf_d[dc, tt // 2, :,
                                      (tt % 2) * TTA:(tt % 2 + 1) * TTA],
                                xnf_t[tt % 2][:, dc * TTA:(dc + 1) * TTA]
                            ).then_inc(S["sSTx"], 16)
                            cadd("sSTx", 16)
                            s.wait_ge(S["sKVe"], i + 1)
                            s.dma_start(stash_d[6, dc, :,
                                                tt * TTA:(tt + 1) * TTA],
                                        rkv[i % 2][:, :]).then_inc(S["sSTa"], 16)
                            s.wait_ge(S["sMG"], 4 * i + 4)
                            s.dma_start(stash_d[7, dc, :,
                                                tt * TTA:(tt + 1) * TTA],
                                        rlg[i % 2][:, :]).then_inc(S["sSTa"], 16)
                            s.dma_start(stash_d[8, dc, :,
                                                tt * TTA:(tt + 1) * TTA],
                                        rth[i % 2][:, :]).then_inc(S["sSTa"], 16)
                            cadd("sSTa", 48)

                @blk.vector
                def _(v):
                    for tt in range(NTT):
                        if tt == 0:
                            v.wait_ge(S["sP0"], 1)
                        v.wait_ge(S["sLN"], 4 * tt)
                        v.wait_ge(S["sXL"], 16 * (tt + 1))
                        v.reduce_sum(
                            st4[:, 0:4],
                            x_tm.ap().rearrange("p (s d) -> p s d", s=4),
                            axis=AX).then_inc(S["sLN"], 1)
                        v.wait_ge(S["sLN"], 4 * tt + 2)
                        v.tensor_scalar(st4[:, 4:8], st4[:, 0:4], 1.0 / D,
                                        None, ALU.mult)
                        v.tensor_scalar(st4[:, 12:16], st4[:, 8:12], 1.0 / D,
                                        None, ALU.mult)
                        v.tensor_mul(st4[:, 16:20], st4[:, 4:8], st4[:, 4:8])
                        v.memset(st4[:, 22:23], 0.0)
                        v.tensor_sub(st4[:, 16:20], st4[:, 12:16],
                                     st4[:, 16:20]).then_inc(S["sLN"], 1)
                        v.wait_ge(S["sLN"], 4 * tt + 4)
                        v.reciprocal(st4[:, 12:16], st4[:, 20:24])
                        v.memset(st4[:, 22:23], 0.0)
                        if tt >= 2:
                            v.wait_ge(S["sTRd"], tt - 1)
                        last = None
                        for s4 in range(4):
                            last = v.tensor_scalar(
                                xn_tmb[tt % 2][:, s4 * D:(s4 + 1) * D],
                                x_tm[:, s4 * D:(s4 + 1) * D],
                                st4[:, 4 + s4:5 + s4], st4[:, 12 + s4:13 + s4],
                                ALU.subtract, ALU.mult)
                        last.then_inc(S["sXN"], 1)
                        v.sem_inc(S["sXF"], 1)
                        for dc in range(DCN):
                            i = tt * 8 + dc
                            v.wait_ge(S["sTR"], i + 1)
                            if tt >= 2:
                                v.wait_ge(S["sSTx"],
                                          16 * (8 * (tt - 2) + dc + 1))
                            ev = v.tensor_copy(
                                xnf_t[tt % 2][:, dc * TTA:(dc + 1) * TTA],
                                tr_ps[i % 2][:, :])
                            ev.then_inc(S["sTRe"], 1)
                            if dc == DCN - 1:
                                v.sem_inc(S["sTRd"], 1)
                        for dc in range(DCN):
                            i = tt * 8 + dc
                            v.wait_ge(S["sPE0"], i * 2 + 1)
                            if i >= 2:
                                v.wait_ge(S["sSTa"], 48 * (i - 1))
                            v.tensor_copy(rk[:, :], kvP[i % 2][:, :])
                            v.tensor_mul(rkv[i % 2][:, :], rk[:, :],
                                         kvP[(i + 1) % 2][:, :]
                                         ).then_inc(S["sKVe"], 1)
                            v.wait_ge(S["sMG"], 4 * i + 1)
                            are = arP[i % 2][:, 0:TTA]
                            aim = arP[i % 2][:, TTA:2 * TTA]
                            v.tensor_add(m2, m2, m3)
                            v.tensor_scalar(m0, m0, -1e-37, 1e-37,
                                            ALU.mult, ALU.add)
                            v.tensor_tensor(m4, are, m0, ALU.add)
                            v.reciprocal(m0, m4)
                            v.tensor_tensor(m3, aim, m0, ALU.mult
                                            ).then_inc(S["sMG"], 1)
                            v.wait_ge(S["sMG"], 4 * i + 3)
                            if i >= 2:
                                v.wait_ge(S["sSTa"], 48 * (i - 1))
                            v.tensor_scalar(m5, m5, -PI / 2, PI / 2,
                                            ALU.mult, ALU.add)
                            v.tensor_mul(m5, m5, m1)
                            v.tensor_add(rth[i % 2][:, :], m0, m5
                                         ).then_inc(S["sMG"], 1)

                @blk.scalar
                def _(a):
                    for tt in range(NTT):
                        if tt == 0:
                            a.wait_ge(S["sP0"], 2)
                        a.wait_ge(S["sLN"], 4 * tt + 1)
                        if tt >= 1:
                            a.wait_ge(S["sMG"], 4 * (8 * tt))
                        last = None
                        for s4 in range(4):
                            last = act(a, lnscr[:, :],
                                       x_tm[:, s4 * D:(s4 + 1) * D],
                                       AF.Square,
                                       accum_out=st4[:, 8 + s4:9 + s4])
                        last.then_inc(S["sLN"], 1)
                        a.wait_ge(S["sLN"], 4 * tt + 3)
                        act(a, st4[:, 20:24], st4[:, 16:20], AF.Sqrt,
                            bias=epsc[:, 0:1]).then_inc(S["sLN"], 1)
                        for dc in range(DCN):
                            i = tt * 8 + dc
                            are = arP[i % 2][:, 0:TTA]
                            aim = arP[i % 2][:, TTA:2 * TTA]
                            a.wait_ge(S["sPE0"], i * 2 + 2)
                            if i >= 1:
                                a.wait_ge(S["sMG"], 4 * i)
                            act(a, m0, are, AF.Sign)
                            act(a, m1, aim, AF.Sign)
                            act(a, m0, m0, AF.Square)
                            act(a, m2, are, AF.Square)
                            act(a, m3, aim, AF.Square).then_inc(S["sMG"], 1)
                            a.wait_ge(S["sMG"], 4 * i + 2)
                            if i >= 2:
                                a.wait_ge(S["sSTa"], 48 * (i - 1))
                            act(a, m2, m2, AF.Sqrt, bias=epsc[:, 2:3])
                            act(a, m5, m4, AF.Sign)
                            act(a, m0, m3, AF.Arctan)
                            act(a, m2, m2, AF.Sigmoid)
                            act(a, rlg[i % 2][:, :], m2, AF.Ln
                                ).then_inc(S["sMG"], 1)

                @blk.tensor
                def _(t):
                    t.wait_ge(S["sQL"], cget("sQL"))
                    for tt in range(NTT):
                        t.wait_ge(S["sXN"], tt + 1)
                        for dc in range(DCN):
                            i = tt * 8 + dc
                            if i >= 2:
                                t.wait_ge(S["sTRe"], i - 1)
                            last = None
                            for s4 in range(4):
                                last = t.transpose(
                                    tr_ps[i % 2][:, s4 * 128:(s4 + 1) * 128],
                                    xn_tmb[tt % 2][:, s4 * D + dc * 128:
                                                   s4 * D + (dc + 1) * 128],
                                    ident[:, :])
                            last.then_inc(S["sTR"], 1)
                        t.wait_ge(S["sTRe"], 8 * (tt + 1))
                        xs = xnf_t[tt % 2]
                        for dc in range(DCN):
                            i = tt * 8 + dc
                            if i >= 1:
                                t.wait_ge(S["sKVe"], i)
                            last = None
                            for (ps, wsb) in ((kvP[i % 2][:, :], wk_sb),
                                              (kvP[(i + 1) % 2][:, :], wv_sb)):
                                for kc in range(8):
                                    last = t.matmul(
                                        ps,
                                        wsb[:, kc * D + dc * 128:
                                            kc * D + dc * 128 + 128],
                                        xs[:, kc * TTA:(kc + 1) * TTA],
                                        start=(kc == 0), stop=(kc == 7))
                            last.then_inc(S["sPE0"], 1)
                            if i >= 2:
                                t.wait_ge(S["sMG"], 4 * (i - 1))
                            last = None
                            for (cps, col0) in ((arP[i % 2][:, 0:TTA],
                                                 dc * 128),
                                                (arP[i % 2][:, TTA:2 * TTA],
                                                 D + dc * 128)):
                                for kc in range(8):
                                    last = t.matmul(
                                        cps,
                                        wa_sb[:, kc * 2 * D + col0:
                                              kc * 2 * D + col0 + 128],
                                        xs[:, kc * TTA:(kc + 1) * TTA],
                                        start=(kc == 0), stop=(kc == 7))
                            last.then_inc(S["sPE0"], 1)

        with ExitStack() as e0b, ExitStack() as e0bp:
            wq_sb = SB(e0b, "wq_sb", [128, 8 * D], BF16)
            wg_sb = SB(e0b, "wg_sb", [128, 8 * D], BF16)
            gwt_sb = SB(e0b, "gwt_sb", [128, 8 * E], BF16)
            xnb_t = [SB(e0b, f"xnb_t{i}", [128, DCN * TTA], BF16)
                     for i in range(2)]
            rq = [SB(e0b, f"rq{i}", [128, TTA], F32) for i in range(2)]
            rsg = [SB(e0b, f"rsg{i}", [128, TTA], F32) for i in range(2)]
            rt = SB(e0b, "rt", [128, 64], F32)
            pq = [e0bp.enter_context(
                nc.psum_tensor(f"pqQ{i}", [128, TTA], F32)) for i in range(2)]
            pg = [e0bp.enter_context(
                nc.psum_tensor(f"pgQ{i}", [128, TTA], F32)) for i in range(2)]
            pr = e0bp.enter_context(nc.psum_tensor("prQ", [128, 16], F32))

            with nc.Block() as blk:
                @blk.gpsimd
                def _(g):
                    g.wait_ge(S["sPE0"], 128)
                    g.wait_ge(S["sMG"], 256)
                    g.wait_ge(S["sSTa"], 48 * 64)
                    g.wait_ge(S["sSTx"], 16 * 64)
                    g.dma_start(wq_sb.ap().rearrange("p (c m) -> p c m", c=8),
                                wq_d.ap().rearrange("(c p) m -> p c m", p=128)
                                ).then_inc(S["sQL"], 16)
                    g.dma_start(wg_sb.ap().rearrange("p (c m) -> p c m", c=8),
                                wg_d.ap().rearrange("(c p) m -> p c m", p=128)
                                ).then_inc(S["sQL"], 16)
                    g.dma_start(gwt_sb.ap().rearrange("p (c e) -> p c e", c=8),
                                gwT_d.ap().rearrange("(c p) e -> p c e", p=128)
                                ).then_inc(S["sQL"], 16)
                    cadd("sQL", 48)
                    g.wait_ge(S["sQL"], cget("sQL"))

                @blk.sync
                def _(s):
                    for tt in range(NTT):
                        s.wait_ge(S["sSTx"], 16 * 8 * (tt + 1))
                        if tt >= 2:
                            s.wait_ge(S["sPQ"], 8 * (tt - 1))
                            s.wait_ge(S["sRT"], 8 * (tt - 1))
                        s.dma_start(
                            xnb_t[tt % 2].ap().rearrange(
                                "p (c t) -> p c t", c=8),
                            xnf_d[:, tt // 2, :,
                                  (tt % 2) * TTA:(tt % 2 + 1) * TTA]
                            .rearrange("c p t -> p c t")
                        ).then_inc(S["sZL"], 16)
                        cadd("sZL", 16)
                        for dc in range(DCN):
                            i = tt * 8 + dc
                            s.wait_ge(S["sQGe"], i * 2 + 2)
                            s.dma_start(stash_d[4, dc, :,
                                                tt * TTA:(tt + 1) * TTA],
                                        rq[i % 2][:, :]).then_inc(S["sSTb"], 16)
                            s.dma_start(stash_d[5, dc, :,
                                                tt * TTA:(tt + 1) * TTA],
                                        rsg[i % 2][:, :]).then_inc(S["sSTb"], 16)
                            cadd("sSTb", 32)

                @blk.tensor
                def _(t):
                    t.wait_ge(S["sQL"], cget("sQL"))
                    for tt in range(NTT):
                        t.wait_ge(S["sZL"], 16 * (tt + 1))
                        xs = xnb_t[tt % 2]
                        for dc in range(DCN):
                            i = tt * 8 + dc
                            if i >= 2:
                                t.wait_ge(S["sQGe"], (i - 1) * 2)
                            last = None
                            for (ps, wsb) in ((pq[i % 2], wq_sb),
                                              (pg[i % 2], wg_sb)):
                                for kc in range(8):
                                    last = t.matmul(
                                        ps[:, :],
                                        wsb[:, kc * D + dc * 128:
                                            kc * D + dc * 128 + 128],
                                        xs[:, kc * TTA:(kc + 1) * TTA],
                                        start=(kc == 0), stop=(kc == 7))
                            last.then_inc(S["sPQ"], 1)
                        for sp in range(2):
                            t.wait_ge(S["sRT"], tt * 8 + sp * 4)
                            last = None
                            for k in range(2):
                                s4 = sp * 2 + k
                                for kc in range(8):
                                    last = t.matmul(
                                        pr[:, k * 8:(k + 1) * 8],
                                        xs[:, kc * TTA + s4 * 128:
                                           kc * TTA + (s4 + 1) * 128],
                                        gwt_sb[:, kc * E:(kc + 1) * E],
                                        start=(kc == 0), stop=(kc == 7))
                            last.then_inc(S["sRT"], 1)

                @blk.vector
                def _(v):
                    for tt in range(NTT):
                        for dc in range(DCN):
                            i = tt * 8 + dc
                            v.wait_ge(S["sPQ"], i + 1)
                            if i >= 2:
                                v.wait_ge(S["sSTb"], 32 * (i - 1))
                            v.tensor_copy(rq[i % 2][:, :], pq[i % 2][:, :]
                                          ).then_inc(S["sQGe"], 1)
                        for sp in range(2):
                            v.wait_ge(S["sRT"], tt * 8 + sp * 4 + 1)
                            v.reduce_max(rt[:, 0:1], pr[:, 0:8], axis=AX,
                                         negate=True)
                            v.reduce_max(rt[:, 32:33], pr[:, 8:16], axis=AX,
                                         negate=True).then_inc(S["sRT"], 1)
                            v.wait_ge(S["sRT"], tt * 8 + sp * 4 + 3)
                            def ops(B, g128):
                                yield lambda: v.reduce_sum(
                                    rt[:, B + 1:B + 2], rt[:, B + 8:B + 16],
                                    axis=AX)
                                yield lambda: v.reciprocal(
                                    rt[:, B + 2:B + 3], rt[:, B + 1:B + 2])
                                yield lambda: v.tensor_scalar(
                                    rt[:, B + 8:B + 16], rt[:, B + 8:B + 16],
                                    rt[:, B + 2:B + 3], None, ALU.mult)
                                yield lambda: v.reduce_max(
                                    rt[:, B + 3:B + 4], rt[:, B + 8:B + 16],
                                    axis=AX)
                                yield lambda: v.tensor_scalar(
                                    rt[:, B + 16:B + 24], rt[:, B + 8:B + 16],
                                    rt[:, B + 3:B + 4], None, ALU.is_ge)
                                yield lambda: v.tensor_scalar(
                                    rt[:, B + 24:B + 32], rt[:, B + 16:B + 24],
                                    -1.0, 1.0, ALU.mult, ALU.add)
                                yield lambda: v.tensor_mul(
                                    rt[:, B + 24:B + 32], rt[:, B + 8:B + 16],
                                    rt[:, B + 24:B + 32])
                                yield lambda: v.reduce_max(
                                    rt[:, B + 4:B + 5], rt[:, B + 24:B + 32],
                                    axis=AX)
                                yield lambda: v.tensor_scalar(
                                    rt[:, B + 24:B + 32], rt[:, B + 24:B + 32],
                                    rt[:, B + 4:B + 5], None, ALU.is_ge)
                                yield lambda: v.tensor_add(
                                    rt[:, B + 5:B + 6], rt[:, B + 3:B + 4],
                                    rt[:, B + 4:B + 5])
                                yield lambda: v.reciprocal(
                                    rt[:, B + 6:B + 7], rt[:, B + 5:B + 6])
                                yield lambda: v.tensor_mul(
                                    rt[:, B + 3:B + 4], rt[:, B + 3:B + 4],
                                    rt[:, B + 6:B + 7])
                                yield lambda: v.tensor_mul(
                                    rt[:, B + 4:B + 5], rt[:, B + 4:B + 5],
                                    rt[:, B + 6:B + 7])
                                yield lambda: v.tensor_scalar(
                                    rt[:, B + 16:B + 24], rt[:, B + 16:B + 24],
                                    rt[:, B + 3:B + 4], None, ALU.mult)
                                yield lambda: v.tensor_scalar(
                                    rt[:, B + 24:B + 32], rt[:, B + 24:B + 32],
                                    rt[:, B + 4:B + 5], None, ALU.mult)
                                yield lambda: v.tensor_add(
                                    combine[:, g128 * 8:(g128 + 1) * 8],
                                    rt[:, B + 16:B + 24], rt[:, B + 24:B + 32])
                            g0 = tt * 4 + sp * 2
                            it0 = ops(0, g0)
                            it1 = ops(32, g0 + 1)
                            last = None
                            for o0, o1 in zip(it0, it1):
                                last = o0()
                                last = o1()
                            last.then_inc(S["sRT"], 1)

                @blk.scalar
                def _(a):
                    for tt in range(NTT):
                        for dc in range(DCN):
                            i = tt * 8 + dc
                            a.wait_ge(S["sPQ"], i + 1)
                            if i >= 2:
                                a.wait_ge(S["sSTb"], 32 * (i - 1))
                            act(a, rsg[i % 2][:, :], pg[i % 2][:, :],
                                AF.Silu).then_inc(S["sQGe"], 1)
                        for sp in range(2):
                            a.wait_ge(S["sRT"], tt * 8 + sp * 4 + 2)
                            act(a, rt[:, 8:16], pr[:, 0:8], AF.Exp,
                                bias=rt[:, 0:1])
                            act(a, rt[:, 40:48], pr[:, 8:16], AF.Exp,
                                bias=rt[:, 32:33]).then_inc(S["sRT"], 1)

        streams = {k: [] for k in "tvasg"}
        seqn = [0]
        lastest = {k: 0.0 for k in "tvasg"}

        def ADD(engs, est, fn):
            streams[engs].append((est, seqn[0], fn))
            seqn[0] += 1

        with ExitStack() as big, ExitStack() as bigp:
            uscr = SB(big, "uscr", [128, 12 * TTA], F32)
            _sl = [uscr.ap()[:, i * TTA:(i + 1) * TTA] for i in range(12)]
            _map = {0: 0, 1: 1, 3: 2, 4: 3, 5: 4, 6: 5, 7: 6, 8: 7,
                    9: 8, 10: 9, 11: 10, 12: 11, 2: 3}
            U = [_sl[_map[i]] for i in range(13)]
            b_kv = SB(big, "b_kv", [128, TTA], F32)
            csm = SB(big, "csm", [128, 120], F32)
            zt = SB(big, "zt", [128, TTA], BF16)
            agp = uscr.ap()[:, 0:256]
            prr = SB(big, "prr", [128, 64], F32)
            pri = SB(big, "pri", [128, 64], F32)
            mtmp = SB(big, "mtmp", [128, 32], F32)
            selb = SB(big, "selb", [128, 64], F32)
            sg43 = SB(big, "sg43", [4, 4], F32)
            sg43b = SB(big, "sg43b", [4, 4], F32)
            row0 = SB(big, "row0", [1, 128], F32)
            row1 = SB(big, "row1", [1, 128], F32)
            grw = SB(big, "grw", [1, 256], F32)
            gnsr = uscr.ap()[0:1, 0:D]
            gnbr = uscr.ap()[0:1, D:2 * D]
            c1row = uscr.ap()[0:1, 2 * D:3 * D]
            c2row = uscr.ap()[0:1, 3 * D:4 * D]
            xnq = [SB(big, f"xnq{i}", [128, 8 * TTA], BF16) for i in range(3)]
            slab = [SB(big, f"slab{i}", [128, 2 * 8 * TTA], BF16)
                    for i in range(2)]
            dwq = [SB(big, f"dwq{i}", [128, 8 * D], BF16) for i in range(2)]
            acc = SB(big, "acc1", [128, 8 * D], F32)
            acc1_ref = acc
            hls = [SB(big, f"hls{i}", [128, 8 * 128], BF16) for i in range(3)]
            gel = [SB(big, f"gel{i}", [128, TTA], BF16) for i in range(2)]
            ups = [SB(big, f"ups{i}", [128, TTA], BF16) for i in range(2)]
            hstg = [SB(big, f"hstg{i}", [128, TTA], BF16) for i in range(2)]
            gms = [SB(big, "gms0", [128, D], BF16)]
            mfl = [SB(big, "mfl0", [128, DCN * 512], BF16)]
            woo = [SB(big, "woo0", [128, D], F32)]
            wo_sb = SB(big, "wo_sb", [128, 8 * D], BF16)
            zl = SB(big, "zlw", [128, TTA], BF16)
            sgl = SB(big, "sglw", [128, TTA], F32)
            mfb = [SB(big, "mfb0", [128, TTA], BF16)]
            pA = [bigp.enter_context(nc.psum_tensor(f"pA{i}", [128, TTA], F32))
                  for i in range(2)]
            pB = [bigp.enter_context(nc.psum_tensor(f"pB{i}", [128, TTA], F32))
                  for i in range(2)]
            pD = [bigp.enter_context(nc.psum_tensor(f"pD{i}", [128, TTA], F32))
                  for i in range(2)]
            pW = [bigp.enter_context(nc.psum_tensor(f"pW{i}", [128, TTA], F32))
                  for i in range(2)]
            ps43 = pW[1].ap()[0:4, 0:4]

            E0DONE = 960.0

            achain = []

            def ach(eng, cost, fn):
                achain.append((eng, cost, fn))

            def emit_scan(tt, dc):
                i = tt * 8 + dc
                t0 = tt * TTA

                def f_load(s, dc=dc, t0=t0, i=i):
                    if i == 0:
                        s.wait_ge(S["sPE0"], 128)
                        s.wait_ge(S["sMG"], 256)
                        s.wait_ge(S["sQGe"], 128)
                        s.wait_ge(S["sRT"], 64)
                        s.wait_ge(S["sPQ"], 64)
                        s.wait_ge(S["sSTb"], 32 * 64)
                    s.wait_ge(S["sSTa"], 48 * (i + 1))
                    s.dma_start(b_kv[:, :], stash_d[6, dc, :, t0:t0 + TTA]
                                ).then_inc(S["sQL"], 16)
                    s.dma_start(U[0], stash_d[7, dc, :, t0:t0 + TTA]
                                ).then_inc(S["sQL"], 16)
                    s.dma_start(U[5], stash_d[8, dc, :, t0:t0 + TTA]
                                ).then_inc(S["sQL"], 16)
                    cadd("sQL", 48)
                    s.wait_ge(S["sQL"], cget("sQL"))
                    return None
                ach("s", 2.5, f_load)

                def f_d4(v):
                    v.tensor_tensor_scan(U[1], mask64[:, :], U[0],
                                         0.0, ALU.mult, ALU.add)
                    v.tensor_tensor_scan(U[3], mask64[:, :], U[5],
                                         0.0, ALU.mult, ALU.add)
                    v.tensor_scalar(U[4], U[3], INV2PI, MAGIC,
                                    ALU.mult, ALU.add)
                    v.tensor_scalar(U[4], U[4], -MAGIC, None, ALU.add)
                    return v.scalar_tensor_tensor(U[4], U[4], -TWOPI, U[3],
                                                  ALU.mult, ALU.add)
                ach("v", 4.0, f_d4)

                def f_a5(a):
                    act(a, U[6], U[4], AF.Abs)
                    act(a, U[7], U[1], AF.Exp, scale=-1.0)
                    return act(a, U[1], U[1], AF.Exp)
                ach("a", 2.2, f_a5)

                def f_d5(v):
                    return v.tensor_scalar(U[6], U[6], -1.0,
                                           PI / 2, ALU.mult, ALU.add)
                ach("v", 0.8, f_d5)

                def f_a6(a):
                    act(a, U[4], U[4], AF.Sin)
                    return act(a, U[6], U[6], AF.Sin)
                ach("a", 1.6, f_a6)

                def f_d6(v, dc=dc):
                    c = csm
                    v.tensor_mul(U[3], U[7], b_kv[:, :])
                    v.tensor_mul(U[8], U[3], U[6])
                    v.tensor_mul(U[9], U[3], U[4])
                    v.reduce_sum(c[:, 0:8],
                                 U[8].rearrange("p (c j) -> p c j", c=8),
                                 axis=AX)
                    v.reduce_sum(c[:, 8:16],
                                 U[9].rearrange("p (c j) -> p c j", c=8),
                                 axis=AX)
                    eL63 = U[1][:, SUB - 1:TTA:SUB]
                    cos63 = U[6][:, SUB - 1:TTA:SUB]
                    sin63 = U[4][:, SUB - 1:TTA:SUB]
                    v.tensor_mul(c[:, 16:24], eL63, cos63)
                    v.tensor_mul(c[:, 24:32], eL63, sin63)
                    v.tensor_copy(c[:, 48:49], summ[:, 16 + dc:17 + dc])
                    v.tensor_copy(c[:, 57:58], summ[:, 24 + dc:25 + dc])
                    v.tensor_copy(c[:, 66:67], summ[:, 0 + dc:1 + dc])
                    v.tensor_copy(c[:, 75:76], summ[:, 8 + dc:9 + dc])
                    v.tensor_mul(c[:, 84:92], c[:, 16:24], c[:, 0:8])
                    v.tensor_mul(c[:, 32:40], c[:, 24:32], c[:, 8:16])
                    v.tensor_mul(c[:, 96:104], c[:, 24:32], c[:, 0:8])
                    v.tensor_mul(c[:, 40:48], c[:, 16:24], c[:, 8:16])
                    v.tensor_add(c[:, 32:40], c[:, 84:92], c[:, 32:40])
                    v.tensor_sub(c[:, 40:48], c[:, 96:104], c[:, 40:48])
                    for cc in range(NSUB):
                        Aer = c[:, 16 + cc:17 + cc]
                        Aei = c[:, 24 + cc:25 + cc]
                        Hr = c[:, 48 + cc:49 + cc]
                        Hi = c[:, 57 + cc:58 + cc]
                        Pr = c[:, 66 + cc:67 + cc]
                        Pi = c[:, 75 + cc:76 + cc]
                        p1 = c[:, 104:105]; p2 = c[:, 105:106]
                        p3 = c[:, 106:107]; p4 = c[:, 107:108]
                        a1 = c[:, 108:109]; a2 = c[:, 109:110]
                        b1 = c[:, 110:111]; b2 = c[:, 111:112]
                        tA = c[:, 112:113]; tB = c[:, 113:114]
                        v.tensor_mul(p1, Pr, Aer)
                        v.tensor_mul(p2, Pi, Aei)
                        v.tensor_mul(p3, Pr, Aei)
                        v.tensor_mul(p4, Pi, Aer)
                        v.tensor_mul(a1, Aer, Hr)
                        v.tensor_mul(a2, Aei, Hi)
                        v.tensor_mul(b1, Aer, Hi)
                        v.tensor_mul(b2, Aei, Hr)
                        v.tensor_sub(c[:, 67 + cc:68 + cc], p1, p2)
                        v.tensor_add(c[:, 76 + cc:77 + cc], p3, p4)
                        v.tensor_sub(tA, a1, a2)
                        v.tensor_add(tB, b1, b2)
                        v.tensor_add(c[:, 49 + cc:50 + cc], tA,
                                     c[:, 32 + cc:33 + cc])
                        v.tensor_add(c[:, 58 + cc:59 + cc], tB,
                                     c[:, 40 + cc:41 + cc])
                    v.tensor_copy(summ[:, 0 + dc:1 + dc], c[:, 74:75])
                    v.tensor_copy(summ[:, 8 + dc:9 + dc], c[:, 83:84])
                    v.tensor_copy(summ[:, 16 + dc:17 + dc], c[:, 56:57])
                    v.tensor_copy(summ[:, 24 + dc:25 + dc], c[:, 65:66])
                    j0r = U[8][:, 0:TTA:SUB]
                    j0i = U[9][:, 0:TTA:SUB]
                    v.tensor_add(j0r, j0r, c[:, 48:56])
                    v.tensor_sub(j0i, j0i, c[:, 57:65])
                    v.tensor_tensor_scan(U[7], mask64[:, :], U[8],
                                         0.0, ALU.mult, ALU.add)
                    v.tensor_tensor_scan(U[3], mask64[:, :], U[9],
                                         0.0, ALU.mult, ALU.add)
                    v.tensor_mul(U[10], U[1], U[6])
                    v.tensor_mul(U[11], U[1], U[4])
                    v.tensor_mul(U[0], U[10], U[7])
                    v.tensor_mul(U[5], U[11], U[3])
                    v.tensor_add(U[0], U[0], U[5])
                    v.tensor_mul(U[5], U[11], U[7])
                    v.tensor_mul(U[6], U[10], U[3])
                    v.tensor_sub(U[5], U[5], U[6])
                    APrb = c[:, 66:74].unsqueeze(2).broadcast_to(
                        (128, 8, SUB))
                    APib = c[:, 75:83].unsqueeze(2).broadcast_to(
                        (128, 8, SUB))
                    Ar3 = U[10].rearrange("p (c j) -> p c j", c=8)
                    Ai3 = U[11].rearrange("p (c j) -> p c j", c=8)
                    t4v = U[4].rearrange("p (c j) -> p c j", c=8)
                    t6v = U[6].rearrange("p (c j) -> p c j", c=8)
                    t12v = U[12].rearrange("p (c j) -> p c j", c=8)
                    v.tensor_tensor(t4v, Ar3, APrb, ALU.mult)
                    v.tensor_tensor(t6v, Ai3, APib, ALU.mult)
                    v.tensor_sub(U[4], U[4], U[6])
                    v.tensor_tensor(t6v, Ar3, APib, ALU.mult)
                    v.tensor_tensor(t12v, Ai3, APrb, ALU.mult)
                    return v.tensor_add(U[6], U[6], U[12])
                ach("v", 20.0, f_d6)

                def f_st(s, tt=tt, dc=dc):
                    t0 = tt * TTA
                    s.dma_start(stash_d[0, dc, :, t0:t0 + TTA],
                                U[0]).then_inc(S["sSTsc"], 16)
                    s.dma_start(stash_d[1, dc, :, t0:t0 + TTA],
                                U[5]).then_inc(S["sSTsc"], 16)
                    s.dma_start(stash_d[2, dc, :, t0:t0 + TTA],
                                U[4]).then_inc(S["sSTsc"], 16)
                    s.dma_start(stash_d[3, dc, :, t0:t0 + TTA],
                                U[6]).then_inc(S["sSTsc"], 16)
                    cadd("sSTsc", 64)
                    s.wait_ge(S["sSTsc"], cget("sSTsc"))
                    return None
                ach("s", 2.5, f_st)

            for tt in range(NTT):
                for dc in range(DCN):
                    emit_scan(tt, dc)

            x1_marker = len(achain)
            X1EST = 4800.0

            def x1_store(s):
                s.dma_start(summ_in_d[:, :], summ[:, :]).then_inc(S["sQS"], 16)
                cadd("sQS", 16)
                s.wait_ge(S["sQS"], cget("sQS"))
                return None
            ach("s", 1.0, x1_store)

            def x1_coll(g):
                g.collective_compute(
                    "AllGather", ALU.bypass,
                    replica_groups=[list(range(8))],
                    ins=[summ_in_d.ap().opt()],
                    outs=[summ_ag_d.ap().opt()]).then_inc(S["sCC"], 1)
                cadd("sCC", 1)
                g.wait_ge(S["sCC"], cget("sCC"))
                return None
            ach("g", 60.0, x1_coll)

            def x1_load(s):
                s.dma_start(
                    agp.rearrange("p (r c) -> p r c", r=8),
                    summ_ag_d.ap().rearrange("(r p) c -> p r c", p=128)
                ).then_inc(S["sQL"], 16)
                cadd("sQL", 16)
                s.wait_ge(S["sQL"], cget("sQL"))
                return None
            ach("s", 1.0, x1_load)

            def x1_pref(v):
                v.memset(prr[:, :], 0.0)
                v.memset(pri[:, :], 0.0)
                for r in range(1, 8):
                    if r == 4:
                        continue
                    co = (r - 1) * 32
                    def colA(dc):
                        return agp[:, co + dc:co + dc + 1]
                    def colAi(dc):
                        return agp[:, co + 8 + dc:co + 8 + dc + 1]
                    def colH(dc):
                        return agp[:, co + 16 + dc:co + 16 + dc + 1]
                    def colHi(dc):
                        return agp[:, co + 24 + dc:co + 24 + dc + 1]
                    for dc in range(DCN):
                        v.tensor_mul(mtmp[:, dc:dc + 1], colA(dc),
                                     prr[:, dc * 8 + r - 1:dc * 8 + r])
                    for dc in range(DCN):
                        v.tensor_mul(mtmp[:, 8 + dc:9 + dc], colAi(dc),
                                     pri[:, dc * 8 + r - 1:dc * 8 + r])
                    for dc in range(DCN):
                        v.tensor_mul(mtmp[:, 16 + dc:17 + dc], colA(dc),
                                     pri[:, dc * 8 + r - 1:dc * 8 + r])
                    for dc in range(DCN):
                        v.tensor_mul(mtmp[:, 24 + dc:25 + dc], colAi(dc),
                                     prr[:, dc * 8 + r - 1:dc * 8 + r])
                    for dc in range(DCN):
                        v.tensor_sub(mtmp[:, dc:dc + 1], mtmp[:, dc:dc + 1],
                                     mtmp[:, 8 + dc:9 + dc])
                    for dc in range(DCN):
                        v.tensor_add(mtmp[:, 16 + dc:17 + dc],
                                     mtmp[:, 16 + dc:17 + dc],
                                     mtmp[:, 24 + dc:25 + dc])
                    for dc in range(DCN):
                        v.tensor_add(prr[:, dc * 8 + r:dc * 8 + r + 1],
                                     mtmp[:, dc:dc + 1], colH(dc))
                    for dc in range(DCN):
                        v.tensor_add(pri[:, dc * 8 + r:dc * 8 + r + 1],
                                     mtmp[:, 16 + dc:17 + dc], colHi(dc))
                for dc in range(DCN):
                    v.tensor_mul(selb[:, dc * 8:(dc + 1) * 8],
                                 prr[:, dc * 8:(dc + 1) * 8], onehot[:, :])
                for dc in range(DCN):
                    v.reduce_sum(hg[:, 2 * dc:2 * dc + 1],
                                 selb[:, dc * 8:(dc + 1) * 8], axis=AX)
                for dc in range(DCN):
                    v.tensor_mul(selb[:, dc * 8:(dc + 1) * 8],
                                 pri[:, dc * 8:(dc + 1) * 8], onehot[:, :])
                last = None
                for dc in range(DCN):
                    last = v.reduce_sum(hg[:, 2 * dc + 1:2 * dc + 2],
                                        selb[:, dc * 8:(dc + 1) * 8], axis=AX)
                return last
            ach("v", 25.0, x1_pref)

            l_h0r = U[1]; l_h0i = U[2]; l_acr = U[3]
            l_aci = U[7]; l_q = U[8]; l_sg = U[9]
            w_1 = U[10]; w_2 = U[11]; w_3 = U[12]
            acc4 = csm.ap()[:, 0:4]

            def emit_a2(tt, dc):
                t0 = tt * TTA
                i = tt * 8 + dc

                def f_l(s, dc=dc, t0=t0, i=i):
                    if i == 0:
                        s.wait_ge(S["sSTsc"], cget("sSTsc"))
                        s.wait_ge(S["sSTb"], 32 * 64)
                    for qi, dst in ((0, l_h0r), (1, l_h0i), (2, l_acr),
                                    (3, l_aci), (4, l_q), (5, l_sg)):
                        s.dma_start(dst, stash_d[qi, dc, :, t0:t0 + TTA]
                                    ).then_inc(S["sQL"], 16)
                        cadd("sQL", 16)
                    s.wait_ge(S["sQL"], cget("sQL"))
                    return None
                ach("s", 4.0, f_l)

                def f_d(v, dc=dc):
                    hgr = hg[:, 2 * dc:2 * dc + 1]
                    hgi = hg[:, 2 * dc + 1:2 * dc + 2]
                    v.tensor_scalar(w_1, l_acr, hgr, None, ALU.mult)
                    v.tensor_scalar(w_2, l_aci, hgi, None, ALU.mult)
                    v.tensor_sub(w_1, w_1, w_2)
                    v.tensor_add(w_1, w_1, l_h0r)
                    v.tensor_scalar(w_2, l_acr, hgi, None, ALU.mult)
                    v.tensor_scalar(w_3, l_aci, hgr, None, ALU.mult)
                    v.tensor_add(w_2, w_2, w_3)
                    v.tensor_add(w_2, w_2, l_h0i)
                    v.tensor_mul(w_1, l_q, w_1)
                    v.tensor_mul(w_2, l_q, w_2)
                    return v.tensor_mul(zt[:, :], w_1, l_sg)
                ach("v", 8.0, f_d)

                def f_a(a):
                    act(a, w_3, w_1, AF.Copy, accum_out=acc4[:, 0:1])
                    act(a, w_3, w_2, AF.Copy, accum_out=acc4[:, 1:2])
                    act(a, w_3, w_1, AF.Square, accum_out=acc4[:, 2:3])
                    return act(a, w_3, w_2, AF.Square, accum_out=acc4[:, 3:4])
                ach("a", 3.0, f_a)

                def f_d2(v, dc=dc):
                    return v.tensor_add(stats_acc[:, dc * 4:(dc + 1) * 4],
                                        stats_acc[:, dc * 4:(dc + 1) * 4],
                                        acc4[:, :])
                ach("v", 0.4, f_d2)

                def f_s(s, dc=dc, t0=t0):
                    s.dma_start(z_d[dc, :, t0:t0 + TTA], zt[:, :]
                                ).then_inc(S["sQS"], 16)
                    cadd("sQS", 16)
                    s.wait_ge(S["sQS"], cget("sQS"))
                    return None
                ach("s", 1.2, f_s)

            for tt in range(NTT):
                for dc in range(DCN):
                    emit_a2(tt, dc)

            X2EST = 20000.0

            def x2_mm(dc):
                def f_t(t, dc=dc):
                    return t.matmul(ps43, gmap[:, :],
                                    stats_acc[:, dc * 4:(dc + 1) * 4],
                                    start=True, stop=True)
                ach("t", 0.3, f_t)

                def f_v(v):
                    v.tensor_scalar(sg43[:, :], ps43, bsel[0:4, 0:1], None,
                                    ALU.mult)
                    return v.tensor_scalar(sg43b[:, :], ps43, bsel[0:4, 1:2],
                                           None, ALU.mult)
                ach("v", 0.3, f_v)

                def f_s(s, dc=dc):
                    s.dma_start(
                        stats_in_d.ap()[0:1, dc * 16:(dc + 1) * 16]
                        .rearrange("o (a b) -> a (o b)", a=4),
                        sg43[:, :]).then_inc(S["sQS"], 16)
                    s.dma_start(
                        stats_in_d.ap()[1:2, dc * 16:(dc + 1) * 16]
                        .rearrange("o (a b) -> a (o b)", a=4),
                        sg43b[:, :]).then_inc(S["sQS"], 16)
                    cadd("sQS", 32)
                    s.wait_ge(S["sQS"], cget("sQS"))
                    return None
                ach("s", 0.6, f_s)

            achain.append(("v", 0.0, None))
            achain.pop()
            x2_start_marker = len(achain)
            for dc in range(DCN):
                x2_mm(dc)

            def x2_coll(g):
                g.collective_compute(
                    "AllReduce", ALU.add,
                    replica_groups=[list(range(8))],
                    ins=[stats_in_d.ap().opt()],
                    outs=[stats_out_d.ap().opt()]).then_inc(S["sCC"], 1)
                cadd("sCC", 1)
                g.wait_ge(S["sCC"], cget("sCC"))
                return None
            ach("g", 60.0, x2_coll)

            def x2_load(s):
                s.dma_start(row0[:, :], stats_out_d[0:1, :]
                            ).then_inc(S["sQL"], 16)
                s.dma_start(row1[:, :], stats_out_d[1:2, :]
                            ).then_inc(S["sQL"], 16)
                s.dma_start(gnsr[:, :], gnsr_d[:, :]).then_inc(S["sQL"], 16)
                s.dma_start(gnbr[:, :], gnbr_d[:, :]).then_inc(S["sQL"], 16)
                cadd("sQL", 64)
                s.wait_ge(S["sQL"], cget("sQL"))
                return None
            ach("s", 2.0, x2_load)

            def x2_v1(v):
                v.tensor_scalar(row0[:, :], row0[:, :], bsel[0:1, 0:1],
                                None, ALU.mult)
                v.tensor_scalar(row1[:, :], row1[:, :], bsel[0:1, 1:2],
                                None, ALU.mult)
                v.tensor_add(row0[:, :], row0[:, :], row1[:, :])
                r4 = row0.ap().rearrange("o (g j) -> o g j", g=32)
                v.tensor_tensor(grw.ap()[0:1, 160:192].unsqueeze(2),
                                r4[:, :, 2:3], r4[:, :, 3:4], ALU.add)
                v.tensor_scalar(grw.ap()[0:1, 0:32].unsqueeze(2),
                                r4[:, :, 0:1], CNT_INV, None, ALU.mult)
                v.tensor_scalar(grw.ap()[0:1, 32:64].unsqueeze(2),
                                r4[:, :, 1:2], CNT_INV, None, ALU.mult)
                v.tensor_scalar(grw[0:1, 64:96], grw[0:1, 160:192],
                                CNT_INV, None, ALU.mult)
                v.tensor_mul(grw[0:1, 96:128], grw[0:1, 0:32],
                             grw[0:1, 0:32])
                v.tensor_mul(grw[0:1, 128:160], grw[0:1, 32:64],
                             grw[0:1, 32:64])
                v.tensor_sub(grw[0:1, 64:96], grw[0:1, 64:96],
                             grw[0:1, 96:128])
                v.memset(grw[0:1, 224:225], 0.0)
                return v.tensor_sub(grw[0:1, 64:96], grw[0:1, 64:96],
                                    grw[0:1, 128:160])
            ach("v", 3.0, x2_v1)

            def x2_a1(a):
                return act(a, grw[0:1, 96:128], grw[0:1, 64:96], AF.Sqrt,
                           bias=epsc[0:1, 1:2])
            ach("a", 1.0, x2_a1)

            def x2_v2(v):
                v.reciprocal(grw[0:1, 192:224], grw[0:1, 96:128])
                rstd_b = grw.ap()[0:1, 192:224].unsqueeze(2).broadcast_to(
                    (1, 32, 32))
                mre_b = grw.ap()[0:1, 0:32].unsqueeze(2).broadcast_to(
                    (1, 32, 32))
                g3 = gnsr.rearrange("o (g j) -> o g j", g=32)
                c13 = c1row.rearrange("o (g j) -> o g j", g=32)
                c23 = c2row.rearrange("o (g j) -> o g j", g=32)
                v.memset(grw[0:1, 224:225], 0.0)
                v.tensor_tensor(c13, g3, rstd_b, ALU.mult)
                v.memset(grw[0:1, 225:226], 0.0)
                v.tensor_tensor(c23, c13, mre_b, ALU.mult)
                v.memset(grw[0:1, 226:227], 0.0)
                return v.tensor_sub(c2row[:, :], gnbr[:, :], c2row[:, :])
            ach("v", 3.0, x2_v2)

            def x2_st(s):
                s.dma_start(crow_d[0:1, :], c1row[:, :]).then_inc(S["sQS"], 16)
                s.dma_start(crow_d[1:2, :], c2row[:, :]).then_inc(S["sQS"], 16)
                cadd("sQS", 32)
                s.wait_ge(S["sQS"], cget("sQS"))
                for dc in range(DCN):
                    s.dma_start(c1pp[:, dc:dc + 1],
                                crow_d[0:1, dc * 128:(dc + 1) * 128]
                                .rearrange("o p -> p o")).then_inc(S["sQL"], 16)
                    s.dma_start(c2pp[:, dc:dc + 1],
                                crow_d[1:2, dc * 128:(dc + 1) * 128]
                                .rearrange("o p -> p o")).then_inc(S["sQL"], 16)
                    cadd("sQL", 32)
                s.wait_ge(S["sQL"], cget("sQL"))
                return None
            ach("s", 3.0, x2_st)

            def emit_wop(tt):
                t0 = tt * TTA

                def f_one(s, dc, t0=t0):
                    s.dma_start(zl[:, :], z_d[dc, :, t0:t0 + TTA]
                                ).then_inc(S["sQL"], 16)
                    s.dma_start(sgl[:, :], stash_d[5, dc, :, t0:t0 + TTA]
                                ).then_inc(S["sQL"], 16)
                    cadd("sQL", 32)
                    s.wait_ge(S["sQL"], cget("sQL"))
                    return None

                for dc in range(DCN):
                    def f_ld(s, dc=dc):
                        return f_one(s, dc)
                    ach("s", 1.5, f_ld)

                    def f_m(v, dc=dc):
                        v.tensor_scalar(U[1], zl[:, :],
                                        c1pp[:, dc:dc + 1], None, ALU.mult)
                        v.tensor_scalar(U[2], sgl[:, :],
                                        c2pp[:, dc:dc + 1], None, ALU.mult)
                        return v.tensor_add(mfb[0][:, :], U[1], U[2])
                    ach("v", 2.2, f_m)

                    def f_st2(s, tt=tt, dc=dc):
                        s.dma_start(mfm_d[tt, dc, :, :], mfb[0][:, :]
                                    ).then_inc(S["sQS"], 16)
                        cadd("sQS", 16)
                        s.wait_ge(S["sQS"], cget("sQS"))
                        return None
                    ach("s", 0.8, f_st2)

            for tt in range(NTT):
                emit_wop(tt)

            aidx = [0]
            acursor = [E0DONE]

            def chain_stage(eng, fn, idx):
                def run(engine, fn=fn, idx=idx):
                    engine.wait_ge(S["sAC"], idx)
                    last = fn(engine)
                    if last is not None:
                        last.then_inc(S["sAC"], 1)
                    else:
                        engine.sem_inc(S["sAC"], 1)
                return run

            for n, (eng, cost, fn) in enumerate(achain):
                if n == x1_marker:
                    acursor[0] = max(acursor[0], X1EST)
                if n == x2_start_marker:
                    acursor[0] = max(acursor[0], X2EST)
                acursor[0] += cost * 1.75
                ADD(eng, acursor[0], chain_stage(eng, fn, n))
            ACH_TOTAL = len(achain)

            E1T0 = 1000.0
            GUS = 2.3
            cnt_g = [0]
            cnt_sb = [0]
            cnt_hst = [0]
            cnt_hls = [0]
            cnt_dwq = [0]
            cnt_j2 = [0]
            cnt_xq = [0]
            pend_st = [None]
            est1 = [E1T0]

            pe_j2_at_hls = {}

            def gu_slab(tcb, e, fb):
                sb = cnt_sb[0]; cnt_sb[0] += 1
                est = est1[0] + (0 if sb < 2 else 0)

                def f_g(g, tcb=tcb, e=e, fb=fb, sb=sb):
                    if sb == 0:
                        g.wait_ge(S["sQGe"], 128)
                        g.wait_ge(S["sRT"], 64)
                    if sb >= 2:
                        g.wait_ge(S["sPE1"], 16 * (sb - 1))
                    g.dma_start(
                        slab[sb % 2].ap()[:, 0:8 * TTA]
                        .rearrange("p (c f) -> p c f", c=8),
                        guT_d[e, :, fb * TTA:(fb + 1) * TTA]
                        .rearrange("(c p) f -> p c f", p=128)
                        ).then_inc(S["sGW"], 16)
                    g.dma_start(
                        slab[sb % 2].ap()[:, 8 * TTA:16 * TTA]
                        .rearrange("p (c f) -> p c f", c=8),
                        guT_d[e, :, FF + fb * TTA:FF + (fb + 1) * TTA]
                        .rearrange("(c p) f -> p c f", p=128)
                        ).then_inc(S["sGW"], 16)
                    return None
                ADD("g", est - GUS * 16, f_g)
                return sb

            def gu_groups(tcb, e, fb, sb):
                for tc in range(2):
                    xq = tcb * 2 + tc
                    if xq >= cnt_xq[0]:
                        cnt_xq[0] = xq + 1

                        def f_xq(s, tcb=tcb, tc=tc, xq=xq):
                            if xq == 0:
                                s.wait_ge(S["sSTx"], 16 * 64)
                                s.wait_ge(S["sPQ"], 64)
                                s.wait_ge(S["sQGe"], 128)
                                s.wait_ge(S["sRT"], 64)
                            if xq >= 3:
                                s.wait_ge(S["sPE1"],
                                          1024 * ((xq - 3) // 2 + 1))
                            s.dma_start(
                                xnq[xq % 3].ap().rearrange(
                                    "p (c t) -> p c t", c=8),
                                xnf_d[:, tcb, :, tc * TTA:(tc + 1) * TTA]
                                .rearrange("c p t -> p c t")
                            ).then_inc(S["sZL"], 16)
                            cadd("sZL", 16)
                            return None
                        ADD("s", est1[0] - GUS * 20, f_xq)
                for tc in range(2):
                    xq = tcb * 2 + tc
                    for fc4 in range(4):
                        fc = fb * 4 + fc4
                        for gu in range(2):
                            gidx = cnt_g[0]; cnt_g[0] += 1
                            pair = gidx // 2
                            est1[0] += GUS

                            def f_t(t, sb=sb, tc=tc, fc4=fc4, gu=gu,
                                    gidx=gidx, pair=pair, xq=xq):
                                t.wait_ge(S["sGW"], 32 * (sb + 1))
                                t.wait_ge(S["sZL"],
                                          128 + 16 * (xq + 1))
                                ps = (pA if gu == 0 else pB)[pair % 2]
                                sem = S["sACg"] if gu == 0 else S["sACu"]
                                if pair >= 2:
                                    t.wait_ge(sem, pair - 1)
                                off = gu * 8 * TTA
                                last = None
                                for kc in range(8):
                                    last = t.matmul(
                                        ps[:, :],
                                        slab[sb % 2][:, off + kc * TTA +
                                                     fc4 * 128:
                                                     off + kc * TTA +
                                                     (fc4 + 1) * 128],
                                        xnq[xq % 3][:, kc * TTA:
                                                    (kc + 1) * TTA],
                                        start=(kc == 0), stop=(kc == 7))
                                last.then_inc(S["sPE1"], 1)
                                return None
                            ADD("t", est1[0], f_t)

                            def f_a(a, gu=gu, pair=pair, gidx=gidx):
                                a.wait_ge(S["sPE1"], gidx + 1)
                                if pair >= 2:
                                    a.wait_ge(S["sGM"], pair - 1)
                                if gu == 0:
                                    act(a, gel[pair % 2][:, :],
                                        pA[pair % 2][:, :],
                                        AF.Gelu).then_inc(S["sACg"], 1)
                                else:
                                    act(a, ups[pair % 2][:, :],
                                        pB[pair % 2][:, :],
                                        AF.Copy).then_inc(S["sACu"], 1)
                                return None
                            ADD("a", est1[0] + 0.3, f_a)

                        pair = (cnt_g[0] - 1) // 2
                        hst = cnt_hst[0]; cnt_hst[0] += 1

                        def f_gm(g, pair=pair, tcb=tcb, e=e, tc=tc, fc=fc,
                                 hst=hst):
                            g.wait_ge(S["sACg"], pair + 1)
                            g.wait_ge(S["sACu"], pair + 1)
                            if pair >= 2:
                                g.wait_ge(S["sHS"], 16 * (pair - 1))
                            g.tensor_mul(hstg[pair % 2][:, :],
                                         ups[pair % 2][:, :],
                                         gel[pair % 2][:, :]
                                         ).then_inc(S["sGM"], 1)
                            if pend_st[0] is not None:
                                pend_st[0](g)
                                pend_st[0] = None

                            def st(g2, pair=pair, e=e, tc=tc, fc=fc):
                                g2.wait_ge(S["sGM"], pair + 1)
                                g2.dma_start(
                                    hact_d[e % 2, fc // 8,
                                           tc * 4:(tc + 1) * 4,
                                           fc % 8, :, :]
                                    .rearrange("t8 p t -> p t8 t"),
                                    hstg[pair % 2].ap().rearrange(
                                        "p (t8 t) -> p t8 t", t8=4)
                                    ).then_inc(S["sHS"], 16)
                            pend_st[0] = st
                            return None
                        ADD("g", est1[0] + 0.6, f_gm)

            def down_pass(tcb, e, q):
                dq = cnt_dwq[0]; cnt_dwq[0] += 1

                def f_dw(g, e=e, q=q, dq=dq):
                    if pend_st[0] is not None:
                        pend_st[0](g)
                        pend_st[0] = None
                    if dq >= 2:
                        g.wait_ge(S["sPE2"], (dq - 1) * 16)
                    g.dma_start(
                        dwq[dq % 2].ap().rearrange("p (c h) -> p c h", c=8),
                        dwq_d[e, q * 8:(q + 1) * 8, :, :]
                        .rearrange("c p h -> p c h")
                        ).then_inc(S["sDWQ"], 16)
                    return None
                ADD("g", est1[0] - GUS * 16, f_dw)

                stneed = (tcb * 8 + e) * 64 + (q + 1) * 8 * 2

                for t8 in range(8):
                    hl = cnt_hls[0]; cnt_hls[0] += 1

                    def f_hl(s, tcb=tcb, e=e, q=q, t8=t8, hl=hl,
                             stneed=stneed):
                        if hl >= 3:
                            s.wait_ge(S["sPE2"], (hl - 2) * 2)
                        s.wait_ge(S["sHS"], 16 * stneed)
                        s.dma_start(
                            hls[hl % 3].ap().rearrange(
                                "p (c t) -> p c t", c=8),
                            hact_d[e % 2, q, t8, :, :, :]
                            .rearrange("c p t -> p c t")
                            ).then_inc(S["sDH"], 16)
                        return None
                    ADD("s", est1[0] - GUS * 5, f_hl)

                    est1[0] += 2.9
                    for h2 in range(2):
                        j2 = cnt_j2[0]; cnt_j2[0] += 1

                        def f_t(t, hl=hl, h2=h2, j2=j2, dq=dq):
                            t.wait_ge(S["sDH"], 16 * (hl + 1))
                            t.wait_ge(S["sDWQ"], 16 * (dq + 1))
                            if j2 >= 2:
                                t.wait_ge(S["sDV2"], j2 - 1)
                            last = None
                            for fcm in range(8):
                                last = t.matmul(
                                    pD[j2 % 2][:, :],
                                    hls[hl % 3][:, fcm * 128:(fcm + 1) * 128],
                                    dwq[dq % 2][:, fcm * D + h2 * TTA:
                                                fcm * D + (h2 + 1) * TTA],
                                    start=(fcm == 0), stop=(fcm == 7))
                            last.then_inc(S["sPE2"], 1)
                            return None
                        ADD("t", est1[0], f_t)

                        def f_v(v, tcb=tcb, e=e, q=q, t8=t8, h2=h2, j2=j2):
                            v.wait_ge(S["sPE2"], j2 + 1)
                            first = (e == 0 and q == 0)
                            cslice = combine[:, (tcb * 8 + t8) * 8 + e:
                                             (tcb * 8 + t8) * 8 + e + 1]
                            dst = acc[:, t8 * D + h2 * TTA:
                                      t8 * D + (h2 + 1) * TTA]
                            if first:
                                if tcb > 0:
                                    v.wait_ge(S["sSIL"], 8 * tcb)
                                v.tensor_scalar(dst, pD[j2 % 2][:, :],
                                                cslice, None, ALU.mult
                                                ).then_inc(S["sDV2"], 1)
                            else:
                                v.scalar_tensor_tensor(
                                    dst, pD[j2 % 2][:, :], cslice, dst,
                                    ALU.mult, ALU.add).then_inc(S["sDV2"], 1)
                            return None
                        ADD("v", est1[0] + 0.3, f_v)

            def quarter_silu(tcb):
                for t8 in range(8):
                    def f_sil(a, tcb=tcb, t8=t8):
                        if t8 == 0:
                            a.wait_ge(S["sDV2"], 512 * (tcb + 1))
                        if tcb * 8 + t8 >= 1:
                            a.wait_ge(S["sGMS"], 16 * (tcb * 8 + t8))
                        act(a, gms[0][:, :], acc[:, t8 * D:(t8 + 1) * D],
                            AF.Silu).then_inc(S["sSIL"], 1)
                        a.wait_ge(S["sSIL"], tcb * 8 + t8 + 1)
                        a.dma_start(gmoe_d[tcb * 8 + t8, :, :],
                                    gms[0][:, :]).then_inc(S["sGMS"], 16)
                        return None
                    ADD("a", est1[0] + 2.0 + t8 * 1.2, f_sil)

            def f_warm(t):
                for ps in (pA[0], pA[1], pB[0], pB[1], pD[0], pD[1],
                           pW[0], pW[1]):
                    for c4 in range(4):
                        t.matmul(ps[:, c4 * 128:(c4 + 1) * 128],
                                 ident[:, :], ident[:, :],
                                 start=True, stop=True)
                return None
            ADD("t", E0DONE + 2.0, f_warm)

            for tcb in range(NQT):
                for e in range(E):
                    sbs = []
                    plan = ["s0", "s1", "s2", "s3", "p0", "s4", "s5", "p1",
                            "s6", "s7", "p2", "p3"]
                    for item in plan:
                        k = int(item[1])
                        if item[0] == "s":
                            sb = gu_slab(tcb, e, k)
                            gu_groups(tcb, e, k, sb)
                            if k == 7:
                                def f_fl(g):
                                    if pend_st[0] is not None:
                                        pend_st[0](g)
                                        pend_st[0] = None
                                    return None
                                ADD("g", est1[0] + 0.7, f_fl)
                        else:
                            down_pass(tcb, e, k)
                quarter_silu(tcb)

            GU_TOTAL = cnt_g[0]
            J2_TOTAL = cnt_j2[0]

            woest = [22000.0]

            def f_wo_w(g):
                g.wait_ge(S["sQGe"], 128)
                g.wait_ge(S["sRT"], 64)
                g.dma_start(wo_sb.ap().rearrange("p (c m) -> p c m", c=8),
                            wo_d.ap().rearrange("(c p) m -> p c m", p=128)
                            ).then_inc(S["sWL"], 16)
                return None
            ADD("g", 1500.0, f_wo_w)

            cnt_wo = [0]
            for tt in range(NTT):
                def f_ml(s, tt=tt):
                    if tt == 0:
                        s.wait_ge(S["sAC"], ACH_TOTAL)
                    if tt >= 1:
                        s.wait_ge(S["sWPE"], tt * 8)
                    s.dma_start(
                        mfl[0].ap().rearrange("p (c t) -> p c t", c=8),
                        mfm_d[tt].rearrange("c p t -> p c t")
                    ).then_inc(S["sWL"], 16)
                    return None
                ADD("s", woest[0], f_ml)
                woest[0] += 4.0

                for s4 in range(4):
                    for half in range(2):
                        w8 = cnt_wo[0]; cnt_wo[0] += 1

                        def f_pe(t, tt=tt, s4=s4, half=half, w8=w8):
                            t.wait_ge(S["sWL"], 16 * (tt + 2))
                            if w8 >= 2:
                                t.wait_ge(S["sWEV"], w8 - 1)
                            last = None
                            for dc in range(DCN):
                                last = t.matmul(
                                    pW[w8 % 2][:, :],
                                    mfl[0][:, dc * 512 + s4 * 128:
                                           dc * 512 + (s4 + 1) * 128],
                                    wo_sb[:, dc * D + half * 512:
                                          dc * D + (half + 1) * 512],
                                    start=(dc == 0), stop=(dc == 7))
                            last.then_inc(S["sWPE"], 1)
                            return None
                        woest[0] += 2.4
                        ADD("t", woest[0], f_pe)

                        def f_ev(v, s4=s4, half=half, w8=w8):
                            v.wait_ge(S["sWPE"], w8 + 1)
                            if w8 >= 2:
                                v.wait_ge(S["sWST"], 16 * (w8 // 2))
                            v.tensor_copy(
                                woo[0][:, half * 512:(half + 1) * 512],
                                pW[w8 % 2][:, :]).then_inc(S["sWEV"], 1)
                            return None
                        ADD("v", woest[0] + 0.4, f_ev)

                    def f_st(s, tt=tt, s4=s4):
                        s.wait_ge(S["sWEV"], tt * 8 + (s4 + 1) * 2)
                        s.dma_start(wo_out_d[tt * 4 + s4, :, :],
                                    woo[0][:, :]).then_inc(S["sWST"], 16)
                        return None
                    ADD("s", woest[0] + 0.8, f_st)

            for k in streams:
                streams[k].sort(key=lambda x: (x[0], x[1]))

            with nc.Block() as blk:
                @blk.tensor
                def _(t):
                    for _, _, fn in streams["t"]:
                        fn(t)

                @blk.vector
                def _(v):
                    for _, _, fn in streams["v"]:
                        fn(v)

                @blk.scalar
                def _(a):
                    for _, _, fn in streams["a"]:
                        fn(a)

                @blk.sync
                def _(s):
                    for _, _, fn in streams["s"]:
                        fn(s)

                @blk.gpsimd
                def _(g):
                    for _, _, fn in streams["g"]:
                        fn(g)

        with ExitStack() as bs, ExitStack() as bsp:
            w1_sb = SB(bs, "w1_sb", [128, 8 * FF], BF16)
            w2_sb = SB(bs, "w2_sb", [128, 32 * D], BF16)
            b1c = SB(bs, "b1c_sb", [128, 32], F32)
            b2b = SB(bs, "b2b_sb", [128, D], BF16)
            l1sb = SB(bs, "l1sb_sb", [128, D], BF16)
            l1bb = SB(bs, "l1bb_sb", [128, D], BF16)
            l2sb = SB(bs, "l2sb_sb", [128, D], BF16)
            l2bb = SB(bs, "l2bb_sb", [128, D], BF16)
            xb_tm = SB(bs, "xb_tm", [128, 2 * D], F32)
            wol = SB(bs, "wol", [128, 2 * D], F32)
            gml = SB(bs, "gml", [128, 2 * D], BF16)
            t1b = SB(bs, "t1b", [128, 2 * D], F32)
            xh_tm = SB(bs, "xh_tm", [128, 2 * D], F32)
            xh_bf = SB(bs, "xh_bf", [128, 2 * D], BF16)
            xh_fm = SB(bs, "xh_fm", [128, 8 * TTB], BF16)
            hh_fm = SB(bs, "hh_fm", [128, 32 * TTB], BF16)
            stb = SB(bs, "stb", [128, 32], F32)
            eps2 = SB(bs, "eps2", [128, 2], F32)
            ps_big = bsp.enter_context(
                nc.psum_tensor("ps_big", [128, 2 * D], F32))
            trb_ps = bsp.enter_context(
                nc.psum_tensor("trb_ps", [128, 2 * D], BF16))
            p1 = [bsp.enter_context(
                nc.psum_tensor(f"p1_{i}", [128, TTB], F32)) for i in range(2)]

            sched = []
            bq0 = cget("sQL")
            cadd("sQL", 128)

            def f_w(g):
                g.wait_ge(S["sPE1"], GU_TOTAL)
                g.wait_ge(S["sPE2"], J2_TOTAL)
                g.wait_ge(S["sWPE"], 64)
                g.wait_ge(S["sSIL"], 32)
                g.wait_ge(S["sWST"], 16 * 32)
                g.wait_ge(S["sGMS"], 16 * 32)
                g.wait_ge(S["sHS"], 16 * 2048)
                g.wait_ge(S["sGM"], 2048)
                g.wait_ge(S["sAC"], ACH_TOTAL)
                g.dma_start(w1_sb.ap().rearrange("p (c m) -> p c m", c=8),
                            w1_d.ap().rearrange("(c p) m -> p c m", p=128)
                            ).then_inc(S["sQL"], 16)
                g.dma_start(w2_sb.ap().rearrange("p (c m) -> p c m", c=32),
                            w2_d.ap().rearrange("(c p) m -> p c m", p=128)
                            ).then_inc(S["sQL"], 16)
                g.dma_start(b1c[:, :], b1c_d[:, :]).then_inc(S["sQL"], 16)
                g.dma_start(b2b[:, :], b2b_d[:, :]).then_inc(S["sQL"], 16)
                g.dma_start(l1sb[:, :], l1sb_d[:, :]).then_inc(S["sQL"], 16)
                g.dma_start(l1bb[:, :], l1bb_d[:, :]).then_inc(S["sQL"], 16)
                g.dma_start(l2sb[:, :], l2sb_d[:, :]).then_inc(S["sQL"], 16)
                g.dma_start(l2bb[:, :], l2bb_d[:, :]).then_inc(S["sQL"], 16)
                g.wait_ge(S["sQL"], bq0 + 128)
                return None
            sched.append(("g", f_w))

            def f_eps(v):
                v.memset(eps2[:, 0:1], 1e-5)
                v.memset(eps2[:, 1:2], 1e-6)
                return None
            sched.append(("v", f_eps))

            def emit_ln_tm(sched, src, dst, epscol, scale_t, bias_t):
                nsub = 2

                def f1(v, src=src):
                    return v.reduce_sum(
                        stb[:, 0:2],
                        src.ap().rearrange("p (s d) -> p s d", s=nsub),
                        axis=AX)
                sched.append(("v", f1))

                def f2(a, src=src, dst=dst):
                    last = None
                    for s4 in range(nsub):
                        last = act(a, dst[:, s4 * D:(s4 + 1) * D],
                                   src[:, s4 * D:(s4 + 1) * D], AF.Square,
                                   accum_out=stb[:, 4 + s4:5 + s4])
                    return last
                sched.append(("a", f2))

                def f3(v):
                    v.tensor_scalar(stb[:, 2:4], stb[:, 0:2], 1.0 / D, None,
                                    ALU.mult)
                    v.tensor_scalar(stb[:, 6:8], stb[:, 4:6], 1.0 / D, None,
                                    ALU.mult)
                    v.tensor_mul(stb[:, 8:10], stb[:, 2:4], stb[:, 2:4])
                    v.memset(stb[:, 30:31], 0.0)
                    return v.tensor_sub(stb[:, 8:10], stb[:, 6:8],
                                        stb[:, 8:10])
                sched.append(("v", f3))

                def f4(a, epscol=epscol):
                    return act(a, stb[:, 10:12], stb[:, 8:10], AF.Sqrt,
                               bias=eps2[:, epscol:epscol + 1])
                sched.append(("a", f4))

                def f5(v, src=src, dst=dst, scale_t=scale_t, bias_t=bias_t):
                    v.reciprocal(stb[:, 12:14], stb[:, 10:12])
                    v.memset(stb[:, 30:31], 0.0)
                    last = None
                    for s4 in range(nsub):
                        last = v.tensor_scalar(dst[:, s4 * D:(s4 + 1) * D],
                                               src[:, s4 * D:(s4 + 1) * D],
                                               stb[:, 2 + s4:3 + s4],
                                               stb[:, 12 + s4:13 + s4],
                                               ALU.subtract, ALU.mult)
                    if scale_t is not None:
                        for s4 in range(nsub):
                            v.tensor_mul(dst[:, s4 * D:(s4 + 1) * D],
                                         dst[:, s4 * D:(s4 + 1) * D],
                                         scale_t[:, :])
                            last = v.tensor_add(dst[:, s4 * D:(s4 + 1) * D],
                                                dst[:, s4 * D:(s4 + 1) * D],
                                                bias_t[:, :])
                    return last
                sched.append(("v", f5))

            def emit_b(ttb):
                t0 = ttb * TTB

                def f_l(s, ttb=ttb, t0=t0):
                    s.dma_start(
                        xb_tm.ap().rearrange("p (s d) -> p s d", s=2),
                        x_d[t0:t0 + TTB, :].rearrange("(s p) d -> p s d",
                                                      p=128)
                    ).then_inc(S["sQL"], 16)
                    cadd("sQL", 16)
                    for s2 in range(2):
                        s.dma_start(wol[:, s2 * D:(s2 + 1) * D],
                                    wo_out_d[ttb * 2 + s2, :, :]
                                    ).then_inc(S["sQL"], 16)
                        s.dma_start(gml[:, s2 * D:(s2 + 1) * D],
                                    gmoe_d[ttb * 2 + s2, :, :]
                                    ).then_inc(S["sQL"], 16)
                        cadd("sQL", 32)
                    s.wait_ge(S["sQL"], cget("sQL"))
                    return None
                sched.append(("s", f_l))

                emit_ln_tm(sched, xb_tm, t1b, 0, None, None)

                def f_t1(v):
                    return v.tensor_add(t1b[:, :], t1b[:, :], wol[:, :])
                sched.append(("v", f_t1))

                emit_ln_tm(sched, t1b, xh_tm, 1, l1sb, l1bb)

                def f_cp(v):
                    return v.tensor_copy(xh_bf[:, :], xh_tm[:, :])
                sched.append(("v", f_cp))

                def f_tr(t):
                    last = None
                    for s4 in range(2):
                        for dc in range(DCN):
                            last = t.transpose(
                                trb_ps[:, dc * TTB + s4 * 128:
                                       dc * TTB + (s4 + 1) * 128],
                                xh_bf[:, s4 * D + dc * 128:
                                      s4 * D + (dc + 1) * 128],
                                ident[:, :])
                    return last
                sched.append(("t", f_tr))

                def f_tre(v):
                    return v.tensor_copy(xh_fm[:, :], trb_ps[:, :])
                sched.append(("v", f_tre))

                for fcp in range(16):
                    def f_w1(t, fcp=fcp):
                        last = None
                        for i in range(2):
                            fc = fcp * 2 + i
                            for kc in range(8):
                                last = t.matmul(
                                    p1[i][:, :],
                                    w1_sb[:, kc * FF + fc * 128:
                                          kc * FF + (fc + 1) * 128],
                                    xh_fm[:, kc * TTB:(kc + 1) * TTB],
                                    start=(kc == 0), stop=(kc == 7))
                        return last
                    sched.append(("t", f_w1))

                    def f_g1(a, fcp=fcp):
                        last = None
                        for i in range(2):
                            fc = fcp * 2 + i
                            last = act(a, hh_fm[:, fc * TTB:(fc + 1) * TTB],
                                       p1[i][:, :], AF.Gelu_apprx_tanh,
                                       bias=b1c[:, fc:fc + 1])
                        return last
                    sched.append(("a", f_g1))

                for s4 in range(2):
                    def f_w2(t, s4=s4):
                        last = None
                        for half in range(2):
                            for fc in range(32):
                                last = t.matmul(
                                    ps_big[:, s4 * D + half * 512:
                                           s4 * D + (half + 1) * 512],
                                    hh_fm[:, fc * TTB + s4 * 128:
                                          fc * TTB + (s4 + 1) * 128],
                                    w2_sb[:, fc * D + half * 512:
                                          fc * D + (half + 1) * 512],
                                    start=(fc == 0), stop=(fc == 31))
                        return last
                    sched.append(("t", f_w2))

                def f_res(v):
                    for s4 in range(2):
                        v.scalar_tensor_tensor(
                            wol[:, s4 * D:(s4 + 1) * D],
                            ps_big[:, s4 * D:(s4 + 1) * D], 1.0, b2b[:, :],
                            ALU.mult, ALU.add)
                    return v.tensor_add(wol[:, :], wol[:, :], xh_tm[:, :])
                sched.append(("v", f_res))

                emit_ln_tm(sched, wol, t1b, 1, l2sb, l2bb)

                def f_fin(v):
                    v.tensor_mul(t1b[:, :], t1b[:, :], gml[:, :])
                    return v.tensor_add(wol[:, :], xb_tm[:, :], t1b[:, :])
                sched.append(("v", f_fin))

                def f_out(g, t0=t0):
                    g.dma_start(
                        y_d[t0:t0 + TTB, :].rearrange("(s p) d -> p s d",
                                                      p=128),
                        wol.ap().rearrange("p (s d) -> p s d", s=2)
                    ).then_inc(S["sQO"], 16)
                    cadd("sQO", 16)
                    return None
                sched.append(("g", f_out))

            for ttb in range(NTTB):
                emit_b(ttb)

            with nc.Block() as blk:
                def body(eng_name):
                    def _(eng):
                        for i, (en, fn) in enumerate(sched):
                            if en != eng_name:
                                continue
                            eng.wait_ge(S["sB"], i)
                            last = fn(eng)
                            if last is not None:
                                last.then_inc(S["sB"], 1)
                            else:
                                eng.sem_inc(S["sB"], 1)
                    return _
                blk.sync(body("s"))
                blk.tensor(body("t"))
                blk.vector(body("v"))
                blk.scalar(body("a"))
                blk.gpsimd(body("g"))

        with nc.Block() as blk:
            @blk.sync
            def _(s):
                s.wait_ge(S["sQO"], cget("sQO"))
        if dbg:
            with nc.Block() as blk:
                @blk.gpsimd
                def _(g):
                    g.dma_start(dbg_stash.ap(), stash_d.ap()).then_inc(
                        S["sQO"], 16)
                    g.dma_start(dbg_z.ap(), z_d.ap()).then_inc(S["sQO"], 16)
                    g.dma_start(dbg_xnf.ap(), xnf_d.ap()).then_inc(
                        S["sQO"], 16)
                    g.dma_start(dbg_crow.ap(), crow_d.ap()).then_inc(
                        S["sQO"], 16)
                    g.dma_start(dbg_summ.ap(), summ_ag_d.ap()).then_inc(
                        S["sQO"], 16)
                    g.dma_start(dbg_wo.ap(), wo_out_d.ap()).then_inc(
                        S["sQO"], 16)
                    g.dma_start(dbg_gm.ap(), gmoe_d.ap()).then_inc(
                        S["sQO"], 16)
                    g.dma_start(dbg_mfm.ap(), mfm_d.ap()).then_inc(
                        S["sQO"], 16)
                    g.dma_start(dbg_hact.ap(), hact_d.ap()).then_inc(
                        S["sQO"], 16)
                    g.dma_start(dbg_comb[:, :], combine[:, :]).then_inc(
                        S["sQO"], 16)
                    g.dma_start(dbg_acc[:, :], acc1_ref[:, :]).then_inc(
                        S["sQO"], 16)
                    cadd("sQO", 176)
                    g.wait_ge(S["sQO"], cget("sQO"))

        for k, vv in C.items():
            assert vv < 60000, (k, vv)
    return nc


def kernel(**inputs):
    x = np.asarray(inputs["L0_attn_x"])
    bf = ml_dtypes.bfloat16

    def f32c(a):
        return np.ascontiguousarray(np.asarray(a), dtype=np.float32)

    wq = f32c(inputs["L0_attn_Wq"]); wk = f32c(inputs["L0_attn_Wk"])
    wv = f32c(inputs["L0_attn_Wv"]); wa = f32c(inputs["L0_attn_Wa"])
    wg = f32c(inputs["L0_attn_Wg"]); wo = f32c(inputs["L0_attn_Wo"])
    w1 = f32c(inputs["L0_attn_W1"]); w2 = f32c(inputs["L0_attn_W2"])
    b1 = f32c(inputs["L0_attn_b1"]); b2 = f32c(inputs["L0_attn_b2"])
    l1s = f32c(inputs["L0_attn_ln1_scale"]); l1b = f32c(inputs["L0_attn_ln1_bias"])
    l2s = f32c(inputs["L0_attn_ln2_scale"]); l2b = f32c(inputs["L0_attn_ln2_bias"])
    gns = f32c(inputs["L0_attn_gn_scale"]); gnb = f32c(inputs["L0_attn_gn_bias"])
    gate_w = f32c(inputs["L0_ffn_gate_w"])
    gu = f32c(inputs["L0_ffn_gate_up_w"])
    dw = f32c(inputs["L0_ffn_down_w"])

    guT = np.ascontiguousarray(gu.transpose(0, 2, 1))
    dwq = np.ascontiguousarray(
        dw.transpose(0, 2, 1).reshape(8, 32, 128, 1024))
    gwT = np.ascontiguousarray(gate_w.T)
    b1c = np.ascontiguousarray(b1.reshape(32, 128).T)
    b2b = np.ascontiguousarray(np.broadcast_to(b2, (128, D)))
    l1sb = np.ascontiguousarray(np.broadcast_to(l1s, (128, D)))
    l1bb = np.ascontiguousarray(np.broadcast_to(l1b, (128, D)))
    l2sb = np.ascontiguousarray(np.broadcast_to(l2s, (128, D)))
    l2bb = np.ascontiguousarray(np.broadcast_to(l2b, (128, D)))
    gnsr = gns.reshape(1, D)
    gnbr = gnb.reshape(1, D)
    ident = np.eye(128, dtype=bf)
    gmap = np.zeros((128, 4), np.float32)
    for p in range(128):
        gmap[p, p // 32] = 1.0

    in_maps = []
    for c in range(8):
        b = c // 4
        t0 = (c % 4) * T
        onehot = np.zeros((128, 8), np.float32)
        onehot[:, c] = 1.0
        bsel = np.zeros((128, 2), np.float32)
        bsel[:, b] = 1.0
        m = {
            "x": np.ascontiguousarray(x[b, t0:t0 + T, :], dtype=np.float32),
            "wq": wq, "wk": wk, "wv": wv, "wa": wa, "wg": wg, "wo": wo,
            "w1": w1, "w2": w2, "b1c": b1c, "b2b": b2b,
            "l1sb": l1sb, "l1bb": l1bb, "l2sb": l2sb, "l2bb": l2bb,
            "gnsr": gnsr, "gnbr": gnbr, "gwT": gwT, "guT": guT, "dwq": dwq,
            "ident": ident, "gmap": gmap, "onehot": onehot, "bsel": bsel,
        }
        in_maps.append(m)

    dbg = bool(inputs.get("_dbg", False))
    nc = build_nc(dbg=dbg)
    res = run_bass_kernel_spmd(nc, in_maps, core_ids=list(range(8)))
    out = np.empty((2, 16384, D), np.float32)
    for c in range(8):
        b = c // 4
        t0 = (c % 4) * T
        out[b, t0:t0 + T, :] = res.results[c]["y"]
    if dbg:
        return out, res
    return out


if __name__ == "__main__":
    print("building...")
    nc = build_nc()
    print("built ok; instructions:", nc.n_instructions())


# version 09 (final)
# speedup vs baseline: 1.5207x; 1.0111x over previous
import heapq
import numpy as np
import ml_dtypes
from contextlib import ExitStack

import concourse.bass as bass
import concourse.mybir as mybir
from concourse.bass_utils import run_bass_kernel_spmd

F32 = mybir.dt.float32
BF16 = mybir.dt.bfloat16
AF = mybir.ActivationFunctionType
ALU = mybir.AluOpType
AX = mybir.AxisListType.X

PI = float(np.pi)
MAGIC = 12582912.0
INV2PI = float(1.0 / (2.0 * np.pi))
TWOPI = float(2.0 * np.pi)

D = 1024
FF = 4096
E = 8
T = 4096
TTA = 512
NTT = T // TTA
TTB = 256
NTTB = T // TTB
DCN = 8
SUB = 64
NSUB = TTA // SUB
QT = 1024
NQT = 4
CNT_INV = float(1.0 / (16384.0 * 32.0))


def _act_setids():
    from concourse.hw_specs import get_activation_tables
    tabs = list(get_activation_tables("gen3").values())
    def find(*funcs):
        for i, fs in enumerate(tabs):
            if all(f in fs for f in funcs):
                return i
        raise ValueError(funcs)
    return {
        AF.Sqrt: find(AF.Sqrt, AF.Square, AF.Sign),
        AF.Sigmoid: find(AF.Sigmoid, AF.Arctan, AF.Sign),
        AF.Arctan: find(AF.Sigmoid, AF.Arctan, AF.Sign),
        AF.Ln: find(AF.Ln, AF.Exp),
        AF.Exp: find(AF.Ln, AF.Exp),
        AF.Sin: find(AF.Sin, AF.Silu),
        AF.Silu: find(AF.Sin, AF.Silu),
        AF.Gelu: find(AF.Gelu),
        AF.Gelu_apprx_tanh: find(AF.Gelu_apprx_tanh),
    }


_ACT_WILD = {AF.Square, AF.Sign, AF.Abs, AF.Copy, AF.Identity, AF.Relu,
             AF.Is_finite}


def build_nc(dbg=False):
    nc = bass.Bass("TRN2", target_bir_lowering=False)
    _setid = _act_setids()
    _att = {"cur": None}

    def _tload(a, sid):
        a.add_instruction(mybir.InstLoadActFuncSet(
            name=nc.get_next_instruction_name(),
            act_func_set_id=sid, ins=[], outs=[]))
        _att["cur"] = sid

    def act(a, out, in_, func, **kw):
        if func in _ACT_WILD:
            if _att["cur"] is None:
                _tload(a, _setid[AF.Sqrt])
        elif _setid[func] != _att["cur"]:
            _tload(a, _setid[func])
        return a.activation(out, in_, func, **kw)

    x_d = nc.dram_tensor("x", [T, D], F32, kind="ExternalInput")
    wq_d = nc.dram_tensor("wq", [D, D], F32, kind="ExternalInput")
    wk_d = nc.dram_tensor("wk", [D, D], F32, kind="ExternalInput")
    wv_d = nc.dram_tensor("wv", [D, D], F32, kind="ExternalInput")
    wa_d = nc.dram_tensor("wa", [D, 2 * D], F32, kind="ExternalInput")
    wg_d = nc.dram_tensor("wg", [D, D], F32, kind="ExternalInput")
    wo_d = nc.dram_tensor("wo", [D, D], F32, kind="ExternalInput")
    w1_d = nc.dram_tensor("w1", [D, FF], F32, kind="ExternalInput")
    w2_d = nc.dram_tensor("w2", [FF, D], F32, kind="ExternalInput")
    b1c_d = nc.dram_tensor("b1c", [128, 32], F32, kind="ExternalInput")
    b2b_d = nc.dram_tensor("b2b", [128, D], F32, kind="ExternalInput")
    l1sb_d = nc.dram_tensor("l1sb", [128, D], F32, kind="ExternalInput")
    l1bb_d = nc.dram_tensor("l1bb", [128, D], F32, kind="ExternalInput")
    l2sb_d = nc.dram_tensor("l2sb", [128, D], F32, kind="ExternalInput")
    l2bb_d = nc.dram_tensor("l2bb", [128, D], F32, kind="ExternalInput")
    gnsr_d = nc.dram_tensor("gnsr", [1, D], F32, kind="ExternalInput")
    gnbr_d = nc.dram_tensor("gnbr", [1, D], F32, kind="ExternalInput")
    gwT_d = nc.dram_tensor("gwT", [D, E], F32, kind="ExternalInput")
    guT_d = nc.dram_tensor("guT", [E, D, 2 * FF], F32, kind="ExternalInput")
    dwq_d = nc.dram_tensor("dwq", [E, 32, 128, D], F32, kind="ExternalInput")
    ident_d = nc.dram_tensor("ident", [128, 128], BF16, kind="ExternalInput")
    gmap_d = nc.dram_tensor("gmap", [128, 4], F32, kind="ExternalInput")
    onehot_d = nc.dram_tensor("onehot", [128, 8], F32, kind="ExternalInput")
    bsel_d = nc.dram_tensor("bsel", [128, 2], F32, kind="ExternalInput")

    y_d = nc.dram_tensor("y", [T, D], F32, kind="ExternalOutput")

    stash_d = nc.dram_tensor("stash", [9, DCN, 128, T], F32)
    z_d = nc.dram_tensor("zst", [DCN, 128, T], BF16)
    xnf_d = nc.dram_tensor("xnfd", [DCN, NQT, 128, QT], BF16)
    hact_d = nc.dram_tensor("hact", [2, 4, 8, 8, 128, 128], BF16)
    mfm_d = nc.dram_tensor("mfmd", [NTT, DCN, 128, TTA], BF16)
    gmoe_d = nc.dram_tensor("gmoe", [32, 128, D], BF16)
    wo_out_d = nc.dram_tensor("wo_out", [32, 128, D], F32)
    summ_in_d = nc.dram_tensor("summ_in", [128, 32], F32)
    summ_ag_d = nc.dram_tensor("summ_ag", [1024, 32], F32)
    stats_in_d = nc.dram_tensor("stats_in", [2, 128], F32)
    stats_out_d = nc.dram_tensor("stats_out", [2, 128], F32)
    crow_d = nc.dram_tensor("crow", [2, D], F32)

    if dbg:
        dbg_stash = nc.dram_tensor("dbg_stash", [9, DCN, 128, T], F32,
                                   kind="ExternalOutput")
        dbg_z = nc.dram_tensor("dbg_z", [DCN, 128, T], BF16,
                               kind="ExternalOutput")
        dbg_xnf = nc.dram_tensor("dbg_xnf", [DCN, NQT, 128, QT], BF16,
                                 kind="ExternalOutput")
        dbg_crow = nc.dram_tensor("dbg_crow", [2, D], F32,
                                  kind="ExternalOutput")
        dbg_summ = nc.dram_tensor("dbg_summ", [1024, 32], F32,
                                  kind="ExternalOutput")
        dbg_wo = nc.dram_tensor("dbg_wo", [32, 128, D], F32,
                                kind="ExternalOutput")
        dbg_gm = nc.dram_tensor("dbg_gm", [32, 128, D], BF16,
                                kind="ExternalOutput")
        dbg_mfm = nc.dram_tensor("dbg_mfm", [NTT, DCN, 128, TTA], BF16,
                                 kind="ExternalOutput")
        dbg_hact = nc.dram_tensor("dbg_hact", [2, 4, 8, 8, 128, 128], BF16,
                                  kind="ExternalOutput")
        dbg_xn = nc.dram_tensor("dbg_xn", [NTT, 128, 4 * D], BF16,
                                kind="ExternalOutput")
        dbg_st4 = nc.dram_tensor("dbg_st4", [NTT, 128, 24], F32,
                                 kind="ExternalOutput")
        dbg_comb = nc.dram_tensor("dbg_comb", [128, 256], F32,
                                  kind="ExternalOutput")
        dbg_acc = nc.dram_tensor("dbg_acc", [128, 8 * D], F32,
                                 kind="ExternalOutput")

    es = ExitStack()
    with es:
        def SEM(name):
            return es.enter_context(nc.semaphore(name))
        C = {}
        def cadd(k, n):
            C[k] = C.get(k, 0) + n
            return C[k]
        def cget(k):
            return C.get(k, 0)

        SN = ("sXL sXF sLN sXN sTRd sTR sTRe sPE0 sKVe sMG sSTa "
              "sPQ sQGe sRT sSTb sSTx "
              "sAC sSTsc sCC "
              "sGW sPE1 sACg sACu sGM sHS sDH sDWQ sPE2 sDV2 sSIL sGMS "
              "sWL sWPE sWEV sWST sZL "
              "sB sQL sQS sQO sP0").split()
        S = {n: SEM(n) for n in SN}

        def SB(scope, name, shape, dt):
            return scope.enter_context(nc.sbuf_tensor(name, shape, dt))

        mask64 = SB(es, "mask64", [128, TTA], F32)
        ident = SB(es, "identsb", [128, 128], BF16)
        gmap = SB(es, "gmapsb", [128, 4], F32)
        onehot = SB(es, "onehotsb", [128, 8], F32)
        bsel = SB(es, "bselsb", [128, 2], F32)
        combine = SB(es, "combine", [128, 32 * E], F32)
        summ = SB(es, "summ", [128, 32], F32)
        hg = SB(es, "hg", [128, 16], F32)
        stats_acc = SB(es, "stats_acc", [128, 32], F32)
        c1pp = SB(es, "c1pp", [128, 8], F32)
        c2pp = SB(es, "c2pp", [128, 8], F32)
        epsc = SB(es, "epsc", [128, 4], F32)

        with nc.Block() as blk:
            @blk.gpsimd
            def _(g):
                g.dma_start(ident[:, :], ident_d[:, :]).then_inc(S["sQL"], 16)
                g.dma_start(gmap[:, :], gmap_d[:, :]).then_inc(S["sQL"], 16)
                g.dma_start(onehot[:, :], onehot_d[:, :]).then_inc(S["sQL"], 16)
                g.dma_start(bsel[:, :], bsel_d[:, :]).then_inc(S["sQL"], 16)
                cadd("sQL", 64)
                g.wait_ge(S["sQL"], cget("sQL"))
                g.sem_inc(S["sP0"], 1)

            @blk.vector
            def _(v):
                v.memset(mask64[:, :], 1.0)
                v.memset(mask64.ap()[:, 0:TTA:SUB], 0.0)
                v.memset(summ[:, :], 0.0)
                v.memset(summ[:, 0:8], 1.0)
                v.memset(stats_acc[:, :], 0.0)
                v.memset(epsc[:, 0:1], 1e-5)
                v.memset(epsc[:, 1:2], 1e-6)
                v.memset(epsc[:, 2:3], 1e-30)
                v.sem_inc(S["sP0"], 1)

        with ExitStack() as e0a, ExitStack() as e0ap:
            wk_sb = SB(e0a, "wk_sb", [128, 8 * D], BF16)
            wv_sb = SB(e0a, "wv_sb", [128, 8 * D], BF16)
            wa_sb = SB(e0a, "wa_sb", [128, 8 * 2 * D], BF16)
            x_tm = SB(e0a, "x_tm", [128, 4 * D], F32)
            xn_tmb = [SB(e0a, f"xn_tmb{i}", [128, 4 * D], BF16)
                      for i in range(2)]
            st4 = SB(e0a, "st4", [128, 24], F32)
            lnscr = SB(e0a, "lnscr", [128, D], F32)
            xnf_t = [SB(e0a, f"xnf_t{i}", [128, DCN * TTA], BF16)
                     for i in range(2)]
            rk = SB(e0a, "rk", [128, TTA], F32)
            rkv = [SB(e0a, f"rkv{i}", [128, TTA], F32) for i in range(2)]
            rlg = [SB(e0a, f"rlg{i}", [128, TTA], F32) for i in range(2)]
            rth = [SB(e0a, f"rth{i}", [128, TTA], F32) for i in range(2)]
            mtp = SB(e0a, "mtp", [128, 6 * TTA], F32)
            tr_ps = [e0ap.enter_context(
                nc.psum_tensor(f"tr_ps{i}", [128, TTA], BF16))
                for i in range(2)]
            kvP = [e0ap.enter_context(
                nc.psum_tensor(f"kvP{i}", [128, TTA], F32))
                for i in range(2)]
            arP = [e0ap.enter_context(
                nc.psum_tensor(f"arP{i}", [128, 2 * TTA], F32))
                for i in range(2)]
            m0 = mtp.ap()[:, 0:TTA]
            m1 = mtp.ap()[:, TTA:2 * TTA]
            m2 = mtp.ap()[:, 2 * TTA:3 * TTA]
            m3 = mtp.ap()[:, 3 * TTA:4 * TTA]
            m4 = mtp.ap()[:, 4 * TTA:5 * TTA]
            m5 = mtp.ap()[:, 5 * TTA:6 * TTA]

            with nc.Block() as blk:
                @blk.gpsimd
                def _(g):
                    g.dma_start(wk_sb.ap().rearrange("p (c m) -> p c m", c=8),
                                wk_d.ap().rearrange("(c p) m -> p c m", p=128)
                                ).then_inc(S["sQL"], 16)
                    g.dma_start(wv_sb.ap().rearrange("p (c m) -> p c m", c=8),
                                wv_d.ap().rearrange("(c p) m -> p c m", p=128)
                                ).then_inc(S["sQL"], 16)
                    g.dma_start(wa_sb.ap().rearrange("p (c m) -> p c m", c=8),
                                wa_d.ap().rearrange("(c p) m -> p c m", p=128)
                                ).then_inc(S["sQL"], 16)
                    cadd("sQL", 48)
                    g.wait_ge(S["sQL"], cget("sQL"))

                @blk.sync
                def _(s):
                    for tt in range(NTT):
                        if tt > 0:
                            s.wait_ge(S["sXF"], tt)
                        s.dma_start(
                            x_tm.ap().rearrange("p (s d) -> p s d", s=4),
                            x_d[tt * TTA:(tt + 1) * TTA, :]
                            .rearrange("(s p) d -> p s d", p=128)
                        ).then_inc(S["sXL"], 16)
                        cadd("sXL", 16)
                        if dbg:
                            s.wait_ge(S["sXN"], tt + 1)
                            s.dma_start(dbg_xn[tt], xn_tmb[tt % 2][:, :]
                                        ).then_inc(S["sQS"], 16)
                            s.dma_start(dbg_st4[tt], st4[:, :]
                                        ).then_inc(S["sQS"], 16)
                            cadd("sQS", 32)
                        for dc in range(DCN):
                            i = tt * 8 + dc
                            s.wait_ge(S["sTRe"], i + 1)
                            s.dma_start(
                                xnf_d[dc, tt // 2, :,
                                      (tt % 2) * TTA:(tt % 2 + 1) * TTA],
                                xnf_t[tt % 2][:, dc * TTA:(dc + 1) * TTA]
                            ).then_inc(S["sSTx"], 16)
                            cadd("sSTx", 16)
                            s.wait_ge(S["sKVe"], i + 1)
                            s.dma_start(stash_d[6, dc, :,
                                                tt * TTA:(tt + 1) * TTA],
                                        rkv[i % 2][:, :]).then_inc(S["sSTa"], 16)
                            s.wait_ge(S["sMG"], 4 * i + 4)
                            s.dma_start(stash_d[7, dc, :,
                                                tt * TTA:(tt + 1) * TTA],
                                        rlg[i % 2][:, :]).then_inc(S["sSTa"], 16)
                            s.dma_start(stash_d[8, dc, :,
                                                tt * TTA:(tt + 1) * TTA],
                                        rth[i % 2][:, :]).then_inc(S["sSTa"], 16)
                            cadd("sSTa", 48)

                @blk.vector
                def _(v):
                    for tt in range(NTT):
                        if tt == 0:
                            v.wait_ge(S["sP0"], 1)
                        v.wait_ge(S["sLN"], 4 * tt)
                        v.wait_ge(S["sXL"], 16 * (tt + 1))
                        v.reduce_sum(
                            st4[:, 0:4],
                            x_tm.ap().rearrange("p (s d) -> p s d", s=4),
                            axis=AX).then_inc(S["sLN"], 1)
                        v.wait_ge(S["sLN"], 4 * tt + 2)
                        v.tensor_scalar(st4[:, 4:8], st4[:, 0:4], 1.0 / D,
                                        None, ALU.mult)
                        v.tensor_scalar(st4[:, 12:16], st4[:, 8:12], 1.0 / D,
                                        None, ALU.mult)
                        v.tensor_mul(st4[:, 16:20], st4[:, 4:8], st4[:, 4:8])
                        v.memset(st4[:, 22:23], 0.0)
                        v.tensor_sub(st4[:, 16:20], st4[:, 12:16],
                                     st4[:, 16:20]).then_inc(S["sLN"], 1)
                        v.wait_ge(S["sLN"], 4 * tt + 4)
                        v.reciprocal(st4[:, 12:16], st4[:, 20:24])
                        v.memset(st4[:, 22:23], 0.0)
                        if tt >= 2:
                            v.wait_ge(S["sTRd"], tt - 1)
                        last = None
                        for s4 in range(4):
                            last = v.tensor_scalar(
                                xn_tmb[tt % 2][:, s4 * D:(s4 + 1) * D],
                                x_tm[:, s4 * D:(s4 + 1) * D],
                                st4[:, 4 + s4:5 + s4], st4[:, 12 + s4:13 + s4],
                                ALU.subtract, ALU.mult)
                        last.then_inc(S["sXN"], 1)
                        v.sem_inc(S["sXF"], 1)
                        for dc in range(DCN):
                            i = tt * 8 + dc
                            v.wait_ge(S["sTR"], i + 1)
                            if tt >= 2:
                                v.wait_ge(S["sSTx"],
                                          16 * (8 * (tt - 2) + dc + 1))
                            ev = v.tensor_copy(
                                xnf_t[tt % 2][:, dc * TTA:(dc + 1) * TTA],
                                tr_ps[i % 2][:, :])
                            ev.then_inc(S["sTRe"], 1)
                            if dc == DCN - 1:
                                v.sem_inc(S["sTRd"], 1)
                        for dc in range(DCN):
                            i = tt * 8 + dc
                            v.wait_ge(S["sPE0"], i * 2 + 1)
                            if i >= 2:
                                v.wait_ge(S["sSTa"], 48 * (i - 1))
                            v.tensor_copy(rk[:, :], kvP[i % 2][:, :])
                            v.tensor_mul(rkv[i % 2][:, :], rk[:, :],
                                         kvP[(i + 1) % 2][:, :]
                                         ).then_inc(S["sKVe"], 1)
                            v.wait_ge(S["sMG"], 4 * i + 1)
                            are = arP[i % 2][:, 0:TTA]
                            aim = arP[i % 2][:, TTA:2 * TTA]
                            v.tensor_add(m2, m2, m3)
                            v.tensor_scalar(m0, m0, -1e-37, 1e-37,
                                            ALU.mult, ALU.add)
                            v.tensor_tensor(m4, are, m0, ALU.add)
                            v.reciprocal(m0, m4)
                            v.tensor_tensor(m3, aim, m0, ALU.mult
                                            ).then_inc(S["sMG"], 1)
                            v.wait_ge(S["sMG"], 4 * i + 3)
                            if i >= 2:
                                v.wait_ge(S["sSTa"], 48 * (i - 1))
                            v.tensor_scalar(m5, m5, -PI / 2, PI / 2,
                                            ALU.mult, ALU.add)
                            v.tensor_mul(m5, m5, m1)
                            v.tensor_add(rth[i % 2][:, :], m0, m5
                                         ).then_inc(S["sMG"], 1)

                @blk.scalar
                def _(a):
                    for tt in range(NTT):
                        if tt == 0:
                            a.wait_ge(S["sP0"], 2)
                        a.wait_ge(S["sLN"], 4 * tt + 1)
                        if tt >= 1:
                            a.wait_ge(S["sMG"], 4 * (8 * tt))
                        last = None
                        for s4 in range(4):
                            last = act(a, lnscr[:, :],
                                       x_tm[:, s4 * D:(s4 + 1) * D],
                                       AF.Square,
                                       accum_out=st4[:, 8 + s4:9 + s4])
                        last.then_inc(S["sLN"], 1)
                        a.wait_ge(S["sLN"], 4 * tt + 3)
                        act(a, st4[:, 20:24], st4[:, 16:20], AF.Sqrt,
                            bias=epsc[:, 0:1]).then_inc(S["sLN"], 1)
                        for dc in range(DCN):
                            i = tt * 8 + dc
                            are = arP[i % 2][:, 0:TTA]
                            aim = arP[i % 2][:, TTA:2 * TTA]
                            a.wait_ge(S["sPE0"], i * 2 + 2)
                            if i >= 1:
                                a.wait_ge(S["sMG"], 4 * i)
                            act(a, m0, are, AF.Sign)
                            act(a, m1, aim, AF.Sign)
                            act(a, m0, m0, AF.Square)
                            act(a, m2, are, AF.Square)
                            act(a, m3, aim, AF.Square).then_inc(S["sMG"], 1)
                            a.wait_ge(S["sMG"], 4 * i + 2)
                            if i >= 2:
                                a.wait_ge(S["sSTa"], 48 * (i - 1))
                            act(a, m2, m2, AF.Sqrt, bias=epsc[:, 2:3])
                            act(a, m5, m4, AF.Sign)
                            act(a, m0, m3, AF.Arctan)
                            act(a, m2, m2, AF.Sigmoid)
                            act(a, rlg[i % 2][:, :], m2, AF.Ln
                                ).then_inc(S["sMG"], 1)

                @blk.tensor
                def _(t):
                    t.wait_ge(S["sQL"], cget("sQL"))
                    for tt in range(NTT):
                        t.wait_ge(S["sXN"], tt + 1)
                        for dc in range(DCN):
                            i = tt * 8 + dc
                            if i >= 2:
                                t.wait_ge(S["sTRe"], i - 1)
                            last = None
                            for s4 in range(4):
                                last = t.transpose(
                                    tr_ps[i % 2][:, s4 * 128:(s4 + 1) * 128],
                                    xn_tmb[tt % 2][:, s4 * D + dc * 128:
                                                   s4 * D + (dc + 1) * 128],
                                    ident[:, :])
                            last.then_inc(S["sTR"], 1)
                        t.wait_ge(S["sTRe"], 8 * (tt + 1))
                        xs = xnf_t[tt % 2]
                        for dc in range(DCN):
                            i = tt * 8 + dc
                            if i >= 1:
                                t.wait_ge(S["sKVe"], i)
                            last = None
                            for (ps, wsb) in ((kvP[i % 2][:, :], wk_sb),
                                              (kvP[(i + 1) % 2][:, :], wv_sb)):
                                for kc in range(8):
                                    last = t.matmul(
                                        ps,
                                        wsb[:, kc * D + dc * 128:
                                            kc * D + dc * 128 + 128],
                                        xs[:, kc * TTA:(kc + 1) * TTA],
                                        start=(kc == 0), stop=(kc == 7))
                            last.then_inc(S["sPE0"], 1)
                            if i >= 2:
                                t.wait_ge(S["sMG"], 4 * (i - 1))
                            last = None
                            for (cps, col0) in ((arP[i % 2][:, 0:TTA],
                                                 dc * 128),
                                                (arP[i % 2][:, TTA:2 * TTA],
                                                 D + dc * 128)):
                                for kc in range(8):
                                    last = t.matmul(
                                        cps,
                                        wa_sb[:, kc * 2 * D + col0:
                                              kc * 2 * D + col0 + 128],
                                        xs[:, kc * TTA:(kc + 1) * TTA],
                                        start=(kc == 0), stop=(kc == 7))
                            last.then_inc(S["sPE0"], 1)

        with ExitStack() as e0b, ExitStack() as e0bp:
            wq_sb = SB(e0b, "wq_sb", [128, 8 * D], BF16)
            wg_sb = SB(e0b, "wg_sb", [128, 8 * D], BF16)
            gwt_sb = SB(e0b, "gwt_sb", [128, 8 * E], BF16)
            xnb_t = [SB(e0b, f"xnb_t{i}", [128, DCN * TTA], BF16)
                     for i in range(2)]
            rq = [SB(e0b, f"rq{i}", [128, TTA], F32) for i in range(2)]
            rsg = [SB(e0b, f"rsg{i}", [128, TTA], F32) for i in range(2)]
            rt = SB(e0b, "rt", [128, 64], F32)
            pq = [e0bp.enter_context(
                nc.psum_tensor(f"pqQ{i}", [128, TTA], F32)) for i in range(2)]
            pg = [e0bp.enter_context(
                nc.psum_tensor(f"pgQ{i}", [128, TTA], F32)) for i in range(2)]
            pr = e0bp.enter_context(nc.psum_tensor("prQ", [128, 16], F32))

            with nc.Block() as blk:
                @blk.gpsimd
                def _(g):
                    g.wait_ge(S["sPE0"], 128)
                    g.wait_ge(S["sMG"], 256)
                    g.wait_ge(S["sSTa"], 48 * 64)
                    g.wait_ge(S["sSTx"], 16 * 64)
                    g.dma_start(wq_sb.ap().rearrange("p (c m) -> p c m", c=8),
                                wq_d.ap().rearrange("(c p) m -> p c m", p=128)
                                ).then_inc(S["sQL"], 16)
                    g.dma_start(wg_sb.ap().rearrange("p (c m) -> p c m", c=8),
                                wg_d.ap().rearrange("(c p) m -> p c m", p=128)
                                ).then_inc(S["sQL"], 16)
                    g.dma_start(gwt_sb.ap().rearrange("p (c e) -> p c e", c=8),
                                gwT_d.ap().rearrange("(c p) e -> p c e", p=128)
                                ).then_inc(S["sQL"], 16)
                    cadd("sQL", 48)
                    g.wait_ge(S["sQL"], cget("sQL"))

                @blk.sync
                def _(s):
                    for tt in range(NTT):
                        s.wait_ge(S["sSTx"], 16 * 8 * (tt + 1))
                        if tt >= 2:
                            s.wait_ge(S["sPQ"], 8 * (tt - 1))
                            s.wait_ge(S["sRT"], 8 * (tt - 1))
                        s.dma_start(
                            xnb_t[tt % 2].ap().rearrange(
                                "p (c t) -> p c t", c=8),
                            xnf_d[:, tt // 2, :,
                                  (tt % 2) * TTA:(tt % 2 + 1) * TTA]
                            .rearrange("c p t -> p c t")
                        ).then_inc(S["sZL"], 16)
                        cadd("sZL", 16)
                        for dc in range(DCN):
                            i = tt * 8 + dc
                            s.wait_ge(S["sQGe"], i * 2 + 2)
                            s.dma_start(stash_d[4, dc, :,
                                                tt * TTA:(tt + 1) * TTA],
                                        rq[i % 2][:, :]).then_inc(S["sSTb"], 16)
                            s.dma_start(stash_d[5, dc, :,
                                                tt * TTA:(tt + 1) * TTA],
                                        rsg[i % 2][:, :]).then_inc(S["sSTb"], 16)
                            cadd("sSTb", 32)

                @blk.tensor
                def _(t):
                    t.wait_ge(S["sQL"], cget("sQL"))
                    for tt in range(NTT):
                        t.wait_ge(S["sZL"], 16 * (tt + 1))
                        xs = xnb_t[tt % 2]
                        for dc in range(DCN):
                            i = tt * 8 + dc
                            if i >= 2:
                                t.wait_ge(S["sQGe"], (i - 1) * 2)
                            last = None
                            for (ps, wsb) in ((pq[i % 2], wq_sb),
                                              (pg[i % 2], wg_sb)):
                                for kc in range(8):
                                    last = t.matmul(
                                        ps[:, :],
                                        wsb[:, kc * D + dc * 128:
                                            kc * D + dc * 128 + 128],
                                        xs[:, kc * TTA:(kc + 1) * TTA],
                                        start=(kc == 0), stop=(kc == 7))
                            last.then_inc(S["sPQ"], 1)
                        for sp in range(2):
                            t.wait_ge(S["sRT"], tt * 8 + sp * 4)
                            last = None
                            for k in range(2):
                                s4 = sp * 2 + k
                                for kc in range(8):
                                    last = t.matmul(
                                        pr[:, k * 8:(k + 1) * 8],
                                        xs[:, kc * TTA + s4 * 128:
                                           kc * TTA + (s4 + 1) * 128],
                                        gwt_sb[:, kc * E:(kc + 1) * E],
                                        start=(kc == 0), stop=(kc == 7))
                            last.then_inc(S["sRT"], 1)

                @blk.vector
                def _(v):
                    for tt in range(NTT):
                        for dc in range(DCN):
                            i = tt * 8 + dc
                            v.wait_ge(S["sPQ"], i + 1)
                            if i >= 2:
                                v.wait_ge(S["sSTb"], 32 * (i - 1))
                            v.tensor_copy(rq[i % 2][:, :], pq[i % 2][:, :]
                                          ).then_inc(S["sQGe"], 1)
                        for sp in range(2):
                            v.wait_ge(S["sRT"], tt * 8 + sp * 4 + 1)
                            v.reduce_max(rt[:, 0:1], pr[:, 0:8], axis=AX,
                                         negate=True)
                            v.reduce_max(rt[:, 32:33], pr[:, 8:16], axis=AX,
                                         negate=True).then_inc(S["sRT"], 1)
                            v.wait_ge(S["sRT"], tt * 8 + sp * 4 + 3)
                            def ops(B, g128):
                                yield lambda: v.reduce_sum(
                                    rt[:, B + 1:B + 2], rt[:, B + 8:B + 16],
                                    axis=AX)
                                yield lambda: v.reciprocal(
                                    rt[:, B + 2:B + 3], rt[:, B + 1:B + 2])
                                yield lambda: v.tensor_scalar(
                                    rt[:, B + 8:B + 16], rt[:, B + 8:B + 16],
                                    rt[:, B + 2:B + 3], None, ALU.mult)
                                yield lambda: v.reduce_max(
                                    rt[:, B + 3:B + 4], rt[:, B + 8:B + 16],
                                    axis=AX)
                                yield lambda: v.tensor_scalar(
                                    rt[:, B + 16:B + 24], rt[:, B + 8:B + 16],
                                    rt[:, B + 3:B + 4], None, ALU.is_ge)
                                yield lambda: v.tensor_scalar(
                                    rt[:, B + 24:B + 32], rt[:, B + 16:B + 24],
                                    -1.0, 1.0, ALU.mult, ALU.add)
                                yield lambda: v.tensor_mul(
                                    rt[:, B + 24:B + 32], rt[:, B + 8:B + 16],
                                    rt[:, B + 24:B + 32])
                                yield lambda: v.reduce_max(
                                    rt[:, B + 4:B + 5], rt[:, B + 24:B + 32],
                                    axis=AX)
                                yield lambda: v.tensor_scalar(
                                    rt[:, B + 24:B + 32], rt[:, B + 24:B + 32],
                                    rt[:, B + 4:B + 5], None, ALU.is_ge)
                                yield lambda: v.tensor_add(
                                    rt[:, B + 5:B + 6], rt[:, B + 3:B + 4],
                                    rt[:, B + 4:B + 5])
                                yield lambda: v.reciprocal(
                                    rt[:, B + 6:B + 7], rt[:, B + 5:B + 6])
                                yield lambda: v.tensor_mul(
                                    rt[:, B + 3:B + 4], rt[:, B + 3:B + 4],
                                    rt[:, B + 6:B + 7])
                                yield lambda: v.tensor_mul(
                                    rt[:, B + 4:B + 5], rt[:, B + 4:B + 5],
                                    rt[:, B + 6:B + 7])
                                yield lambda: v.tensor_scalar(
                                    rt[:, B + 16:B + 24], rt[:, B + 16:B + 24],
                                    rt[:, B + 3:B + 4], None, ALU.mult)
                                yield lambda: v.tensor_scalar(
                                    rt[:, B + 24:B + 32], rt[:, B + 24:B + 32],
                                    rt[:, B + 4:B + 5], None, ALU.mult)
                                yield lambda: v.tensor_add(
                                    combine[:, g128 * 8:(g128 + 1) * 8],
                                    rt[:, B + 16:B + 24], rt[:, B + 24:B + 32])
                            g0 = tt * 4 + sp * 2
                            it0 = ops(0, g0)
                            it1 = ops(32, g0 + 1)
                            last = None
                            for o0, o1 in zip(it0, it1):
                                last = o0()
                                last = o1()
                            last.then_inc(S["sRT"], 1)

                @blk.scalar
                def _(a):
                    for tt in range(NTT):
                        for dc in range(DCN):
                            i = tt * 8 + dc
                            a.wait_ge(S["sPQ"], i + 1)
                            if i >= 2:
                                a.wait_ge(S["sSTb"], 32 * (i - 1))
                            act(a, rsg[i % 2][:, :], pg[i % 2][:, :],
                                AF.Silu).then_inc(S["sQGe"], 1)
                        for sp in range(2):
                            a.wait_ge(S["sRT"], tt * 8 + sp * 4 + 2)
                            act(a, rt[:, 8:16], pr[:, 0:8], AF.Exp,
                                bias=rt[:, 0:1])
                            act(a, rt[:, 40:48], pr[:, 8:16], AF.Exp,
                                bias=rt[:, 32:33]).then_inc(S["sRT"], 1)

        streams = {k: [] for k in "tvasg"}
        seqn = [0]
        lastest = {k: 0.0 for k in "tvasg"}

        def ADD(engs, est, fn):
            streams[engs].append((est, seqn[0], fn))
            seqn[0] += 1

        with ExitStack() as big, ExitStack() as bigp:
            uscr = SB(big, "uscr", [128, 12 * TTA], F32)
            _sl = [uscr.ap()[:, i * TTA:(i + 1) * TTA] for i in range(12)]
            _map = {0: 0, 1: 1, 3: 2, 4: 3, 5: 4, 6: 5, 7: 6, 8: 7,
                    9: 8, 10: 9, 11: 10, 12: 11, 2: 3}
            U = [_sl[_map[i]] for i in range(13)]
            b_kv = SB(big, "b_kv", [128, TTA], F32)
            csm = SB(big, "csm", [128, 120], F32)
            zt = SB(big, "zt", [128, TTA], BF16)
            agp = uscr.ap()[:, 0:256]
            prr = SB(big, "prr", [128, 64], F32)
            pri = SB(big, "pri", [128, 64], F32)
            mtmp = SB(big, "mtmp", [128, 32], F32)
            selb = SB(big, "selb", [128, 64], F32)
            sg43 = SB(big, "sg43", [4, 4], F32)
            sg43b = SB(big, "sg43b", [4, 4], F32)
            row0 = SB(big, "row0", [1, 128], F32)
            row1 = SB(big, "row1", [1, 128], F32)
            grw = SB(big, "grw", [1, 256], F32)
            gnsr = uscr.ap()[0:1, 0:D]
            gnbr = uscr.ap()[0:1, D:2 * D]
            c1row = uscr.ap()[0:1, 2 * D:3 * D]
            c2row = uscr.ap()[0:1, 3 * D:4 * D]
            xnq = [SB(big, f"xnq{i}", [128, 8 * TTA], BF16) for i in range(3)]
            slab = [SB(big, f"slab{i}", [128, 2 * 8 * TTA], BF16)
                    for i in range(2)]
            dwq = [SB(big, f"dwq{i}", [128, 8 * D], BF16) for i in range(2)]
            acc = SB(big, "acc1", [128, 8 * D], F32)
            acc1_ref = acc
            hls = [SB(big, f"hls{i}", [128, 8 * 128], BF16) for i in range(3)]
            gel = [SB(big, f"gel{i}", [128, TTA], BF16) for i in range(2)]
            ups = [SB(big, f"ups{i}", [128, TTA], BF16) for i in range(2)]
            hstg = [SB(big, f"hstg{i}", [128, TTA], BF16) for i in range(2)]
            gms = [SB(big, "gms0", [128, D], BF16)]
            mfl = [SB(big, "mfl0", [128, DCN * 512], BF16)]
            woo = [SB(big, "woo0", [128, D], F32)]
            wo_sb = SB(big, "wo_sb", [128, 8 * D], BF16)
            zl = SB(big, "zlw", [128, TTA], BF16)
            sgl = SB(big, "sglw", [128, TTA], F32)
            mfb = [SB(big, "mfb0", [128, TTA], BF16)]
            pA = [bigp.enter_context(nc.psum_tensor(f"pA{i}", [128, TTA], F32))
                  for i in range(2)]
            pB = [bigp.enter_context(nc.psum_tensor(f"pB{i}", [128, TTA], F32))
                  for i in range(2)]
            pD = [bigp.enter_context(nc.psum_tensor(f"pD{i}", [128, TTA], F32))
                  for i in range(2)]
            pW = [bigp.enter_context(nc.psum_tensor(f"pW{i}", [128, TTA], F32))
                  for i in range(2)]
            ps43 = pW[1].ap()[0:4, 0:4]

            E0DONE = 960.0

            achain = []

            def ach(eng, cost, fn):
                achain.append((eng, cost, fn))

            def emit_scan(tt, dc):
                i = tt * 8 + dc
                t0 = tt * TTA

                def f_load(s, dc=dc, t0=t0, i=i):
                    if i == 0:
                        s.wait_ge(S["sPE0"], 128)
                        s.wait_ge(S["sMG"], 256)
                        s.wait_ge(S["sQGe"], 128)
                        s.wait_ge(S["sRT"], 64)
                        s.wait_ge(S["sPQ"], 64)
                        s.wait_ge(S["sSTb"], 32 * 64)
                    s.wait_ge(S["sSTa"], 48 * (i + 1))
                    s.dma_start(b_kv[:, :], stash_d[6, dc, :, t0:t0 + TTA]
                                ).then_inc(S["sQL"], 16)
                    s.dma_start(U[0], stash_d[7, dc, :, t0:t0 + TTA]
                                ).then_inc(S["sQL"], 16)
                    s.dma_start(U[5], stash_d[8, dc, :, t0:t0 + TTA]
                                ).then_inc(S["sQL"], 16)
                    cadd("sQL", 48)
                    s.wait_ge(S["sQL"], cget("sQL"))
                    return None
                ach("s", 2.5, f_load)

                def f_d4(v):
                    v.tensor_tensor_scan(U[1], mask64[:, :], U[0],
                                         0.0, ALU.mult, ALU.add)
                    v.tensor_tensor_scan(U[3], mask64[:, :], U[5],
                                         0.0, ALU.mult, ALU.add)
                    v.tensor_scalar(U[4], U[3], INV2PI, MAGIC,
                                    ALU.mult, ALU.add)
                    v.tensor_scalar(U[4], U[4], -MAGIC, None, ALU.add)
                    return v.scalar_tensor_tensor(U[4], U[4], -TWOPI, U[3],
                                                  ALU.mult, ALU.add)
                ach("v", 4.0, f_d4)

                def f_a5(a):
                    act(a, U[6], U[4], AF.Abs)
                    act(a, U[7], U[1], AF.Exp, scale=-1.0)
                    return act(a, U[1], U[1], AF.Exp)
                ach("a", 2.2, f_a5)

                def f_d5(v):
                    return v.tensor_scalar(U[6], U[6], -1.0,
                                           PI / 2, ALU.mult, ALU.add)
                ach("v", 0.8, f_d5)

                def f_a6(a):
                    act(a, U[4], U[4], AF.Sin)
                    return act(a, U[6], U[6], AF.Sin)
                ach("a", 1.6, f_a6)

                def f_d6(v, dc=dc):
                    c = csm
                    v.tensor_mul(U[3], U[7], b_kv[:, :])
                    v.tensor_mul(U[8], U[3], U[6])
                    v.tensor_mul(U[9], U[3], U[4])
                    v.reduce_sum(c[:, 0:8],
                                 U[8].rearrange("p (c j) -> p c j", c=8),
                                 axis=AX)
                    v.reduce_sum(c[:, 8:16],
                                 U[9].rearrange("p (c j) -> p c j", c=8),
                                 axis=AX)
                    eL63 = U[1][:, SUB - 1:TTA:SUB]
                    cos63 = U[6][:, SUB - 1:TTA:SUB]
                    sin63 = U[4][:, SUB - 1:TTA:SUB]
                    v.tensor_mul(c[:, 16:24], eL63, cos63)
                    v.tensor_mul(c[:, 24:32], eL63, sin63)
                    v.tensor_copy(c[:, 48:49], summ[:, 16 + dc:17 + dc])
                    v.tensor_copy(c[:, 57:58], summ[:, 24 + dc:25 + dc])
                    v.tensor_copy(c[:, 66:67], summ[:, 0 + dc:1 + dc])
                    v.tensor_copy(c[:, 75:76], summ[:, 8 + dc:9 + dc])
                    v.tensor_mul(c[:, 84:92], c[:, 16:24], c[:, 0:8])
                    v.tensor_mul(c[:, 32:40], c[:, 24:32], c[:, 8:16])
                    v.tensor_mul(c[:, 96:104], c[:, 24:32], c[:, 0:8])
                    v.tensor_mul(c[:, 40:48], c[:, 16:24], c[:, 8:16])
                    v.tensor_add(c[:, 32:40], c[:, 84:92], c[:, 32:40])
                    v.tensor_sub(c[:, 40:48], c[:, 96:104], c[:, 40:48])
                    for cc in range(NSUB):
                        Aer = c[:, 16 + cc:17 + cc]
                        Aei = c[:, 24 + cc:25 + cc]
                        Hr = c[:, 48 + cc:49 + cc]
                        Hi = c[:, 57 + cc:58 + cc]
                        Pr = c[:, 66 + cc:67 + cc]
                        Pi = c[:, 75 + cc:76 + cc]
                        p1 = c[:, 104:105]; p2 = c[:, 105:106]
                        p3 = c[:, 106:107]; p4 = c[:, 107:108]
                        a1 = c[:, 108:109]; a2 = c[:, 109:110]
                        b1 = c[:, 110:111]; b2 = c[:, 111:112]
                        tA = c[:, 112:113]; tB = c[:, 113:114]
                        v.tensor_mul(p1, Pr, Aer)
                        v.tensor_mul(p2, Pi, Aei)
                        v.tensor_mul(p3, Pr, Aei)
                        v.tensor_mul(p4, Pi, Aer)
                        v.tensor_mul(a1, Aer, Hr)
                        v.tensor_mul(a2, Aei, Hi)
                        v.tensor_mul(b1, Aer, Hi)
                        v.tensor_mul(b2, Aei, Hr)
                        v.tensor_sub(c[:, 67 + cc:68 + cc], p1, p2)
                        v.tensor_add(c[:, 76 + cc:77 + cc], p3, p4)
                        v.tensor_sub(tA, a1, a2)
                        v.tensor_add(tB, b1, b2)
                        v.tensor_add(c[:, 49 + cc:50 + cc], tA,
                                     c[:, 32 + cc:33 + cc])
                        v.tensor_add(c[:, 58 + cc:59 + cc], tB,
                                     c[:, 40 + cc:41 + cc])
                    v.tensor_copy(summ[:, 0 + dc:1 + dc], c[:, 74:75])
                    v.tensor_copy(summ[:, 8 + dc:9 + dc], c[:, 83:84])
                    v.tensor_copy(summ[:, 16 + dc:17 + dc], c[:, 56:57])
                    v.tensor_copy(summ[:, 24 + dc:25 + dc], c[:, 65:66])
                    j0r = U[8][:, 0:TTA:SUB]
                    j0i = U[9][:, 0:TTA:SUB]
                    v.tensor_add(j0r, j0r, c[:, 48:56])
                    v.tensor_sub(j0i, j0i, c[:, 57:65])
                    v.tensor_tensor_scan(U[7], mask64[:, :], U[8],
                                         0.0, ALU.mult, ALU.add)
                    v.tensor_tensor_scan(U[3], mask64[:, :], U[9],
                                         0.0, ALU.mult, ALU.add)
                    v.tensor_mul(U[10], U[1], U[6])
                    v.tensor_mul(U[11], U[1], U[4])
                    v.tensor_mul(U[0], U[10], U[7])
                    v.tensor_mul(U[5], U[11], U[3])
                    v.tensor_add(U[0], U[0], U[5])
                    v.tensor_mul(U[5], U[11], U[7])
                    v.tensor_mul(U[6], U[10], U[3])
                    v.tensor_sub(U[5], U[5], U[6])
                    APrb = c[:, 66:74].unsqueeze(2).broadcast_to(
                        (128, 8, SUB))
                    APib = c[:, 75:83].unsqueeze(2).broadcast_to(
                        (128, 8, SUB))
                    Ar3 = U[10].rearrange("p (c j) -> p c j", c=8)
                    Ai3 = U[11].rearrange("p (c j) -> p c j", c=8)
                    t4v = U[4].rearrange("p (c j) -> p c j", c=8)
                    t6v = U[6].rearrange("p (c j) -> p c j", c=8)
                    t12v = U[12].rearrange("p (c j) -> p c j", c=8)
                    v.tensor_tensor(t4v, Ar3, APrb, ALU.mult)
                    v.tensor_tensor(t6v, Ai3, APib, ALU.mult)
                    v.tensor_sub(U[4], U[4], U[6])
                    v.tensor_tensor(t6v, Ar3, APib, ALU.mult)
                    v.tensor_tensor(t12v, Ai3, APrb, ALU.mult)
                    return v.tensor_add(U[6], U[6], U[12])
                ach("v", 20.0, f_d6)

                def f_st(s, tt=tt, dc=dc):
                    t0 = tt * TTA
                    s.dma_start(stash_d[0, dc, :, t0:t0 + TTA],
                                U[0]).then_inc(S["sSTsc"], 16)
                    s.dma_start(stash_d[1, dc, :, t0:t0 + TTA],
                                U[5]).then_inc(S["sSTsc"], 16)
                    s.dma_start(stash_d[2, dc, :, t0:t0 + TTA],
                                U[4]).then_inc(S["sSTsc"], 16)
                    s.dma_start(stash_d[3, dc, :, t0:t0 + TTA],
                                U[6]).then_inc(S["sSTsc"], 16)
                    cadd("sSTsc", 64)
                    s.wait_ge(S["sSTsc"], cget("sSTsc"))
                    return None
                ach("s", 2.5, f_st)

            for tt in range(NTT):
                for dc in range(DCN):
                    emit_scan(tt, dc)

            x1_marker = len(achain)
            X1EST = 6400.0

            def x1_store(s):
                s.dma_start(summ_in_d[:, :], summ[:, :]).then_inc(S["sQS"], 16)
                cadd("sQS", 16)
                s.wait_ge(S["sQS"], cget("sQS"))
                return None
            ach("s", 1.0, x1_store)

            def x1_coll(g):
                g.collective_compute(
                    "AllGather", ALU.bypass,
                    replica_groups=[list(range(8))],
                    ins=[summ_in_d.ap().opt()],
                    outs=[summ_ag_d.ap().opt()]).then_inc(S["sCC"], 1)
                cadd("sCC", 1)
                g.wait_ge(S["sCC"], cget("sCC"))
                return None
            ach("g", 60.0, x1_coll)

            def x1_load(s):
                s.dma_start(
                    agp.rearrange("p (r c) -> p r c", r=8),
                    summ_ag_d.ap().rearrange("(r p) c -> p r c", p=128)
                ).then_inc(S["sQL"], 16)
                cadd("sQL", 16)
                s.wait_ge(S["sQL"], cget("sQL"))
                return None
            ach("s", 1.0, x1_load)

            def x1_pref(v):
                v.memset(prr[:, :], 0.0)
                v.memset(pri[:, :], 0.0)
                for r in range(1, 8):
                    if r == 4:
                        continue
                    co = (r - 1) * 32
                    def colA(dc):
                        return agp[:, co + dc:co + dc + 1]
                    def colAi(dc):
                        return agp[:, co + 8 + dc:co + 8 + dc + 1]
                    def colH(dc):
                        return agp[:, co + 16 + dc:co + 16 + dc + 1]
                    def colHi(dc):
                        return agp[:, co + 24 + dc:co + 24 + dc + 1]
                    for dc in range(DCN):
                        v.tensor_mul(mtmp[:, dc:dc + 1], colA(dc),
                                     prr[:, dc * 8 + r - 1:dc * 8 + r])
                    for dc in range(DCN):
                        v.tensor_mul(mtmp[:, 8 + dc:9 + dc], colAi(dc),
                                     pri[:, dc * 8 + r - 1:dc * 8 + r])
                    for dc in range(DCN):
                        v.tensor_mul(mtmp[:, 16 + dc:17 + dc], colA(dc),
                                     pri[:, dc * 8 + r - 1:dc * 8 + r])
                    for dc in range(DCN):
                        v.tensor_mul(mtmp[:, 24 + dc:25 + dc], colAi(dc),
                                     prr[:, dc * 8 + r - 1:dc * 8 + r])
                    for dc in range(DCN):
                        v.tensor_sub(mtmp[:, dc:dc + 1], mtmp[:, dc:dc + 1],
                                     mtmp[:, 8 + dc:9 + dc])
                    for dc in range(DCN):
                        v.tensor_add(mtmp[:, 16 + dc:17 + dc],
                                     mtmp[:, 16 + dc:17 + dc],
                                     mtmp[:, 24 + dc:25 + dc])
                    for dc in range(DCN):
                        v.tensor_add(prr[:, dc * 8 + r:dc * 8 + r + 1],
                                     mtmp[:, dc:dc + 1], colH(dc))
                    for dc in range(DCN):
                        v.tensor_add(pri[:, dc * 8 + r:dc * 8 + r + 1],
                                     mtmp[:, 16 + dc:17 + dc], colHi(dc))
                for dc in range(DCN):
                    v.tensor_mul(selb[:, dc * 8:(dc + 1) * 8],
                                 prr[:, dc * 8:(dc + 1) * 8], onehot[:, :])
                for dc in range(DCN):
                    v.reduce_sum(hg[:, 2 * dc:2 * dc + 1],
                                 selb[:, dc * 8:(dc + 1) * 8], axis=AX)
                for dc in range(DCN):
                    v.tensor_mul(selb[:, dc * 8:(dc + 1) * 8],
                                 pri[:, dc * 8:(dc + 1) * 8], onehot[:, :])
                last = None
                for dc in range(DCN):
                    last = v.reduce_sum(hg[:, 2 * dc + 1:2 * dc + 2],
                                        selb[:, dc * 8:(dc + 1) * 8], axis=AX)
                return last
            ach("v", 25.0, x1_pref)

            l_h0r = U[1]; l_h0i = U[2]; l_acr = U[3]
            l_aci = U[7]; l_q = U[8]; l_sg = U[9]
            w_1 = U[10]; w_2 = U[11]; w_3 = U[12]
            acc4 = csm.ap()[:, 0:4]

            def emit_a2(tt, dc):
                t0 = tt * TTA
                i = tt * 8 + dc

                def f_l(s, dc=dc, t0=t0, i=i):
                    if i == 0:
                        s.wait_ge(S["sSTsc"], cget("sSTsc"))
                        s.wait_ge(S["sSTb"], 32 * 64)
                    for qi, dst in ((0, l_h0r), (1, l_h0i), (2, l_acr),
                                    (3, l_aci), (4, l_q), (5, l_sg)):
                        s.dma_start(dst, stash_d[qi, dc, :, t0:t0 + TTA]
                                    ).then_inc(S["sQL"], 16)
                        cadd("sQL", 16)
                    s.wait_ge(S["sQL"], cget("sQL"))
                    return None
                ach("s", 4.0, f_l)

                def f_d(v, dc=dc):
                    hgr = hg[:, 2 * dc:2 * dc + 1]
                    hgi = hg[:, 2 * dc + 1:2 * dc + 2]
                    v.tensor_scalar(w_1, l_acr, hgr, None, ALU.mult)
                    v.tensor_scalar(w_2, l_aci, hgi, None, ALU.mult)
                    v.tensor_sub(w_1, w_1, w_2)
                    v.tensor_add(w_1, w_1, l_h0r)
                    v.tensor_scalar(w_2, l_acr, hgi, None, ALU.mult)
                    v.tensor_scalar(w_3, l_aci, hgr, None, ALU.mult)
                    v.tensor_add(w_2, w_2, w_3)
                    v.tensor_add(w_2, w_2, l_h0i)
                    v.tensor_mul(w_1, l_q, w_1)
                    v.tensor_mul(w_2, l_q, w_2)
                    return v.tensor_mul(zt[:, :], w_1, l_sg)
                ach("v", 8.0, f_d)

                def f_a(a):
                    act(a, w_3, w_1, AF.Copy, accum_out=acc4[:, 0:1])
                    act(a, w_3, w_2, AF.Copy, accum_out=acc4[:, 1:2])
                    act(a, w_3, w_1, AF.Square, accum_out=acc4[:, 2:3])
                    return act(a, w_3, w_2, AF.Square, accum_out=acc4[:, 3:4])
                ach("a", 3.0, f_a)

                def f_d2(v, dc=dc):
                    return v.tensor_add(stats_acc[:, dc * 4:(dc + 1) * 4],
                                        stats_acc[:, dc * 4:(dc + 1) * 4],
                                        acc4[:, :])
                ach("v", 0.4, f_d2)

                def f_s(s, dc=dc, t0=t0):
                    s.dma_start(z_d[dc, :, t0:t0 + TTA], zt[:, :]
                                ).then_inc(S["sQS"], 16)
                    cadd("sQS", 16)
                    s.wait_ge(S["sQS"], cget("sQS"))
                    return None
                ach("s", 1.2, f_s)

            for tt in range(NTT):
                for dc in range(DCN):
                    emit_a2(tt, dc)

            X2EST = 20000.0

            def x2_mm(dc):
                def f_t(t, dc=dc):
                    return t.matmul(ps43, gmap[:, :],
                                    stats_acc[:, dc * 4:(dc + 1) * 4],
                                    start=True, stop=True)
                ach("t", 0.3, f_t)

                def f_v(v):
                    v.tensor_scalar(sg43[:, :], ps43, bsel[0:4, 0:1], None,
                                    ALU.mult)
                    return v.tensor_scalar(sg43b[:, :], ps43, bsel[0:4, 1:2],
                                           None, ALU.mult)
                ach("v", 0.3, f_v)

                def f_s(s, dc=dc):
                    s.dma_start(
                        stats_in_d.ap()[0:1, dc * 16:(dc + 1) * 16]
                        .rearrange("o (a b) -> a (o b)", a=4),
                        sg43[:, :]).then_inc(S["sQS"], 16)
                    s.dma_start(
                        stats_in_d.ap()[1:2, dc * 16:(dc + 1) * 16]
                        .rearrange("o (a b) -> a (o b)", a=4),
                        sg43b[:, :]).then_inc(S["sQS"], 16)
                    cadd("sQS", 32)
                    s.wait_ge(S["sQS"], cget("sQS"))
                    return None
                ach("s", 0.6, f_s)

            achain.append(("v", 0.0, None))
            achain.pop()
            x2_start_marker = len(achain)
            for dc in range(DCN):
                x2_mm(dc)

            def x2_coll(g):
                g.collective_compute(
                    "AllReduce", ALU.add,
                    replica_groups=[list(range(8))],
                    ins=[stats_in_d.ap().opt()],
                    outs=[stats_out_d.ap().opt()]).then_inc(S["sCC"], 1)
                cadd("sCC", 1)
                g.wait_ge(S["sCC"], cget("sCC"))
                return None
            ach("g", 60.0, x2_coll)

            def x2_load(s):
                s.dma_start(row0[:, :], stats_out_d[0:1, :]
                            ).then_inc(S["sQL"], 16)
                s.dma_start(row1[:, :], stats_out_d[1:2, :]
                            ).then_inc(S["sQL"], 16)
                s.dma_start(gnsr[:, :], gnsr_d[:, :]).then_inc(S["sQL"], 16)
                s.dma_start(gnbr[:, :], gnbr_d[:, :]).then_inc(S["sQL"], 16)
                cadd("sQL", 64)
                s.wait_ge(S["sQL"], cget("sQL"))
                return None
            ach("s", 2.0, x2_load)

            def x2_v1(v):
                v.tensor_scalar(row0[:, :], row0[:, :], bsel[0:1, 0:1],
                                None, ALU.mult)
                v.tensor_scalar(row1[:, :], row1[:, :], bsel[0:1, 1:2],
                                None, ALU.mult)
                v.tensor_add(row0[:, :], row0[:, :], row1[:, :])
                r4 = row0.ap().rearrange("o (g j) -> o g j", g=32)
                v.tensor_tensor(grw.ap()[0:1, 160:192].unsqueeze(2),
                                r4[:, :, 2:3], r4[:, :, 3:4], ALU.add)
                v.tensor_scalar(grw.ap()[0:1, 0:32].unsqueeze(2),
                                r4[:, :, 0:1], CNT_INV, None, ALU.mult)
                v.tensor_scalar(grw.ap()[0:1, 32:64].unsqueeze(2),
                                r4[:, :, 1:2], CNT_INV, None, ALU.mult)
                v.tensor_scalar(grw[0:1, 64:96], grw[0:1, 160:192],
                                CNT_INV, None, ALU.mult)
                v.tensor_mul(grw[0:1, 96:128], grw[0:1, 0:32],
                             grw[0:1, 0:32])
                v.tensor_mul(grw[0:1, 128:160], grw[0:1, 32:64],
                             grw[0:1, 32:64])
                v.tensor_sub(grw[0:1, 64:96], grw[0:1, 64:96],
                             grw[0:1, 96:128])
                v.memset(grw[0:1, 224:225], 0.0)
                return v.tensor_sub(grw[0:1, 64:96], grw[0:1, 64:96],
                                    grw[0:1, 128:160])
            ach("v", 3.0, x2_v1)

            def x2_a1(a):
                return act(a, grw[0:1, 96:128], grw[0:1, 64:96], AF.Sqrt,
                           bias=epsc[0:1, 1:2])
            ach("a", 1.0, x2_a1)

            def x2_v2(v):
                v.reciprocal(grw[0:1, 192:224], grw[0:1, 96:128])
                rstd_b = grw.ap()[0:1, 192:224].unsqueeze(2).broadcast_to(
                    (1, 32, 32))
                mre_b = grw.ap()[0:1, 0:32].unsqueeze(2).broadcast_to(
                    (1, 32, 32))
                g3 = gnsr.rearrange("o (g j) -> o g j", g=32)
                c13 = c1row.rearrange("o (g j) -> o g j", g=32)
                c23 = c2row.rearrange("o (g j) -> o g j", g=32)
                v.memset(grw[0:1, 224:225], 0.0)
                v.tensor_tensor(c13, g3, rstd_b, ALU.mult)
                v.memset(grw[0:1, 225:226], 0.0)
                v.tensor_tensor(c23, c13, mre_b, ALU.mult)
                v.memset(grw[0:1, 226:227], 0.0)
                return v.tensor_sub(c2row[:, :], gnbr[:, :], c2row[:, :])
            ach("v", 3.0, x2_v2)

            def x2_st(s):
                s.dma_start(crow_d[0:1, :], c1row[:, :]).then_inc(S["sQS"], 16)
                s.dma_start(crow_d[1:2, :], c2row[:, :]).then_inc(S["sQS"], 16)
                cadd("sQS", 32)
                s.wait_ge(S["sQS"], cget("sQS"))
                for dc in range(DCN):
                    s.dma_start(c1pp[:, dc:dc + 1],
                                crow_d[0:1, dc * 128:(dc + 1) * 128]
                                .rearrange("o p -> p o")).then_inc(S["sQL"], 16)
                    s.dma_start(c2pp[:, dc:dc + 1],
                                crow_d[1:2, dc * 128:(dc + 1) * 128]
                                .rearrange("o p -> p o")).then_inc(S["sQL"], 16)
                    cadd("sQL", 32)
                s.wait_ge(S["sQL"], cget("sQL"))
                return None
            ach("s", 3.0, x2_st)

            def emit_wop(tt):
                t0 = tt * TTA

                def f_one(s, dc, t0=t0):
                    s.dma_start(zl[:, :], z_d[dc, :, t0:t0 + TTA]
                                ).then_inc(S["sQL"], 16)
                    s.dma_start(sgl[:, :], stash_d[5, dc, :, t0:t0 + TTA]
                                ).then_inc(S["sQL"], 16)
                    cadd("sQL", 32)
                    s.wait_ge(S["sQL"], cget("sQL"))
                    return None

                for dc in range(DCN):
                    def f_ld(s, dc=dc):
                        return f_one(s, dc)
                    ach("s", 1.5, f_ld)

                    def f_m(v, dc=dc):
                        v.tensor_scalar(U[1], zl[:, :],
                                        c1pp[:, dc:dc + 1], None, ALU.mult)
                        v.tensor_scalar(U[2], sgl[:, :],
                                        c2pp[:, dc:dc + 1], None, ALU.mult)
                        return v.tensor_add(mfb[0][:, :], U[1], U[2])
                    ach("v", 2.2, f_m)

                    def f_st2(s, tt=tt, dc=dc):
                        s.dma_start(mfm_d[tt, dc, :, :], mfb[0][:, :]
                                    ).then_inc(S["sQS"], 16)
                        cadd("sQS", 16)
                        s.wait_ge(S["sQS"], cget("sQS"))
                        return None
                    ach("s", 0.8, f_st2)

            for tt in range(NTT):
                emit_wop(tt)

            aidx = [0]
            acursor = [E0DONE]

            def chain_stage(eng, fn, idx):
                def run(engine, fn=fn, idx=idx):
                    engine.wait_ge(S["sAC"], idx)
                    last = fn(engine)
                    if last is not None:
                        last.then_inc(S["sAC"], 1)
                    else:
                        engine.sem_inc(S["sAC"], 1)
                return run

            for n, (eng, cost, fn) in enumerate(achain):
                if n == x1_marker:
                    acursor[0] = max(acursor[0], X1EST)
                if n == x2_start_marker:
                    acursor[0] = max(acursor[0], X2EST)
                acursor[0] += cost * 2.5
                ADD(eng, acursor[0], chain_stage(eng, fn, n))
            ACH_TOTAL = len(achain)

            E1T0 = 1000.0
            GUS = 2.3
            cnt_g = [0]
            cnt_sb = [0]
            cnt_hst = [0]
            cnt_hls = [0]
            cnt_dwq = [0]
            cnt_j2 = [0]
            cnt_xq = [0]
            pend_st = [None]
            est1 = [E1T0]

            pe_j2_at_hls = {}

            def gu_slab(tcb, e, fb):
                sb = cnt_sb[0]; cnt_sb[0] += 1
                est = est1[0] + (0 if sb < 2 else 0)

                def f_g(g, tcb=tcb, e=e, fb=fb, sb=sb):
                    if sb == 0:
                        g.wait_ge(S["sQGe"], 128)
                        g.wait_ge(S["sRT"], 64)
                    if sb >= 2:
                        g.wait_ge(S["sPE1"], 16 * (sb - 1))
                    g.dma_start(
                        slab[sb % 2].ap()[:, 0:8 * TTA]
                        .rearrange("p (c f) -> p c f", c=8),
                        guT_d[e, :, fb * TTA:(fb + 1) * TTA]
                        .rearrange("(c p) f -> p c f", p=128)
                        ).then_inc(S["sGW"], 16)
                    g.dma_start(
                        slab[sb % 2].ap()[:, 8 * TTA:16 * TTA]
                        .rearrange("p (c f) -> p c f", c=8),
                        guT_d[e, :, FF + fb * TTA:FF + (fb + 1) * TTA]
                        .rearrange("(c p) f -> p c f", p=128)
                        ).then_inc(S["sGW"], 16)
                    return None
                ADD("g", est - GUS * 16, f_g)
                return sb

            def gu_groups(tcb, e, fb, sb):
                for tc in range(2):
                    xq = tcb * 2 + tc
                    if xq >= cnt_xq[0]:
                        cnt_xq[0] = xq + 1

                        def f_xq(s, tcb=tcb, tc=tc, xq=xq):
                            if xq == 0:
                                s.wait_ge(S["sSTx"], 16 * 64)
                                s.wait_ge(S["sPQ"], 64)
                                s.wait_ge(S["sQGe"], 128)
                                s.wait_ge(S["sRT"], 64)
                            if xq >= 3:
                                s.wait_ge(S["sPE1"],
                                          1024 * ((xq - 3) // 2 + 1))
                            s.dma_start(
                                xnq[xq % 3].ap().rearrange(
                                    "p (c t) -> p c t", c=8),
                                xnf_d[:, tcb, :, tc * TTA:(tc + 1) * TTA]
                                .rearrange("c p t -> p c t")
                            ).then_inc(S["sZL"], 16)
                            cadd("sZL", 16)
                            return None
                        ADD("s", est1[0] - GUS * 20, f_xq)
                for tc in range(2):
                    xq = tcb * 2 + tc
                    for fc4 in range(4):
                        fc = fb * 4 + fc4
                        for gu in range(2):
                            gidx = cnt_g[0]; cnt_g[0] += 1
                            pair = gidx // 2
                            est1[0] += GUS

                            def f_t(t, sb=sb, tc=tc, fc4=fc4, gu=gu,
                                    gidx=gidx, pair=pair, xq=xq):
                                t.wait_ge(S["sGW"], 32 * (sb + 1))
                                t.wait_ge(S["sZL"],
                                          128 + 16 * (xq + 1))
                                ps = (pA if gu == 0 else pB)[pair % 2]
                                sem = S["sACg"] if gu == 0 else S["sACu"]
                                if pair >= 2:
                                    t.wait_ge(sem, pair - 1)
                                off = gu * 8 * TTA
                                last = None
                                for kc in range(8):
                                    last = t.matmul(
                                        ps[:, :],
                                        slab[sb % 2][:, off + kc * TTA +
                                                     fc4 * 128:
                                                     off + kc * TTA +
                                                     (fc4 + 1) * 128],
                                        xnq[xq % 3][:, kc * TTA:
                                                    (kc + 1) * TTA],
                                        start=(kc == 0), stop=(kc == 7))
                                last.then_inc(S["sPE1"], 1)
                                return None
                            ADD("t", est1[0], f_t)

                            def f_a(a, gu=gu, pair=pair, gidx=gidx):
                                a.wait_ge(S["sPE1"], gidx + 1)
                                if pair >= 2:
                                    a.wait_ge(S["sGM"], pair - 1)
                                if gu == 0:
                                    act(a, gel[pair % 2][:, :],
                                        pA[pair % 2][:, :],
                                        AF.Gelu).then_inc(S["sACg"], 1)
                                else:
                                    act(a, ups[pair % 2][:, :],
                                        pB[pair % 2][:, :],
                                        AF.Copy).then_inc(S["sACu"], 1)
                                return None
                            ADD("a", est1[0] + 0.3, f_a)

                        pair = (cnt_g[0] - 1) // 2
                        hst = cnt_hst[0]; cnt_hst[0] += 1

                        def f_gm(g, pair=pair, tcb=tcb, e=e, tc=tc, fc=fc,
                                 hst=hst):
                            g.wait_ge(S["sACg"], pair + 1)
                            g.wait_ge(S["sACu"], pair + 1)
                            if pair >= 2:
                                g.wait_ge(S["sHS"], 16 * (pair - 1))
                            g.tensor_mul(hstg[pair % 2][:, :],
                                         ups[pair % 2][:, :],
                                         gel[pair % 2][:, :]
                                         ).then_inc(S["sGM"], 1)
                            if pend_st[0] is not None:
                                pend_st[0](g)
                                pend_st[0] = None

                            def st(g2, pair=pair, e=e, tc=tc, fc=fc):
                                g2.wait_ge(S["sGM"], pair + 1)
                                g2.dma_start(
                                    hact_d[e % 2, fc // 8,
                                           tc * 4:(tc + 1) * 4,
                                           fc % 8, :, :]
                                    .rearrange("t8 p t -> p t8 t"),
                                    hstg[pair % 2].ap().rearrange(
                                        "p (t8 t) -> p t8 t", t8=4)
                                    ).then_inc(S["sHS"], 16)
                            pend_st[0] = st
                            return None
                        ADD("g", est1[0] + 0.6, f_gm)

            def down_pass(tcb, e, q):
                dq = cnt_dwq[0]; cnt_dwq[0] += 1

                def f_dw(g, e=e, q=q, dq=dq):
                    if pend_st[0] is not None:
                        pend_st[0](g)
                        pend_st[0] = None
                    if dq >= 2:
                        g.wait_ge(S["sPE2"], (dq - 1) * 16)
                    g.dma_start(
                        dwq[dq % 2].ap().rearrange("p (c h) -> p c h", c=8),
                        dwq_d[e, q * 8:(q + 1) * 8, :, :]
                        .rearrange("c p h -> p c h")
                        ).then_inc(S["sDWQ"], 16)
                    return None
                ADD("g", est1[0] - GUS * 16, f_dw)

                stneed = (tcb * 8 + e) * 64 + (q + 1) * 8 * 2

                for t8 in range(8):
                    hl = cnt_hls[0]; cnt_hls[0] += 1

                    def f_hl(s, tcb=tcb, e=e, q=q, t8=t8, hl=hl,
                             stneed=stneed):
                        if hl >= 3:
                            s.wait_ge(S["sPE2"], (hl - 2) * 2)
                        s.wait_ge(S["sHS"], 16 * stneed)
                        s.dma_start(
                            hls[hl % 3].ap().rearrange(
                                "p (c t) -> p c t", c=8),
                            hact_d[e % 2, q, t8, :, :, :]
                            .rearrange("c p t -> p c t")
                            ).then_inc(S["sDH"], 16)
                        return None
                    ADD("s", est1[0] - GUS * 5, f_hl)

                    est1[0] += 2.9
                    for h2 in range(2):
                        j2 = cnt_j2[0]; cnt_j2[0] += 1

                        def f_t(t, hl=hl, h2=h2, j2=j2, dq=dq):
                            t.wait_ge(S["sDH"], 16 * (hl + 1))
                            t.wait_ge(S["sDWQ"], 16 * (dq + 1))
                            if j2 >= 2:
                                t.wait_ge(S["sDV2"], j2 - 1)
                            last = None
                            for fcm in range(8):
                                last = t.matmul(
                                    pD[j2 % 2][:, :],
                                    hls[hl % 3][:, fcm * 128:(fcm + 1) * 128],
                                    dwq[dq % 2][:, fcm * D + h2 * TTA:
                                                fcm * D + (h2 + 1) * TTA],
                                    start=(fcm == 0), stop=(fcm == 7))
                            last.then_inc(S["sPE2"], 1)
                            return None
                        ADD("t", est1[0], f_t)

                        def f_v(v, tcb=tcb, e=e, q=q, t8=t8, h2=h2, j2=j2):
                            v.wait_ge(S["sPE2"], j2 + 1)
                            first = (e == 0 and q == 0)
                            cslice = combine[:, (tcb * 8 + t8) * 8 + e:
                                             (tcb * 8 + t8) * 8 + e + 1]
                            dst = acc[:, t8 * D + h2 * TTA:
                                      t8 * D + (h2 + 1) * TTA]
                            if first:
                                if tcb > 0:
                                    v.wait_ge(S["sSIL"], 8 * tcb)
                                v.tensor_scalar(dst, pD[j2 % 2][:, :],
                                                cslice, None, ALU.mult
                                                ).then_inc(S["sDV2"], 1)
                            else:
                                v.scalar_tensor_tensor(
                                    dst, pD[j2 % 2][:, :], cslice, dst,
                                    ALU.mult, ALU.add).then_inc(S["sDV2"], 1)
                            return None
                        ADD("v", est1[0] + 0.3, f_v)

            def quarter_silu(tcb):
                for t8 in range(8):
                    def f_sil(a, tcb=tcb, t8=t8):
                        if t8 == 0:
                            a.wait_ge(S["sDV2"], 512 * (tcb + 1))
                        if tcb * 8 + t8 >= 1:
                            a.wait_ge(S["sGMS"], 16 * (tcb * 8 + t8))
                        act(a, gms[0][:, :], acc[:, t8 * D:(t8 + 1) * D],
                            AF.Silu).then_inc(S["sSIL"], 1)
                        a.wait_ge(S["sSIL"], tcb * 8 + t8 + 1)
                        a.dma_start(gmoe_d[tcb * 8 + t8, :, :],
                                    gms[0][:, :]).then_inc(S["sGMS"], 16)
                        return None
                    ADD("a", est1[0] + 2.0 + t8 * 1.2, f_sil)

            def f_warm(t):
                for ps in (pA[0], pA[1], pB[0], pB[1], pD[0], pD[1],
                           pW[0], pW[1]):
                    for c4 in range(4):
                        t.matmul(ps[:, c4 * 128:(c4 + 1) * 128],
                                 ident[:, :], ident[:, :],
                                 start=True, stop=True)
                return None
            ADD("t", E0DONE + 2.0, f_warm)

            for tcb in range(NQT):
                for e in range(E):
                    sbs = []
                    plan = ["s0", "s1", "s2", "s3", "p0", "s4", "s5", "p1",
                            "s6", "s7", "p2", "p3"]
                    for item in plan:
                        k = int(item[1])
                        if item[0] == "s":
                            sb = gu_slab(tcb, e, k)
                            gu_groups(tcb, e, k, sb)
                            if k == 7:
                                def f_fl(g):
                                    if pend_st[0] is not None:
                                        pend_st[0](g)
                                        pend_st[0] = None
                                    return None
                                ADD("g", est1[0] + 0.7, f_fl)
                        else:
                            down_pass(tcb, e, k)
                quarter_silu(tcb)

            GU_TOTAL = cnt_g[0]
            J2_TOTAL = cnt_j2[0]

            woest = [22000.0]

            def f_wo_w(g):
                g.wait_ge(S["sQGe"], 128)
                g.wait_ge(S["sRT"], 64)
                g.dma_start(wo_sb.ap().rearrange("p (c m) -> p c m", c=8),
                            wo_d.ap().rearrange("(c p) m -> p c m", p=128)
                            ).then_inc(S["sWL"], 16)
                return None
            ADD("g", 1500.0, f_wo_w)

            cnt_wo = [0]
            for tt in range(NTT):
                def f_ml(s, tt=tt):
                    if tt == 0:
                        s.wait_ge(S["sAC"], ACH_TOTAL)
                    if tt >= 1:
                        s.wait_ge(S["sWPE"], tt * 8)
                    s.dma_start(
                        mfl[0].ap().rearrange("p (c t) -> p c t", c=8),
                        mfm_d[tt].rearrange("c p t -> p c t")
                    ).then_inc(S["sWL"], 16)
                    return None
                ADD("s", woest[0], f_ml)
                woest[0] += 4.0

                for s4 in range(4):
                    for half in range(2):
                        w8 = cnt_wo[0]; cnt_wo[0] += 1

                        def f_pe(t, tt=tt, s4=s4, half=half, w8=w8):
                            t.wait_ge(S["sWL"], 16 * (tt + 2))
                            if w8 >= 2:
                                t.wait_ge(S["sWEV"], w8 - 1)
                            last = None
                            for dc in range(DCN):
                                last = t.matmul(
                                    pW[w8 % 2][:, :],
                                    mfl[0][:, dc * 512 + s4 * 128:
                                           dc * 512 + (s4 + 1) * 128],
                                    wo_sb[:, dc * D + half * 512:
                                          dc * D + (half + 1) * 512],
                                    start=(dc == 0), stop=(dc == 7))
                            last.then_inc(S["sWPE"], 1)
                            return None
                        woest[0] += 2.4
                        ADD("t", woest[0], f_pe)

                        def f_ev(v, s4=s4, half=half, w8=w8):
                            v.wait_ge(S["sWPE"], w8 + 1)
                            if w8 >= 2:
                                v.wait_ge(S["sWST"], 16 * (w8 // 2))
                            v.tensor_copy(
                                woo[0][:, half * 512:(half + 1) * 512],
                                pW[w8 % 2][:, :]).then_inc(S["sWEV"], 1)
                            return None
                        ADD("v", woest[0] + 0.4, f_ev)

                    def f_st(s, tt=tt, s4=s4):
                        s.wait_ge(S["sWEV"], tt * 8 + (s4 + 1) * 2)
                        s.dma_start(wo_out_d[tt * 4 + s4, :, :],
                                    woo[0][:, :]).then_inc(S["sWST"], 16)
                        return None
                    ADD("s", woest[0] + 0.8, f_st)

            for k in streams:
                streams[k].sort(key=lambda x: (x[0], x[1]))

            with nc.Block() as blk:
                @blk.tensor
                def _(t):
                    for _, _, fn in streams["t"]:
                        fn(t)

                @blk.vector
                def _(v):
                    for _, _, fn in streams["v"]:
                        fn(v)

                @blk.scalar
                def _(a):
                    for _, _, fn in streams["a"]:
                        fn(a)

                @blk.sync
                def _(s):
                    for _, _, fn in streams["s"]:
                        fn(s)

                @blk.gpsimd
                def _(g):
                    for _, _, fn in streams["g"]:
                        fn(g)

        with ExitStack() as bs, ExitStack() as bsp:
            w1_sb = SB(bs, "w1_sb", [128, 8 * FF], BF16)
            w2_sb = SB(bs, "w2_sb", [128, 32 * D], BF16)
            b1c = SB(bs, "b1c_sb", [128, 32], F32)
            b2b = SB(bs, "b2b_sb", [128, D], BF16)
            l1sb = SB(bs, "l1sb_sb", [128, D], BF16)
            l1bb = SB(bs, "l1bb_sb", [128, D], BF16)
            l2sb = SB(bs, "l2sb_sb", [128, D], BF16)
            l2bb = SB(bs, "l2bb_sb", [128, D], BF16)
            xb_tm = SB(bs, "xb_tm", [128, 2 * D], F32)
            wol = SB(bs, "wol", [128, 2 * D], F32)
            gml = SB(bs, "gml", [128, 2 * D], BF16)
            t1b = SB(bs, "t1b", [128, 2 * D], F32)
            xh_tm = SB(bs, "xh_tm", [128, 2 * D], F32)
            xh_bf = SB(bs, "xh_bf", [128, 2 * D], BF16)
            xh_fm = SB(bs, "xh_fm", [128, 8 * TTB], BF16)
            hh_fm = SB(bs, "hh_fm", [128, 32 * TTB], BF16)
            stb = SB(bs, "stb", [128, 32], F32)
            eps2 = SB(bs, "eps2", [128, 2], F32)
            ps_big = bsp.enter_context(
                nc.psum_tensor("ps_big", [128, 2 * D], F32))
            trb_ps = bsp.enter_context(
                nc.psum_tensor("trb_ps", [128, 2 * D], BF16))
            p1 = [bsp.enter_context(
                nc.psum_tensor(f"p1_{i}", [128, TTB], F32)) for i in range(2)]

            sched = []
            bq0 = cget("sQL")
            cadd("sQL", 128)

            def f_w(g):
                g.wait_ge(S["sPE1"], GU_TOTAL)
                g.wait_ge(S["sPE2"], J2_TOTAL)
                g.wait_ge(S["sWPE"], 64)
                g.wait_ge(S["sSIL"], 32)
                g.wait_ge(S["sWST"], 16 * 32)
                g.wait_ge(S["sGMS"], 16 * 32)
                g.wait_ge(S["sHS"], 16 * 2048)
                g.wait_ge(S["sGM"], 2048)
                g.wait_ge(S["sAC"], ACH_TOTAL)
                g.dma_start(w1_sb.ap().rearrange("p (c m) -> p c m", c=8),
                            w1_d.ap().rearrange("(c p) m -> p c m", p=128)
                            ).then_inc(S["sQL"], 16)
                g.dma_start(w2_sb.ap().rearrange("p (c m) -> p c m", c=32),
                            w2_d.ap().rearrange("(c p) m -> p c m", p=128)
                            ).then_inc(S["sQL"], 16)
                g.dma_start(b1c[:, :], b1c_d[:, :]).then_inc(S["sQL"], 16)
                g.dma_start(b2b[:, :], b2b_d[:, :]).then_inc(S["sQL"], 16)
                g.dma_start(l1sb[:, :], l1sb_d[:, :]).then_inc(S["sQL"], 16)
                g.dma_start(l1bb[:, :], l1bb_d[:, :]).then_inc(S["sQL"], 16)
                g.dma_start(l2sb[:, :], l2sb_d[:, :]).then_inc(S["sQL"], 16)
                g.dma_start(l2bb[:, :], l2bb_d[:, :]).then_inc(S["sQL"], 16)
                g.wait_ge(S["sQL"], bq0 + 128)
                return None
            sched.append(("g", f_w))

            def f_eps(v):
                v.memset(eps2[:, 0:1], 1e-5)
                v.memset(eps2[:, 1:2], 1e-6)
                return None
            sched.append(("v", f_eps))

            def emit_ln_tm(sched, src, dst, epscol, scale_t, bias_t):
                nsub = 2

                def f1(v, src=src):
                    return v.reduce_sum(
                        stb[:, 0:2],
                        src.ap().rearrange("p (s d) -> p s d", s=nsub),
                        axis=AX)
                sched.append(("v", f1))

                def f2(a, src=src, dst=dst):
                    last = None
                    for s4 in range(nsub):
                        last = act(a, dst[:, s4 * D:(s4 + 1) * D],
                                   src[:, s4 * D:(s4 + 1) * D], AF.Square,
                                   accum_out=stb[:, 4 + s4:5 + s4])
                    return last
                sched.append(("a", f2))

                def f3(v):
                    v.tensor_scalar(stb[:, 2:4], stb[:, 0:2], 1.0 / D, None,
                                    ALU.mult)
                    v.tensor_scalar(stb[:, 6:8], stb[:, 4:6], 1.0 / D, None,
                                    ALU.mult)
                    v.tensor_mul(stb[:, 8:10], stb[:, 2:4], stb[:, 2:4])
                    v.memset(stb[:, 30:31], 0.0)
                    return v.tensor_sub(stb[:, 8:10], stb[:, 6:8],
                                        stb[:, 8:10])
                sched.append(("v", f3))

                def f4(a, epscol=epscol):
                    return act(a, stb[:, 10:12], stb[:, 8:10], AF.Sqrt,
                               bias=eps2[:, epscol:epscol + 1])
                sched.append(("a", f4))

                def f5(v, src=src, dst=dst, scale_t=scale_t, bias_t=bias_t):
                    v.reciprocal(stb[:, 12:14], stb[:, 10:12])
                    v.memset(stb[:, 30:31], 0.0)
                    last = None
                    for s4 in range(nsub):
                        last = v.tensor_scalar(dst[:, s4 * D:(s4 + 1) * D],
                                               src[:, s4 * D:(s4 + 1) * D],
                                               stb[:, 2 + s4:3 + s4],
                                               stb[:, 12 + s4:13 + s4],
                                               ALU.subtract, ALU.mult)
                    if scale_t is not None:
                        for s4 in range(nsub):
                            v.tensor_mul(dst[:, s4 * D:(s4 + 1) * D],
                                         dst[:, s4 * D:(s4 + 1) * D],
                                         scale_t[:, :])
                            last = v.tensor_add(dst[:, s4 * D:(s4 + 1) * D],
                                                dst[:, s4 * D:(s4 + 1) * D],
                                                bias_t[:, :])
                    return last
                sched.append(("v", f5))

            def emit_b(ttb):
                t0 = ttb * TTB

                def f_l(s, ttb=ttb, t0=t0):
                    s.dma_start(
                        xb_tm.ap().rearrange("p (s d) -> p s d", s=2),
                        x_d[t0:t0 + TTB, :].rearrange("(s p) d -> p s d",
                                                      p=128)
                    ).then_inc(S["sQL"], 16)
                    cadd("sQL", 16)
                    for s2 in range(2):
                        s.dma_start(wol[:, s2 * D:(s2 + 1) * D],
                                    wo_out_d[ttb * 2 + s2, :, :]
                                    ).then_inc(S["sQL"], 16)
                        s.dma_start(gml[:, s2 * D:(s2 + 1) * D],
                                    gmoe_d[ttb * 2 + s2, :, :]
                                    ).then_inc(S["sQL"], 16)
                        cadd("sQL", 32)
                    s.wait_ge(S["sQL"], cget("sQL"))
                    return None
                sched.append(("s", f_l))

                emit_ln_tm(sched, xb_tm, t1b, 0, None, None)

                def f_t1(v):
                    return v.tensor_add(t1b[:, :], t1b[:, :], wol[:, :])
                sched.append(("v", f_t1))

                emit_ln_tm(sched, t1b, xh_tm, 1, l1sb, l1bb)

                def f_cp(v):
                    return v.tensor_copy(xh_bf[:, :], xh_tm[:, :])
                sched.append(("v", f_cp))

                def f_tr(t):
                    last = None
                    for s4 in range(2):
                        for dc in range(DCN):
                            last = t.transpose(
                                trb_ps[:, dc * TTB + s4 * 128:
                                       dc * TTB + (s4 + 1) * 128],
                                xh_bf[:, s4 * D + dc * 128:
                                      s4 * D + (dc + 1) * 128],
                                ident[:, :])
                    return last
                sched.append(("t", f_tr))

                def f_tre(v):
                    return v.tensor_copy(xh_fm[:, :], trb_ps[:, :])
                sched.append(("v", f_tre))

                for fcp in range(16):
                    def f_w1(t, fcp=fcp):
                        last = None
                        for i in range(2):
                            fc = fcp * 2 + i
                            for kc in range(8):
                                last = t.matmul(
                                    p1[i][:, :],
                                    w1_sb[:, kc * FF + fc * 128:
                                          kc * FF + (fc + 1) * 128],
                                    xh_fm[:, kc * TTB:(kc + 1) * TTB],
                                    start=(kc == 0), stop=(kc == 7))
                        return last
                    sched.append(("t", f_w1))

                    def f_g1(a, fcp=fcp):
                        last = None
                        for i in range(2):
                            fc = fcp * 2 + i
                            last = act(a, hh_fm[:, fc * TTB:(fc + 1) * TTB],
                                       p1[i][:, :], AF.Gelu_apprx_tanh,
                                       bias=b1c[:, fc:fc + 1])
                        return last
                    sched.append(("a", f_g1))

                for s4 in range(2):
                    def f_w2(t, s4=s4):
                        last = None
                        for half in range(2):
                            for fc in range(32):
                                last = t.matmul(
                                    ps_big[:, s4 * D + half * 512:
                                           s4 * D + (half + 1) * 512],
                                    hh_fm[:, fc * TTB + s4 * 128:
                                          fc * TTB + (s4 + 1) * 128],
                                    w2_sb[:, fc * D + half * 512:
                                          fc * D + (half + 1) * 512],
                                    start=(fc == 0), stop=(fc == 31))
                        return last
                    sched.append(("t", f_w2))

                def f_res(v):
                    for s4 in range(2):
                        v.scalar_tensor_tensor(
                            wol[:, s4 * D:(s4 + 1) * D],
                            ps_big[:, s4 * D:(s4 + 1) * D], 1.0, b2b[:, :],
                            ALU.mult, ALU.add)
                    return v.tensor_add(wol[:, :], wol[:, :], xh_tm[:, :])
                sched.append(("v", f_res))

                emit_ln_tm(sched, wol, t1b, 1, l2sb, l2bb)

                def f_fin(v):
                    v.tensor_mul(t1b[:, :], t1b[:, :], gml[:, :])
                    return v.tensor_add(wol[:, :], xb_tm[:, :], t1b[:, :])
                sched.append(("v", f_fin))

                def f_out(g, t0=t0):
                    g.dma_start(
                        y_d[t0:t0 + TTB, :].rearrange("(s p) d -> p s d",
                                                      p=128),
                        wol.ap().rearrange("p (s d) -> p s d", s=2)
                    ).then_inc(S["sQO"], 16)
                    cadd("sQO", 16)
                    return None
                sched.append(("g", f_out))

            for ttb in range(NTTB):
                emit_b(ttb)

            with nc.Block() as blk:
                def body(eng_name):
                    def _(eng):
                        for i, (en, fn) in enumerate(sched):
                            if en != eng_name:
                                continue
                            eng.wait_ge(S["sB"], i)
                            last = fn(eng)
                            if last is not None:
                                last.then_inc(S["sB"], 1)
                            else:
                                eng.sem_inc(S["sB"], 1)
                    return _
                blk.sync(body("s"))
                blk.tensor(body("t"))
                blk.vector(body("v"))
                blk.scalar(body("a"))
                blk.gpsimd(body("g"))

        with nc.Block() as blk:
            @blk.sync
            def _(s):
                s.wait_ge(S["sQO"], cget("sQO"))
        if dbg:
            with nc.Block() as blk:
                @blk.gpsimd
                def _(g):
                    g.dma_start(dbg_stash.ap(), stash_d.ap()).then_inc(
                        S["sQO"], 16)
                    g.dma_start(dbg_z.ap(), z_d.ap()).then_inc(S["sQO"], 16)
                    g.dma_start(dbg_xnf.ap(), xnf_d.ap()).then_inc(
                        S["sQO"], 16)
                    g.dma_start(dbg_crow.ap(), crow_d.ap()).then_inc(
                        S["sQO"], 16)
                    g.dma_start(dbg_summ.ap(), summ_ag_d.ap()).then_inc(
                        S["sQO"], 16)
                    g.dma_start(dbg_wo.ap(), wo_out_d.ap()).then_inc(
                        S["sQO"], 16)
                    g.dma_start(dbg_gm.ap(), gmoe_d.ap()).then_inc(
                        S["sQO"], 16)
                    g.dma_start(dbg_mfm.ap(), mfm_d.ap()).then_inc(
                        S["sQO"], 16)
                    g.dma_start(dbg_hact.ap(), hact_d.ap()).then_inc(
                        S["sQO"], 16)
                    g.dma_start(dbg_comb[:, :], combine[:, :]).then_inc(
                        S["sQO"], 16)
                    g.dma_start(dbg_acc[:, :], acc1_ref[:, :]).then_inc(
                        S["sQO"], 16)
                    cadd("sQO", 176)
                    g.wait_ge(S["sQO"], cget("sQO"))

        for k, vv in C.items():
            assert vv < 60000, (k, vv)
    return nc


def kernel(**inputs):
    x = np.asarray(inputs["L0_attn_x"])
    bf = ml_dtypes.bfloat16

    def f32c(a):
        return np.ascontiguousarray(np.asarray(a), dtype=np.float32)

    wq = f32c(inputs["L0_attn_Wq"]); wk = f32c(inputs["L0_attn_Wk"])
    wv = f32c(inputs["L0_attn_Wv"]); wa = f32c(inputs["L0_attn_Wa"])
    wg = f32c(inputs["L0_attn_Wg"]); wo = f32c(inputs["L0_attn_Wo"])
    w1 = f32c(inputs["L0_attn_W1"]); w2 = f32c(inputs["L0_attn_W2"])
    b1 = f32c(inputs["L0_attn_b1"]); b2 = f32c(inputs["L0_attn_b2"])
    l1s = f32c(inputs["L0_attn_ln1_scale"]); l1b = f32c(inputs["L0_attn_ln1_bias"])
    l2s = f32c(inputs["L0_attn_ln2_scale"]); l2b = f32c(inputs["L0_attn_ln2_bias"])
    gns = f32c(inputs["L0_attn_gn_scale"]); gnb = f32c(inputs["L0_attn_gn_bias"])
    gate_w = f32c(inputs["L0_ffn_gate_w"])
    gu = f32c(inputs["L0_ffn_gate_up_w"])
    dw = f32c(inputs["L0_ffn_down_w"])

    guT = np.ascontiguousarray(gu.transpose(0, 2, 1))
    dwq = np.ascontiguousarray(
        dw.transpose(0, 2, 1).reshape(8, 32, 128, 1024))
    gwT = np.ascontiguousarray(gate_w.T)
    b1c = np.ascontiguousarray(b1.reshape(32, 128).T)
    b2b = np.ascontiguousarray(np.broadcast_to(b2, (128, D)))
    l1sb = np.ascontiguousarray(np.broadcast_to(l1s, (128, D)))
    l1bb = np.ascontiguousarray(np.broadcast_to(l1b, (128, D)))
    l2sb = np.ascontiguousarray(np.broadcast_to(l2s, (128, D)))
    l2bb = np.ascontiguousarray(np.broadcast_to(l2b, (128, D)))
    gnsr = gns.reshape(1, D)
    gnbr = gnb.reshape(1, D)
    ident = np.eye(128, dtype=bf)
    gmap = np.zeros((128, 4), np.float32)
    for p in range(128):
        gmap[p, p // 32] = 1.0

    in_maps = []
    for c in range(8):
        b = c // 4
        t0 = (c % 4) * T
        onehot = np.zeros((128, 8), np.float32)
        onehot[:, c] = 1.0
        bsel = np.zeros((128, 2), np.float32)
        bsel[:, b] = 1.0
        m = {
            "x": np.ascontiguousarray(x[b, t0:t0 + T, :], dtype=np.float32),
            "wq": wq, "wk": wk, "wv": wv, "wa": wa, "wg": wg, "wo": wo,
            "w1": w1, "w2": w2, "b1c": b1c, "b2b": b2b,
            "l1sb": l1sb, "l1bb": l1bb, "l2sb": l2sb, "l2bb": l2bb,
            "gnsr": gnsr, "gnbr": gnbr, "gwT": gwT, "guT": guT, "dwq": dwq,
            "ident": ident, "gmap": gmap, "onehot": onehot, "bsel": bsel,
        }
        in_maps.append(m)

    dbg = bool(inputs.get("_dbg", False))
    nc = build_nc(dbg=dbg)
    res = run_bass_kernel_spmd(nc, in_maps, core_ids=list(range(8)))
    out = np.empty((2, 16384, D), np.float32)
    for c in range(8):
        b = c // 4
        t0 = (c % 4) * T
        out[b, t0:t0 + T, :] = res.results[c]["y"]
    if dbg:
        return out, res
    return out


if __name__ == "__main__":
    print("building...")
    nc = build_nc()
    print("built ok; instructions:", nc.n_instructions())
